# version 08
# baseline (speedup 1.0000x reference)
import numpy as np
import ml_dtypes
from contextlib import ExitStack

import concourse.bass as bass
import concourse.mybir as mybir
from concourse.bass_utils import run_bass_kernel_spmd

F32 = mybir.dt.float32
BF16 = mybir.dt.bfloat16
I32 = mybir.dt.int32
AF = mybir.ActivationFunctionType
ALU = mybir.AluOpType
AX = mybir.AxisListType.X

B = 4
NCORES = 8
T = 4096
TC = 512
NCH = T // TC
TT = TC // 128
DM = 1024
FT = DM // 128
DI = 1024
DT = DI // 128
N = 16
SL = DT * N
RK = 64
NE = 2 * DI // 128
M_RND = 12582912.0
LN_EPS = 1e-5
HALO = 6

BF = ml_dtypes.bfloat16


class Plan:
    def __init__(self):
        self.sems = {}
        self.steps = []

    def cnt(self, name):
        return self.sems.get(name, 0)

    def step(self, eng, emit, waits=(), inc=None, raw=False):
        if inc is not None:
            name, amt = inc
            self.sems[name] = self.sems.get(name, 0) + amt
        w = [(n, v) for (n, v) in waits if v > 0]
        self.steps.append((eng, emit, w, None if raw else inc, raw))


def build_nc(trunc=None, debug=False):
    nc = bass.Bass("TRN2", target_bir_lowering=False, detect_race_conditions=False)

    x_d = nc.dram_tensor("x", [T, DM], F32, kind="ExternalInput").ap()
    w_in_d = nc.dram_tensor("w_in", [2 * DI, DM], F32, kind="ExternalInput").ap()
    w_out_d = nc.dram_tensor("w_out", [DM, DI], F32, kind="ExternalInput").ap()
    w_x_d = nc.dram_tensor("w_x", [96, DI], F32, kind="ExternalInput").ap()
    w_dt_d = nc.dram_tensor("w_dt", [DI, RK], F32, kind="ExternalInput").ap()
    a_log_d = nc.dram_tensor("a_log", [DI, N], F32, kind="ExternalInput").ap()
    b_dt_d = nc.dram_tensor("b_dt", [DT, 128], F32, kind="ExternalInput").ap()
    dpar_d = nc.dram_tensor("dpar", [DT, 128], F32, kind="ExternalInput").ap()
    cb1_d = nc.dram_tensor("cb1", [DT, 128], F32, kind="ExternalInput").ap()
    cb2_d = nc.dram_tensor("cb2", [DT, 128], F32, kind="ExternalInput").ap()
    cw1_d = nc.dram_tensor("cw1", [DT, 512], F32, kind="ExternalInput").ap()
    cw2_d = nc.dram_tensor("cw2", [DT, 512], F32, kind="ExternalInput").ap()
    gates_d = nc.dram_tensor("gates", [1, 2], F32, kind="ExternalInput").ap()
    packed_d = nc.dram_tensor("packed", [256, 1024], I32, kind="ExternalInput").ap()
    wscale_d = nc.dram_tensor("wscale", [1, 1], F32, kind="ExternalInput").ap()
    fbias_d = nc.dram_tensor("fbias", [8, 128], F32, kind="ExternalInput").ap()
    cvec_d = nc.dram_tensor("cvec", [8, 128], F32, kind="ExternalInput").ap()
    wmod_d = nc.dram_tensor("wmod", [DM, 6 * DM], F32, kind="ExternalInput").ap()
    idbf_d = nc.dram_tensor("idbf", [128, 128], BF16, kind="ExternalInput").ap()
    idf_d = nc.dram_tensor("idf", [128, 128], F32, kind="ExternalInput").ap()
    sel_d = nc.dram_tensor("sel", [32, 32 * 128], BF16, kind="ExternalInput").ap()
    ones_d = nc.dram_tensor("onesr", [1, 128], F32, kind="ExternalInput").ap()

    sl_d = nc.dram_tensor("slid", [1, 1], I32, kind="ExternalInput").ap()
    out_d = nc.dram_tensor("out", [T, DM], F32, kind="ExternalOutput").ap()
    shxd = nc.dram_tensor("shxd", [2, 96, TC], F32, kind="Internal",
                          addr_space="Shared").ap()
    shmx = nc.dram_tensor("shmx", [2, 8, 128, TC], F32, kind="Internal",
                          addr_space="Shared").ap()
    shf = nc.dram_tensor("shf", [2, 1, 4], I32, kind="Internal",
                         addr_space="Shared").ap()
    wintd = nc.dram_tensor("w_inT_d", [NE, 128, DM], BF16, kind="Internal").ap()
    woutd = nc.dram_tensor("w_outT_d", [8, 128, DT * 128], BF16, kind="Internal").ap()
    wffnd = nc.dram_tensor("w_ffnT_d", [8, 128, 8 * 128], BF16, kind="Internal").ap()

    p = Plan()

    with ExitStack() as ctx:
        sb = lambda name, shape, dt: ctx.enter_context(nc.sbuf_tensor(name, shape, dt))
        psb = lambda name, shape, dt: ctx.enter_context(nc.psum_tensor(name, shape, dt))

        wxT = sb("wxT", [128, DT * 96], BF16)
        wdtT = sb("wdtT", [64, DT * 128], BF16)
        a_sb = sb("a_sb", [128, DT * N], F32)
        bdt_sb = sb("bdt_sb", [128, DT], F32)
        d_sb = sb("d_sb", [128, DT], F32)
        b1_sb = sb("b1_sb", [128, DT], F32)
        b2_sb = sb("b2_sb", [128, DT], F32)
        w1_sb = sb("w1_sb", [128, DT * 4], F32)
        w2_sb = sb("w2_sb", [128, DT * 4], F32)
        ca1_sb = sb("ca1_sb", [128, FT], F32)
        sa_sb = sb("sa_sb", [128, FT], F32)
        ga_sb = sb("ga_sb", [128, FT], F32)
        cb1r = sb("cb1r", [128, DM], F32)
        sbr = sb("sbr", [128, DM], F32)
        gbr = sb("gbr", [128, DM], F32)
        gbiasr = sb("gbiasr", [128, DM], F32)
        ws_bc = sb("ws_bc", [128, 1], F32)
        g0_bc = sb("g0_bc", [128, 1], F32)
        g1_bc = sb("g1_bc", [128, 1], F32)
        id_bf = sb("id_bf", [128, 128], BF16)
        id_f = sb("id_f", [128, 128], F32)
        sel_sb = sb("sel_sb", [32, 32 * 128], BF16)
        ones_f = sb("ones_f", [1, 128], F32)
        hstate = sb("hstate", [128, SL], F32)
        mod_sb = sb("mod_sb", [128, 48], F32)
        ct_sb = sb("ct_sb", [128, 8], F32)
        gt_sb = sb("gt_sb", [1, 8], F32)
        vec8 = sb("vec8", [128, 64], F32)
        cst = sb("cst", [128, 4], F32)

        with ExitStack() as pctx:
            psbuf = lambda name, shape, dt: pctx.enter_context(nc.sbuf_tensor(name, shape, dt))
            pw0 = psbuf("pw0", [128, 2048], F32)
            pw1 = psbuf("pw1", [128, 2048], F32)
            pwb = psbuf("pwb", [128, 2048], BF16)
            pwb2 = psbuf("pwb2", [128, 2048], BF16)
            pmod0 = psbuf("pmod0", [128, 6144], F32)
            pmod1 = psbuf("pmod1", [128, 6144], F32)
            pwx = psbuf("pwx", [96, DI], F32)
            pwxb = psbuf("pwxb", [96, DI], BF16)
            ppk = psbuf("ppk", [128, 2048], I32)
            pi1 = psbuf("pi1", [128, 1024], I32)
            praw = psbuf("praw", [16, 512], F32)
            praw2 = psbuf("praw2", [16, 512], F32)
            pa_raw = psbuf("pa_raw", [128, DT * N], F32)
            grow = psbuf("grow", [1, 1024], F32)
            bias_f = psbuf("bias_f", [128, 8], F32)
            row8 = psbuf("row8", [8, 128], F32)

        x_tm = sb("x_tm", [128, TT * DM], F32)
        out_tm = sb("out_tm", [128, TT * DM], F32)
        xconv = sb("xconv", [128, DT * (TC + HALO)], BF16)
        z_sb = sb("z_sb", [128, DT * TC], BF16)
        xc_sb = sb("xc_sb", [128, DT * TC], BF16)
        shbf = sb("shbf", [128, 2 * FT * TC], BF16)
        brep = sb("brep", [128, N * TC], BF16)
        crep = sb("crep", [128, N * TC], BF16)
        wbuf = sb("wbuf", [128, 2 * DM], BF16)
        wobuf = sb("wobuf", [128, 2 * DT * 128], BF16)
        wfbuf = sb("wfbuf", [128, 8 * 128], BF16)
        dtlow = sb("dtlow", [64, TC], BF16)
        xdbc = sb("xdbc", [32, TC], BF16)
        xdblp = sb("xdblp", [96, TC], F32)
        xdb0 = sb("xdb0", [96, TC], F32)
        xdb1 = sb("xdb1", [96, TC], F32)
        mx0 = sb("mx0", [128, 2 * TC], F32)
        mx1 = sb("mx1", [128, 2 * TC], F32)
        slsb = sb("slsb", [1, 1], I32)
        fsb = sb("fsb", [1, 8], I32)
        flagtab = sb("flagtab", [1, 72], I32)
        mixst = sb("mixst", [128, 2 * TC], F32)
        sm_sl = sb("sm_sl", [128, TC], F32)
        dl_sl = sb("dl_sl", [128, TC], F32)
        g_sl = sb("g_sl", [128, TC], BF16)
        da_sl = sb("da_sl", [128, 2 * TC], F32)
        u_sl = sb("u_sl", [128, TC], BF16)
        h_sl = sb("h_sl", [128, 2 * TC], BF16)
        t1b_sl = sb("t1b_sl", [128, TC], BF16)
        yt_sl = sb("yt_sl", [128, TC], F32)
        acc1 = sb("acc1", [128, TC], F32)
        acc2 = sb("acc2", [128, TC], F32)
        mixfm = sb("mixfm", [128, TC], F32)
        stat = sb("stat", [128, 32], F32)
        amax = sb("amax", [128, TT], F32)
        as_s = sb("as_s", [128, TT], F32)
        was_s = sb("was_s", [128, TT], F32)

        ps_a = psb("ps_a", [128, TC], F32)
        ps_b = psb("ps_b", [128, TC], F32)
        ps_t1 = psb("ps_t1", [128, TC], F32)
        ps_t2 = psb("ps_t2", [128, TC], F32)
        ps_c1 = psb("ps_c1", [128, TC], BF16)
        ps_c2 = psb("ps_c2", [128, TC], BF16)
        ps_r = psb("ps_r", [128, 512], F32)

        L = "ld"

        def ld(dst, src):
            p.step("s", lambda e, d=dst, s=src: e.dma_start(out=d, in_=s), inc=(L, 16))

        ld(id_bf[:], idbf_d)
        ld(id_f[:], idf_d)
        ld(sel_sb[:], sel_d)
        ld(ones_f[:], ones_d)
        a_r = a_log_d.rearrange("(dt p) n -> dt p n", p=128)
        for dd in range(DT):
            ld(pa_raw[:, dd * N:(dd + 1) * N], a_r[dd])
        ld(praw[0:DT, 0:128], b_dt_d)
        ld(praw[0:DT, 128:256], dpar_d)
        ld(praw[0:DT, 256:384], cb1_d)
        ld(praw[0:DT, 384:512], cb2_d)
        ld(praw2[0:DT, 0:512], cw1_d)
        ld(pw0[0:DT, 0:512], cw2_d)
        ld(pw0[0:8, 1024:1152], fbias_d)
        ld(pw0[0:8, 1280:1408], cvec_d)
        ld(gt_sb[0:1, 0:2], gates_d)
        ld(pw0[0:1, 1536:1537], wscale_d)
        ld(pwx[:], w_x_d)
        wdt_r = w_dt_d.rearrange("(dt p) r -> dt p r", p=128)
        for dd in range(DT):
            ld(pw1[:, dd * 64:(dd + 1) * 64], wdt_r[dd])
        ld(ppk[:, 0:1024], packed_d[0:128, :])
        ld(ppk[:, 1024:2048], packed_d[128:256, :])
        nld = p.cnt(L)

        p.step("a", lambda e: e.activation(out=pa_raw[:], in_=pa_raw[:], func=AF.Exp),
               waits=[(L, nld)], inc=("pa", 1))
        p.step("v", lambda e: e.tensor_scalar(out=a_sb[:], in0=pa_raw[:], scalar1=-1.0,
                                              scalar2=None, op0=ALU.mult),
               waits=[("pa", p.cnt("pa"))], inc=("pv", 1))

        def tr_small(dst, src_ap):
            p.step("t", lambda e, s=src_ap: e.transpose(ps_t1[0:128, 0:DT], s, id_f[0:DT, 0:DT]),
                   waits=[(L, nld), ("pvsm", p.cnt("pvsm"))], inc=("ptsm", 1))
            p.step("v", lambda e, d=dst: e.tensor_copy(out=d, in_=ps_t1[0:128, 0:DT]),
                   waits=[("ptsm", p.cnt("ptsm"))], inc=("pvsm", 1))

        tr_small(bdt_sb[:], praw[0:DT, 0:128])
        tr_small(d_sb[:], praw[0:DT, 128:256])
        tr_small(b1_sb[:], praw[0:DT, 256:384])
        tr_small(b2_sb[:], praw[0:DT, 384:512])

        for k in range(4):
            src = praw2[0:DT, :].rearrange("a (pp k) -> a k pp", k=4)[:, k]
            p.step("t", lambda e, s=src: e.transpose(ps_t1[0:128, 0:DT], s, id_f[0:DT, 0:DT]),
                   waits=[(L, nld), ("pvsm", p.cnt("pvsm"))], inc=("ptsm", 1))
            dst = w1_sb[:].rearrange("a (d k) -> a k d", k=4)[:, k]
            p.step("v", lambda e, d=dst: e.tensor_copy(out=d, in_=ps_t1[0:128, 0:DT]),
                   waits=[("ptsm", p.cnt("ptsm"))], inc=("pvsm", 1))
        for k in range(4):
            src = pw0[0:DT, 0:512].rearrange("a (pp k) -> a k pp", k=4)[:, k]
            p.step("t", lambda e, s=src: e.transpose(ps_t1[0:128, 0:DT], s, id_f[0:DT, 0:DT]),
                   waits=[("pvsm", p.cnt("pvsm"))], inc=("ptsm", 1))
            dst = w2_sb[:].rearrange("a (d k) -> a k d", k=4)[:, k]
            p.step("v", lambda e, d=dst: e.tensor_copy(out=d, in_=ps_t1[0:128, 0:DT]),
                   waits=[("ptsm", p.cnt("ptsm"))], inc=("pvsm", 1))

        p.step("t", lambda e: e.transpose(ps_t1[0:128, 16:24], pw0[0:8, 1024:1152], id_f[0:8, 0:8]),
               waits=[(L, nld), ("pvsm", p.cnt("pvsm"))], inc=("ptsm", 1))
        p.step("t", lambda e: e.transpose(ps_t1[0:128, 24:32], pw0[0:8, 1280:1408], id_f[0:8, 0:8]),
               inc=("ptsm", 1))
        p.step("v", lambda e: e.tensor_copy(out=bias_f[:], in_=ps_t1[0:128, 16:24]),
               waits=[("ptsm", p.cnt("ptsm"))], inc=("pvsm", 1))
        p.step("a", lambda e: e.activation(out=ct_sb[:], in_=ps_t1[0:128, 24:32], func=AF.Silu),
               waits=[("ptsm", p.cnt("ptsm"))], inc=("pa", 1))
        PTSM_SMALL = p.cnt("ptsm")
        PA_CT = p.cnt("pa")

        p.step("a", lambda e: e.activation(out=gt_sb[0:1, 2:4], in_=gt_sb[0:1, 0:2], func=AF.Exp),
               waits=[(L, nld)], inc=("pa", 1))

        p.step("v", lambda e: e.tensor_tensor(out=gt_sb[0:1, 4:5], in0=gt_sb[0:1, 2:3],
                                              in1=gt_sb[0:1, 3:4], op=ALU.add),
               waits=[("pa", p.cnt("pa"))], inc=("gs", 1))
        p.step("v", lambda e: e.reciprocal(out=gt_sb[0:1, 5:6], in_=gt_sb[0:1, 4:5]),
               waits=[("gs", 1)], inc=("gs", 1))
        p.step("v", lambda e: e.tensor_tensor(out=gt_sb[0:1, 6:7], in0=gt_sb[0:1, 2:3],
                                              in1=gt_sb[0:1, 5:6], op=ALU.mult),
               waits=[("gs", 2)], inc=("gs", 1))
        p.step("v", lambda e: e.tensor_tensor(out=gt_sb[0:1, 7:8], in0=gt_sb[0:1, 3:4],
                                              in1=gt_sb[0:1, 5:6], op=ALU.mult),
               waits=[("gs", 3)], inc=("pv", 1))
        p.step("t", lambda e: e.matmul(ps_r[0:128, 0:1], ones_f[0:1, :], gt_sb[0:1, 6:7],
                                       start=True, stop=True),
               waits=[("pv", p.cnt("pv"))], inc=("pt", 1))
        p.step("t", lambda e: e.matmul(ps_r[0:128, 1:2], ones_f[0:1, :], gt_sb[0:1, 7:8],
                                       start=True, stop=True), inc=("pt", 1))
        p.step("t", lambda e: e.matmul(ps_r[0:128, 2:3], ones_f[0:1, :], pw0[0:1, 1536:1537],
                                       start=True, stop=True), waits=[(L, nld)], inc=("pt", 1))
        p.step("v", lambda e: e.tensor_copy(out=g0_bc[:], in_=ps_r[0:128, 0:1]),
               waits=[("pt", p.cnt("pt"))], inc=("pv", 1))
        p.step("v", lambda e: e.tensor_copy(out=g1_bc[:], in_=ps_r[0:128, 1:2]), inc=("pv", 1))
        p.step("v", lambda e: e.tensor_copy(out=ws_bc[:], in_=ps_r[0:128, 2:3]), inc=("pv", 1))
        PV_G = p.cnt("pv")

        for kt in range(8):
            buf = (pmod0, pmod1)[kt % 2]
            p.step("s", lambda e, b=buf, k=kt: e.dma_start(out=b[:], in_=wmod_d[k * 128:(k + 1) * 128, :]),
                   waits=[("ptm", max(0, p.cnt("ptm") - 48))], inc=("wm", 16))
            for jt in range(48):
                p.step("t", lambda e, b=buf, j=jt, k=kt: e.matmul(
                    ps_a[0:128, j:j + 1], b[:, j * 128:(j + 1) * 128], ct_sb[:, k:k + 1],
                    start=True, stop=True),
                    waits=([("wm", 16 * (kt + 1)), ("pa", p.cnt("pa")),
                            ("pmodf", max(0, kt))] if jt == 0 else []),
                    inc=("ptm", 1))
            if kt == 0:
                p.step("v", lambda e: e.tensor_copy(out=mod_sb[:], in_=ps_a[0:128, 0:48]),
                       waits=[("ptm", p.cnt("ptm"))], inc=("pmodf", 1))
            else:
                p.step("v", lambda e: e.tensor_tensor(out=mod_sb[:], in0=mod_sb[:],
                                                      in1=ps_a[0:128, 0:48], op=ALU.add),
                       waits=[("ptm", p.cnt("ptm"))], inc=("pmodf", 1))
        p.step("v", lambda e: e.engine_nop(), waits=[("pmodf", 8)], inc=("pv", 1))

        def em_modpost(e):
            ga_m, gb_m = mod_sb[:, 16:24], mod_sb[:, 40:48]
            e.tensor_scalar(out=ca1_sb[:], in0=mod_sb[:, 8:16], scalar1=1.0, scalar2=None, op0=ALU.add)
            e.tensor_copy(out=sa_sb[:], in_=mod_sb[:, 0:8])
            e.tensor_scalar(out=vec8[:, 0:8], in0=ga_m, scalar1=0.0, scalar2=None, op0=ALU.is_ge)
            e.tensor_scalar(out=vec8[:, 8:16], in0=gb_m, scalar1=0.0, scalar2=None, op0=ALU.is_ge)
            e.tensor_scalar(out=vec8[:, 16:24], in0=ga_m, scalar1=0.25, scalar2=None, op0=ALU.max)
            e.tensor_scalar(out=vec8[:, 24:32], in0=gb_m, scalar1=0.25, scalar2=None, op0=ALU.max)
            e.tensor_scalar(out=vec8[:, 32:40], in0=ga_m, scalar1=-0.25, scalar2=None, op0=ALU.min)
            e.tensor_scalar(out=vec8[:, 40:48], in0=gb_m, scalar1=-0.25, scalar2=None, op0=ALU.min)
            e.tensor_tensor(out=vec8[:, 16:24], in0=vec8[:, 16:24], in1=vec8[:, 0:8], op=ALU.mult)
            e.tensor_tensor(out=vec8[:, 24:32], in0=vec8[:, 24:32], in1=vec8[:, 8:16], op=ALU.mult)
            e.tensor_scalar(out=vec8[:, 0:8], in0=vec8[:, 0:8], scalar1=-1.0, scalar2=1.0,
                            op0=ALU.mult, op1=ALU.add)
            e.tensor_scalar(out=vec8[:, 8:16], in0=vec8[:, 8:16], scalar1=-1.0, scalar2=1.0,
                            op0=ALU.mult, op1=ALU.add)
            e.tensor_tensor(out=vec8[:, 32:40], in0=vec8[:, 32:40], in1=vec8[:, 0:8], op=ALU.mult)
            e.tensor_tensor(out=vec8[:, 40:48], in0=vec8[:, 40:48], in1=vec8[:, 8:16], op=ALU.mult)
            e.tensor_tensor(out=ga_sb[:], in0=vec8[:, 16:24], in1=vec8[:, 32:40], op=ALU.add)
            return e.tensor_tensor(out=vec8[:, 48:56], in0=vec8[:, 24:32], in1=vec8[:, 40:48],
                                   op=ALU.add)
        p.step("v", em_modpost, waits=[("pv", p.cnt("pv"))], inc=("mp", 1))
        p.step("v", lambda e: e.tensor_scalar(out=vec8[:, 56:64], in0=mod_sb[:, 32:40],
                                              scalar1=1.0, scalar2=None, op0=ALU.add),
               waits=[("mp", 1)], inc=("pv", 1))
        PV_MOD = p.cnt("pv")

        def make_rep(src_col_ap, dst):
            p.step("t", lambda e, s=src_col_ap: e.transpose(ps_t1[0:8, 0:128], s, id_f[:]),
                   waits=[("pv", PV_MOD), ("prv", p.cnt("prv")), ("pa", PA_CT)],
                   inc=("prt", 1))
            p.step("v", lambda e: e.tensor_copy(out=row8[:], in_=ps_t1[0:8, 0:128]),
                   waits=[("prt", p.cnt("prt")), ("prd", p.cnt("prd"))], inc=("prv", 1))
            def em_growdma(e, sems):
                for i in range(8):
                    e.dma_start(out=grow[0:1, i * 128:(i + 1) * 128],
                                in_=row8[i:i + 1, :]).then_inc(sems["prd"], 16)
            p.step("s", em_growdma,
                   waits=[("prv", p.cnt("prv")), ("prt2", p.cnt("prt2"))], inc=("prd", 128), raw=True)
            for half in range(2):
                p.step("t", lambda e, h=half: e.matmul(
                    ps_r[0:128, 0:512] if h == 0 else ps_r[0:128, 0:512],
                    ones_f[0:1, :], grow[0:1, h * 512:(h + 1) * 512],
                    start=True, stop=True),
                    waits=[("prd", p.cnt("prd")), ("prv2", p.cnt("prv2"))], inc=("prt2", 1))
                p.step("v", lambda e, d=dst, h=half: e.tensor_copy(
                    out=d[:, h * 512:(h + 1) * 512], in_=ps_r[0:128, 0:512]),
                    waits=[("prt2", p.cnt("prt2"))], inc=("prv2", 1))

        make_rep(vec8[:, 56:64], cb1r)
        make_rep(mod_sb[:, 24:32], sbr)
        make_rep(vec8[:, 48:56], gbr)
        make_rep(bias_f[:], gbiasr)
        p.step("v", lambda e: e.tensor_tensor(out=gbiasr[:], in0=gbiasr[:], in1=gbr[:], op=ALU.mult),
               waits=[("prv2", p.cnt("prv2"))], inc=("pv", 1))

        p.step("v", lambda e: e.tensor_copy(out=pwxb[:], in_=pwx[:]),
               waits=[(L, nld)], inc=("pwxc", 1))
        for dd in range(DT):
            p.step("t", lambda e, d=dd: e.transpose(
                ps_c1[0:128, 0:96], pwxb[0:96, d * 128:(d + 1) * 128], id_bf[0:96, 0:96]),
                waits=[("pwxc", 1), ("pwxv", p.cnt("pwxv"))], inc=("pwxt", 1))
            p.step("v", lambda e, d=dd: e.tensor_copy(
                out=wxT[:, d * 96:(d + 1) * 96], in_=ps_c1[0:128, 0:96]),
                waits=[("pwxt", p.cnt("pwxt"))], inc=("pwxv", 1))

        p.step("v", lambda e: e.tensor_copy(out=pwb[:, 0:DT * 64], in_=pw1[:, 0:DT * 64]),
               waits=[(L, nld)], inc=("pdtc", 1))
        for dd in range(DT):
            p.step("t", lambda e, d=dd: e.transpose(
                ps_c2[0:64, 0:128], pwb[:, d * 64:(d + 1) * 64], id_bf[:]),
                waits=[("pdtc", 1), ("pdtv", p.cnt("pdtv"))], inc=("pdtt", 1))
            p.step("v", lambda e, d=dd: e.tensor_copy(
                out=wdtT[0:64, d * 128:(d + 1) * 128], in_=ps_c2[0:64, 0:128]),
                waits=[("pdtt", p.cnt("pdtt"))], inc=("pdtv", 1))
        PDTV = p.cnt("pdtv")

        for j in range(8):
            src = ppk[:, (j % 2) * 1024:(j % 2) * 1024 + 1024]
            sh = 2 * (j // 2)

            def em_unpack(e, s=src, sh_=sh):
                e.tensor_scalar(out=pi1[:], in0=s, scalar1=sh_, scalar2=3,
                                op0=ALU.logical_shift_right, op1=ALU.bitwise_and)
                e.tensor_copy(out=pw0[:, 0:1024], in_=pi1[:])
                return e.tensor_scalar(out=pwb[:, 0:1024], in0=pw0[:, 0:1024], scalar1=1.0,
                                       scalar2=None, op0=ALU.subtract)
            p.step("v", em_unpack,
                   waits=[(L, nld), ("ptsm", PTSM_SMALL), ("pdtt", p.cnt("pdtt")),
                          ("pft", p.cnt("pft"))],
                   inc=("pfu", 1))
            for kt in range(8):
                p.step("t", lambda e, k=kt: e.transpose(
                    ps_c1[0:128, 0:128], pwb[:, k * 128:(k + 1) * 128], id_bf[:]),
                    waits=[("pfu", p.cnt("pfu")), ("pfv", p.cnt("pfv")), ("pwxv", DT)],
                    inc=("pft", 1))
                p.step("v", lambda e, k=kt: e.tensor_copy(
                    out=pwb[:, 1024 + k * 128:1024 + (k + 1) * 128], in_=ps_c1[0:128, 0:128]),
                    waits=[("pft", p.cnt("pft")), ("pfs", 16 * j)], inc=("pfv", 1))
            p.step("s", lambda e, j_=j: e.dma_start(out=wffnd[j_], in_=pwb[:, 1024:2048]),
                   waits=[("pfv", p.cnt("pfv"))], inc=("pfs", 16))
        PFT = p.cnt("pft")

        for ot in range(8):
            buf = (pw0, pw1)[ot % 2]
            p.step("s", lambda e, b=buf, o=ot: e.dma_start(
                out=b[:, 0:DI], in_=w_out_d[o * 128:(o + 1) * 128, :]),
                waits=[("pov", max(0, p.cnt("pov") - DT)), ("pfu", p.cnt("pfu"))],
                inc=("pod", 16))

            def em_wout_cast(e, b=buf, o=ot):
                e.tensor_scalar(out=b[:, 0:DI], in0=b[:, 0:DI], scalar1=ga_sb[:, o:o + 1],
                                scalar2=None, op0=ALU.mult)
                return e.tensor_copy(out=pwb[:, 0:DI], in_=b[:, 0:DI])
            p.step("v", em_wout_cast,
                   waits=[("pod", 16 * (ot + 1)), ("pv", PV_MOD), ("pft", PFT),
                          ("pot", p.cnt("pot")), ("pfs", 128)],
                   inc=("poc", 1))
            for dd in range(DT):
                p.step("t", lambda e, d=dd: e.transpose(
                    ps_c1[0:128, 0:128], pwb[:, d * 128:(d + 1) * 128], id_bf[:]),
                    waits=[("poc", p.cnt("poc")), ("pov", p.cnt("pov")), ("pfv", 64)],
                    inc=("pot", 1))
                p.step("v", lambda e, d=dd: e.tensor_copy(
                    out=pwb2[:, d * 128:(d + 1) * 128], in_=ps_c1[0:128, 0:128]),
                    waits=[("pot", p.cnt("pot")), ("pos", 16 * ot)], inc=("pov", 1))
            p.step("s", lambda e, o=ot: e.dma_start(
                out=woutd[o], in_=pwb2[:, 0:DT * 128]),
                waits=[("pov", p.cnt("pov"))], inc=("pos", 16))
        POT = p.cnt("pot")

        for et in range(NE):
            buf = (pw0, pw1)[et % 2]
            p.step("s", lambda e, b=buf, E=et: e.dma_start(
                out=b[:, 0:1024], in_=w_in_d[E * 128:(E + 1) * 128, :]),
                waits=[("pic", max(0, p.cnt("pic") - 1)), ("pov", p.cnt("pov"))],
                inc=("pid", 16))
            p.step("v", lambda e, b=buf: e.tensor_copy(out=pwb[:, 0:1024], in_=b[:, 0:1024]),
                   waits=[("pid", 16 * (et + 1)), ("pot", POT), ("pit", p.cnt("pit"))],
                   inc=("pic", 1))
            for kt in range(8):
                p.step("t", lambda e, k=kt: e.transpose(
                    ps_c1[0:128, 0:128], pwb[:, k * 128:(k + 1) * 128], id_bf[:]),
                    waits=[("pic", p.cnt("pic")), ("piv", p.cnt("piv")), ("pov", 8 * DT)],
                    inc=("pit", 1))
                p.step("v", lambda e, k=kt: e.tensor_copy(
                    out=pwb[:, 1024 + k * 128:1024 + (k + 1) * 128], in_=ps_c1[0:128, 0:128]),
                    waits=[("pit", p.cnt("pit")), ("pis", 16 * et)], inc=("piv", 1))
            p.step("s", lambda e, E=et: e.dma_start(out=wintd[E], in_=pwb[:, 1024:2048]),
                   waits=[("piv", p.cnt("piv"))], inc=("pis", 16))

        p.step("g", lambda e: e.memset(hstate[:], 0.0), inc=("pg", 1))
        p.step("g", lambda e: e.memset(xconv[:], 0.0),
               waits=[(L, nld), ("pis", p.cnt("pis")), ("piv", p.cnt("piv")),
                      ("pov", p.cnt("pov")), ("pfv", p.cnt("pfv")),
                      ("pwxv", p.cnt("pwxv")), ("pdtv", p.cnt("pdtv")),
                      ("pvsm", p.cnt("pvsm")), ("prv2", p.cnt("prv2")),
                      ("pv", p.cnt("pv")), ("pa", p.cnt("pa")),
                      ("ptm", p.cnt("ptm")), ("pit", p.cnt("pit")),
                      ("pot", p.cnt("pot")), ("pft", p.cnt("pft")),
                      ("prd", p.cnt("prd")), ("wm", p.cnt("wm")),
                      ("pod", p.cnt("pod")), ("pid", p.cnt("pid")),
                      ("pmodf", p.cnt("pmodf")), ("pfu", p.cnt("pfu")),
                      ("poc", p.cnt("poc")), ("pic", p.cnt("pic"))],
               inc=("pg", 1))
        p.step("g", lambda e: e.memset(cst[:, 0:1], LN_EPS), inc=("pg", 1))
        p.step("g", lambda e: e.memset(cst[:, 1:2], -0.5), inc=("pg", 1))
        p.step("g", lambda e: e.memset(cst[:, 2:3], 1.0), inc=("pg", 1))

        p.step("s", lambda e: e.dma_start(out=gt_sb[0:1, 0:1], in_=ones_d[0:1, 0:1]),
               waits=[("pis", p.cnt("pis")), (L, nld), ("wm", p.cnt("wm")),
                      ("pod", p.cnt("pod")), ("pid", p.cnt("pid")), ("prd", p.cnt("prd"))],
               inc=("pdone", 16))
        p.step("v", lambda e: e.engine_nop(),
               waits=[("pv", p.cnt("pv")), ("prv2", p.cnt("prv2")), ("pvsm", p.cnt("pvsm")),
                      ("pfv", p.cnt("pfv")), ("pov", p.cnt("pov")), ("piv", p.cnt("piv")),
                      ("pwxv", p.cnt("pwxv")), ("pdtv", PDTV)],
               inc=("pdone", 1))
        p.step("a", lambda e: e.copy(out=gt_sb[0:1, 1:2], in_=gt_sb[0:1, 1:2]),
               waits=[("pa", p.cnt("pa"))], inc=("pdone", 1))
        p.step("t", lambda e: e.matmul(ps_r[0:128, 3:4], ones_f[0:1, :], ones_f[0:1, 0:1],
                                       start=True, stop=True),
               waits=[("ptm", p.cnt("ptm")), ("prt2", p.cnt("prt2")), ("pit", p.cnt("pit")),
                      ("prv2", p.cnt("prv2")), ("pv", p.cnt("pv"))],
               inc=("pdone", 1))
        p.step("g", lambda e: e.engine_nop(), waits=[("pg", p.cnt("pg"))], inc=("pdone", 1))
        PD = p.cnt("pdone")

        ET = mybir.EngineType
        slreg = list(nc.alloc_registers("slreg", engines=[ET.Pool]))[0]
        pcnt = list(nc.alloc_registers("pollcnt", engines=[ET.Pool]))[0]
        fvreg = list(nc.alloc_registers("fvreg", engines=[ET.Pool]))[0]
        slreg_sp = list(nc.alloc_registers("slreg_sp", engines=[ET.SP]))[0]
        spk = {"ws": 0}

        def em_flagtab(g, sems):
            g.memset(flagtab[0:1, 0:1], 999)
            g.memset(flagtab[0:1, 1:2], 1000)
            g.memset(flagtab[0:1, 2:3], 999)
            for c_ in range(NCH):
                vals = [(1001 + 2 * c_, 1000 + 2 * c_), (1001 + 2 * c_, 1001 + 2 * c_),
                        (1002 + 2 * c_, 1001 + 2 * c_), (1002 + 2 * c_, 1002 + 2 * c_)]
                for e_, (wv, rv) in enumerate(vals):
                    off = 4 + 2 * (4 * c_ + e_)
                    g.memset(flagtab[0:1, off:off + 1], wv)
                    g.memset(flagtab[0:1, off + 1:off + 2], rv)
            g.engine_nop().then_inc(sems["pfl"], 1)
        p.step("g", em_flagtab, inc=("pfl", 1), raw=True)

        def sp_slotw(e, sems, tag, dst0, dst1, srcap):
            nc.br_cmp(slreg_sp, 0, on_true=f"sw0_{tag}", on_false=f"sw1_{tag}",
                      comp_op="IS_EQ", engines=[ET.SP])
            with nc.body(f"sw0_{tag}", valid_engines=[ET.SP]):
                e.dma_start(out=dst0, in_=srcap).then_inc(sems["sws"], 16)
                nc.br(f"swj_{tag}", engines=[ET.SP])
            with nc.body(f"sw1_{tag}", valid_engines=[ET.SP]):
                e.dma_start(out=dst1, in_=srcap).then_inc(sems["sws"], 16)
                nc.br(f"swj_{tag}", engines=[ET.SP])
            nc.switch_bb(f"swj_{tag}")
            spk["ws"] += 16
            e.wait_ge(sems["sws"], spk["ws"])

        def sp_flagw(e, sems, tag, entry):
            off = 4 + 2 * entry
            sp_slotw(e, sems, f"f{tag}", shf[0, 0:1, 0:2], shf[1, 0:1, 0:2],
                     flagtab[0:1, off:off + 2])

        def gp_poll(g, sems, tag, word, target, exact=False):
            nc.br(f"pp_{tag}", engines=[ET.Pool])
            with nc.body(f"pp_{tag}", valid_engines=[ET.Pool]):
                g.reg_alu(pcnt, pcnt, 32, op=ALU.add)
                g.dma_start(out=fsb[0:1, 0:4], in_=shf[0, 0:1, 0:4]).then_inc(sems["xpl"], 16)
                g.dma_start(out=fsb[0:1, 4:8], in_=shf[1, 0:1, 0:4]).then_inc(sems["xpl"], 16)
                g.wait_ge(sems["xpl"], pcnt)
                nc.br_cmp(slreg, 0, on_true=f"pr1_{tag}", on_false=f"pr0_{tag}",
                          comp_op="IS_EQ", engines=[ET.Pool])
                with nc.body(f"pr0_{tag}", valid_engines=[ET.Pool]):
                    g.reg_load(fvreg, fsb[0:1, word:word + 1])
                    nc.br(f"pc_{tag}", engines=[ET.Pool])
                with nc.body(f"pr1_{tag}", valid_engines=[ET.Pool]):
                    g.reg_load(fvreg, fsb[0:1, 4 + word:5 + word])
                    nc.br(f"pc_{tag}", engines=[ET.Pool])
                nc.switch_bb(f"pc_{tag}")
                cop = "IS_EQ" if exact else "IS_GE"
                nc.br_cmp(fvreg, target, on_true=f"pdn_{tag}", on_false=f"pp_{tag}",
                          comp_op=cop, engines=[ET.Pool])
            nc.switch_bb(f"pdn_{tag}")

        p.step("s", lambda e: e.dma_start(out=slsb[:], in_=sl_d), inc=("sld", 16))

        def em_spinit(e, sems):
            e.reg_load(slreg_sp, slsb[0:1, 0:1])
            sp_slotw(e, sems, "init", shf[0, 0:1, 0:3], shf[1, 0:1, 0:3],
                     flagtab[0:1, 0:3])
            e.nop().then_inc(sems["sxi"], 1)
        p.step("s", em_spinit, waits=[("sld", 16), ("pfl", 1)], inc=("sxi", 1), raw=True)

        def em_xinit(g, sems):
            g.reg_load(slreg, slsb[0:1, 0:1])
            nc.reg_mov(pcnt, 0)
            gp_poll(g, sems, "init", 2, 999, exact=True)
            g.engine_nop().then_inc(sems["xin"], 1)
        p.step("g", em_xinit, waits=[("pdone", PD), ("sld", 16)], inc=("xin", 1), raw=True)

        x_r = x_d.rearrange("(c tt pp) f -> c tt pp f", pp=128, tt=TT)
        out_r = out_d.rearrange("(c tt pp) f -> c tt pp f", pp=128, tt=TT)

        for c in range(NCH):
            fw = [("pdone", PD)] if c == 0 else []

            def em_xload(e, sems, c_=c):
                for t in range(TT):
                    e.dma_start(out=x_tm[:, t * DM:(t + 1) * DM],
                                in_=x_r[c_, t]).then_inc(sems["x"], 16)
            p.step("s", em_xload, waits=fw + [("ffd", c * 8)], inc=("x", 16 * TT), raw=True)

            for tt in range(TT):
                p.step("a", lambda e, t=tt: e.activation(
                    out=out_tm[:, t * DM:(t + 1) * DM], in_=x_tm[:, t * DM:(t + 1) * DM],
                    func=AF.Square, accum_out=stat[:, 4 + t:5 + t]),
                    waits=fw + [("x", 16 * TT * (c + 1)), ("q", c), ("out", 16 * TT * c)], inc=("sq", 1))
                p.step("v", lambda e, t=tt: e.reduce_sum(
                    out=stat[:, t:t + 1], in_=x_tm[:, t * DM:(t + 1) * DM], axis=AX),
                    waits=fw + [("x", 16 * TT * (c + 1)), ("ff", c * 8)], inc=("vst", 1))

            def em_lnp_a(e):
                e.tensor_scalar(out=stat[:, 8:8 + TT], in0=stat[:, 0:TT], scalar1=1.0 / DM,
                                scalar2=None, op0=ALU.mult)
                e.tensor_scalar(out=stat[:, 12:12 + TT], in0=stat[:, 4:4 + TT], scalar1=1.0 / DM,
                                scalar2=None, op0=ALU.mult)
                return e.tensor_tensor(out=stat[:, 16:16 + TT], in0=stat[:, 8:8 + TT],
                                       in1=stat[:, 8:8 + TT], op=ALU.mult)

            def em_lnp_b(e):
                return e.tensor_tensor(out=stat[:, 12:12 + TT], in0=stat[:, 12:12 + TT],
                                       in1=stat[:, 16:16 + TT], op=ALU.subtract)

            def em_lnr(e):
                e.activation(out=stat[:, 20:20 + TT], in_=stat[:, 12:12 + TT], func=AF.Ln,
                             bias=cst[:, 0:1])
                return e.activation(out=stat[:, 24:24 + TT], in_=stat[:, 20:20 + TT], func=AF.Exp,
                                    scale=cst[:, 1:2])

            def em_nmr(e):
                return e.scalar_tensor_tensor(out=stat[:, 28:28 + TT], in0=stat[:, 8:8 + TT],
                                              scalar=-1.0, in1=stat[:, 24:24 + TT],
                                              op0=ALU.mult, op1=ALU.mult)

            p.step("v", em_lnp_a, waits=[("sq", (2 * c + 1) * TT), ("vst", p.cnt("vst"))],
                   inc=("vst", 1))
            p.step("v", em_lnp_b, waits=[("vst", p.cnt("vst"))], inc=("vst", 1))
            p.step("a", em_lnr, waits=[("vst", p.cnt("vst"))], inc=("ln1", 1))
            p.step("v", em_nmr, waits=[("ln1", c + 1)], inc=("vst", 1))
            VST_N1 = p.cnt("vst")

            for tt in range(TT):
                p.step("v", lambda e, t=tt: e.tensor_scalar(
                    out=out_tm[:, t * DM:(t + 1) * DM], in0=x_tm[:, t * DM:(t + 1) * DM],
                    scalar1=stat[:, 24 + t:25 + t], scalar2=stat[:, 28 + t:29 + t],
                    op0=ALU.mult, op1=ALU.add),
                    waits=[("vst", VST_N1), ("out", 16 * TT * c)], inc=("xlns", 1))
            for ft in range(FT):
                pst = (ps_t1, ps_t2)[ft % 2]
                for tt in range(TT):
                    p.step("t", lambda e, f=ft, t=tt, ps=pst: e.transpose(
                        ps[0:128, t * 128:(t + 1) * 128],
                        out_tm[:, t * DM + f * 128:t * DM + (f + 1) * 128], id_f[:]),
                        waits=([("xlns", c * TT + TT), ("tpf", max(0, c * FT + ft - 1)),
                                ("ffd", c * 8)] if tt == 0 else []),
                        inc=("tp", 1) if tt == TT - 1 else None)
                for tt in range(TT):
                    p.step("v", lambda e, f=ft, t=tt, ps=pst: e.tensor_scalar(
                        out=shbf[:, f * TC + t * 128:f * TC + (t + 1) * 128],
                        in0=ps[0:128, t * 128:(t + 1) * 128],
                        scalar1=ca1_sb[:, f:f + 1], scalar2=sa_sb[:, f:f + 1],
                        op0=ALU.mult, op1=ALU.add),
                        waits=([("tp", c * FT + ft + 1), ("ff", c * 8)] if tt == 0 else []),
                        inc=("tpf", 1) if tt == TT - 1 else None)

            for et in range(NE):
                ws = et % 2
                p.step("s", lambda e, E=et, w=ws: e.dma_start(
                    out=wbuf[:, w * DM:(w + 1) * DM], in_=wintd[E]),
                    waits=[("xz", max(0, c * NE + et - 1))], inc=("wd", 16))
                psm = (ps_a, ps_b)[et % 2]
                if et < NE // 2:
                    bw = ("mmfx", max(0, c * (NE // 2) + et - 1))
                else:
                    bw = ("mmfz", max(0, c * (NE // 2) + (et - NE // 2) - 1))
                for kt in range(8):
                    p.step("t", lambda e, k=kt, w=ws, ps=psm: e.matmul(
                        ps[:], wbuf[:, w * DM + k * 128:w * DM + (k + 1) * 128],
                        shbf[:, k * TC:(k + 1) * TC],
                        start=(k == 0), stop=(k == 7)),
                        waits=([("wd", 16 * (c * NE + et + 1)), ("tpf", c * FT + FT), bw,
                                ("ffe0", c * 8)] if kt == 0 else []),
                        inc=("xz", 1) if kt == 7 else None)
                if et < NE // 2:
                    p.step("v", lambda e, E=et, ps=psm: e.tensor_copy(
                        out=xconv[:, E * (TC + HALO) + HALO:(E + 1) * (TC + HALO)], in_=ps[:]),
                        waits=[("xz", c * NE + et + 1), ("wo", c * 8), ("halo", c)],
                        inc=("mmfx", 1))
                else:
                    p.step("a", lambda e, E=et, ps=psm: e.activation(
                        out=z_sb[:, (E - NE // 2) * TC:(E - NE // 2 + 1) * TC], in_=ps[:],
                        func=AF.Silu),
                        waits=[("xz", c * NE + et + 1), ("y2", c * DT)],
                        inc=("mmfz", 1))

            for dd in range(DT):
                base = dd * (TC + HALO)

                def em_c1(e, d=dd, b=base):
                    e.tensor_scalar(out=acc1[:], in0=xconv[:, b + 3:b + 3 + TC],
                                    scalar1=w1_sb[:, d * 4:d * 4 + 1], scalar2=None, op0=ALU.mult)
                    e.scalar_tensor_tensor(out=acc1[:], in0=xconv[:, b + 4:b + 4 + TC],
                                           scalar=w1_sb[:, d * 4 + 1:d * 4 + 2], in1=acc1[:],
                                           op0=ALU.mult, op1=ALU.add)
                    e.scalar_tensor_tensor(out=acc1[:], in0=xconv[:, b + 5:b + 5 + TC],
                                           scalar=w1_sb[:, d * 4 + 2:d * 4 + 3], in1=acc1[:],
                                           op0=ALU.mult, op1=ALU.add)
                    return e.scalar_tensor_tensor(out=acc1[:], in0=xconv[:, b + 6:b + 6 + TC],
                                                  scalar=w1_sb[:, d * 4 + 3:d * 4 + 4], in1=acc1[:],
                                                  op0=ALU.mult, op1=ALU.add)
                p.step("v", em_c1,
                       waits=[("mmfx", c * DT + dd + 1), ("cvs", max(0, c * DT + dd))],
                       inc=("cva", 1))

                def em_c2(e, d=dd, b=base):
                    e.tensor_scalar(out=acc2[:], in0=xconv[:, b + 0:b + 0 + TC],
                                    scalar1=w2_sb[:, d * 4:d * 4 + 1], scalar2=None, op0=ALU.mult)
                    e.scalar_tensor_tensor(out=acc2[:], in0=xconv[:, b + 2:b + 2 + TC],
                                           scalar=w2_sb[:, d * 4 + 1:d * 4 + 2], in1=acc2[:],
                                           op0=ALU.mult, op1=ALU.add)
                    e.scalar_tensor_tensor(out=acc2[:], in0=xconv[:, b + 4:b + 4 + TC],
                                           scalar=w2_sb[:, d * 4 + 2:d * 4 + 3], in1=acc2[:],
                                           op0=ALU.mult, op1=ALU.add)
                    return e.scalar_tensor_tensor(out=acc2[:], in0=xconv[:, b + 6:b + 6 + TC],
                                                  scalar=w2_sb[:, d * 4 + 3:d * 4 + 4], in1=acc2[:],
                                                  op0=ALU.mult, op1=ALU.add)
                p.step("v", em_c2, inc=("cva", 1))

                def em_si(e, d=dd):
                    e.activation(out=acc1[:], in_=acc1[:], func=AF.Silu, bias=b1_sb[:, d:d + 1])
                    return e.activation(out=acc2[:], in_=acc2[:], func=AF.Silu, bias=b2_sb[:, d:d + 1])
                p.step("a", em_si, waits=[("cva", (c * DT + dd) * 2 + 2)], inc=("cvs", 1))

                def em_gate(e, d=dd):
                    e.tensor_scalar(out=acc1[:], in0=acc1[:], scalar1=g0_bc[:], scalar2=None,
                                    op0=ALU.mult)
                    return e.scalar_tensor_tensor(out=xc_sb[:, d * TC:(d + 1) * TC], in0=acc2[:],
                                                  scalar=g1_bc[:], in1=acc1[:],
                                                  op0=ALU.mult, op1=ALU.add)
                p.step("v", em_gate, waits=[("cvs", c * DT + dd + 1)], inc=("cvg", 1))

            def em_halo(e):
                inst = None
                for d in range(DT):
                    bb = d * (TC + HALO)
                    inst = e.tensor_copy(out=xconv[:, bb:bb + HALO],
                                         in_=xconv[:, bb + TC:bb + TC + HALO])
                return inst
            p.step("g", em_halo, waits=[("cvg", c * DT + DT)], inc=("halo", 1))

            for dd in range(DT):
                p.step("t", lambda e, d=dd: e.matmul(
                    ps_a[0:96, :], wxT[:, d * 96:(d + 1) * 96], xc_sb[:, d * TC:(d + 1) * TC],
                    start=(d == 0), stop=(d == DT - 1)),
                    waits=([("cvg", c * DT + dd + 1)] +
                           ([("xde", c), ("mmfx", c * DT + DT), ("mmfz", c * DT + DT)]
                            if dd == 0 else [])),
                    inc=("xd", 1))
            p.step("v", lambda e: e.tensor_copy(out=xdblp[:], in_=ps_a[0:96, :]),
                   waits=[("xd", c * DT + DT), ("pdt", c * DT), ("sxw", c)], inc=("xde0", 1))

            def em_xclr(g, sems, c_=c):
                gp_poll(g, sems, f"xr{c_}", 1, 1000 + 2 * c_)
                g.engine_nop().then_inc(sems["xclr"], 1)
            p.step("g", em_xclr, waits=[("xin", 1)] if c == 0 else [],
                   inc=("xclr", 1), raw=True)

            def em_sxw(e, sems, c_=c):
                sp_slotw(e, sems, f"xd{c_}", shxd[0], shxd[1], xdblp[:])
                sp_flagw(e, sems, f"xw{c_}", 4 * c_ + 0)
                e.nop().then_inc(sems["sxw"], 1)
            p.step("s", em_sxw, waits=[("xde0", c + 1), ("xclr", c + 1), ("sxi", 1)],
                   inc=("sxw", 1), raw=True)

            def em_xrdy(g, sems, c_=c):
                gp_poll(g, sems, f"xw{c_}", 0, 1001 + 2 * c_)
                g.engine_nop().then_inc(sems["xrdy"], 1)
            p.step("g", em_xrdy, inc=("xrdy", 1), raw=True)

            def em_sxr(e, sems, c_=c):
                e.dma_start(out=xdb0[:], in_=shxd[0]).then_inc(sems["sws"], 16)
                e.dma_start(out=xdb1[:], in_=shxd[1]).then_inc(sems["sws"], 16)
                spk["ws"] += 32
                e.wait_ge(sems["sws"], spk["ws"])
                e.nop().then_inc(sems["sxr"], 1)
                sp_flagw(e, sems, f"xa{c_}", 4 * c_ + 1)
            p.step("s", em_sxr, waits=[("xrdy", c + 1), ("xde", c)],
                   inc=("sxr", 1), raw=True)

            def em_xde(e):
                e.tensor_tensor(out=xdblp[:], in0=xdb0[:], in1=xdb1[:], op=ALU.add)
                e.tensor_copy(out=dtlow[:], in_=xdblp[0:64, :])
                return e.tensor_copy(out=xdbc[:], in_=xdblp[64:96, :])
            p.step("v", em_xde, waits=[("sxr", c + 1)], inc=("xde", 1))

            def em_dtmm(dd):
                psd = (ps_a, ps_b)[dd % 2]
                p.step("t", lambda e, d=dd, ps=psd: e.matmul(
                    ps[:], wdtT[0:64, d * 128:(d + 1) * 128], dtlow[:], start=True, stop=True),
                    waits=([("xde", c + 1), ("dl", max(0, c * DT + dd - 1))] +
                           ([("mmfx", c * DT + DT), ("mmfz", c * DT + DT)] if dd < 2 else [])),
                    inc=("pdt", 1))
            em_dtmm(0)
            em_dtmm(1)

            border = [16 * k + n for n in range(16) for k in range(2)]
            for idx, j in enumerate(border):
                psc = (ps_t1, ps_t2)[idx % 2]
                p.step("t", lambda e, j_=j, ps=psc: e.matmul(
                    ps[:], sel_sb[:, j_ * 128:(j_ + 1) * 128], xdbc[:], start=True, stop=True),
                    waits=[("xde", c + 1), ("bcf", max(0, c * 32 + idx - 1)),
                           ("tpf", c * FT + FT)],
                    inc=("bc", 1))
                dst = brep if j < 16 else crep
                jn = j % 16
                p.step("a", lambda e, d=dst, n_=jn, ps=psc: e.copy(
                    out=d[:, n_ * TC:(n_ + 1) * TC], in_=ps[:]),
                    waits=[("bc", c * 32 + idx + 1), ("sl", c * SL)],
                    inc=("bcf", 1))

            for dd in range(DT):
                psd = (ps_a, ps_b)[dd % 2]
                if dd >= 2:
                    em_dtmm(dd)
                p.step("v", lambda e, d=dd, ps=psd: e.tensor_scalar(
                    out=sm_sl[:], in0=ps[:], scalar1=bdt_sb[:, d:d + 1], scalar2=30.0,
                    op0=ALU.add, op1=ALU.min),
                    waits=[("pdt", c * DT + dd + 1), ("sfl", max(0, c * DT + dd))],
                    inc=("sm", 1))

                def em_sfl(e):
                    e.activation(out=mixfm[:], in_=sm_sl[:], func=AF.Exp)
                    return e.activation(out=sm_sl[:], in_=mixfm[:], func=AF.Ln, bias=cst[:, 2:3])
                p.step("a", em_sfl,
                       waits=[("sm", c * DT + dd + 1), ("dl", max(0, c * DT + dd))],
                       inc=("sfl", 1))

                def em_dl(e, d=dd, ps=psd):
                    e.scalar_tensor_tensor(out=dl_sl[:], in0=ps[:], scalar=bdt_sb[:, d:d + 1],
                                           in1=sm_sl[:], op0=ALU.add, op1=ALU.max)
                    return e.tensor_tensor(out=g_sl[:], in0=dl_sl[:],
                                           in1=xc_sb[:, d * TC:(d + 1) * TC], op=ALU.mult)
                p.step("v", em_dl,
                       waits=[("sfl", c * DT + dd + 1), ("da", c * SL + dd * 16)],
                       inc=("dl", 1))

                for n in range(N):
                    si = dd * N + n
                    ds = si % 2
                    p.step("a", lambda e, d=dd, n_=n, s=ds: e.activation(
                        out=da_sl[:, s * TC:(s + 1) * TC], in_=dl_sl[:], func=AF.Exp,
                        scale=a_sb[:, d * N + n_:d * N + n_ + 1]),
                        waits=[("dl", c * DT + dd + 1), ("scn", max(0, c * SL + si - 1))],
                        inc=("da", 1))

                    def em_slab(e, d=dd, n_=n, s=si, dsl=ds):
                        hs = s % 2
                        e.tensor_tensor(out=u_sl[:], in0=g_sl[:],
                                        in1=brep[:, n_ * TC:(n_ + 1) * TC], op=ALU.mult)
                        return e.tensor_tensor_scan(
                            out=h_sl[:, hs * TC:(hs + 1) * TC],
                            data0=da_sl[:, dsl * TC:(dsl + 1) * TC], data1=u_sl[:],
                            initial=hstate[:, d * N + n_:d * N + n_ + 1],
                            op0=ALU.mult, op1=ALU.add)
                    p.step("v", em_slab,
                           waits=[("da", c * SL + si + 1), ("st", max(0, c * SL + si - 1))] +
                                 ([("bcf", c * 32 + 2 * n + 1)] if dd == 0 else []),
                           inc=("scn", 1))

                    def em_y(e, d=dd, n_=n, s=si):
                        hs = s % 2
                        yv = xconv[:, d * (TC + HALO) + HALO:(d + 1) * (TC + HALO)]
                        if n_ == 0:
                            return e.tensor_tensor(out=yv, in0=h_sl[:, hs * TC:(hs + 1) * TC],
                                                   in1=crep[:, n_ * TC:(n_ + 1) * TC], op=ALU.mult)
                        e.tensor_tensor(out=t1b_sl[:], in0=h_sl[:, hs * TC:(hs + 1) * TC],
                                        in1=crep[:, n_ * TC:(n_ + 1) * TC], op=ALU.mult)
                        return e.tensor_tensor(out=yv, in0=yv, in1=t1b_sl[:], op=ALU.add)
                    p.step("v", em_y,
                           waits=([("halo", c + 1)] if n == 0 else []) +
                                 ([("bcf", c * 32 + 2 * n + 2)] if dd == 0 else []),
                           inc=("sl", 1))

                    p.step("g", lambda e, d=dd, n_=n, s=si: e.tensor_copy(
                        out=hstate[:, d * N + n_:d * N + n_ + 1],
                        in_=h_sl[:, (s % 2) * TC + TC - 1:(s % 2) * TC + TC]),
                        waits=[("scn", c * SL + si + 1)], inc=("st", 1))

                def em_y2(e, d=dd):
                    yv = xconv[:, d * (TC + HALO) + HALO:(d + 1) * (TC + HALO)]
                    e.scalar_tensor_tensor(out=yt_sl[:], in0=xc_sb[:, d * TC:(d + 1) * TC],
                                           scalar=d_sb[:, d:d + 1], in1=yv,
                                           op0=ALU.mult, op1=ALU.add)
                    return e.tensor_tensor(out=shbf[:, d * TC:(d + 1) * TC], in0=yt_sl[:],
                                           in1=z_sb[:, d * TC:(d + 1) * TC], op=ALU.mult)
                p.step("v", em_y2, waits=[("mmfz", c * (NE // 2) + NE // 2)], inc=("y2", 1))

            def em_mclr(g, sems, c_=c):
                gp_poll(g, sems, f"mr{c_}", 1, 1001 + 2 * c_)
                g.engine_nop().then_inc(sems["mclr"], 1)
            p.step("g", em_mclr, inc=("mclr", 1), raw=True)

            for ot in range(8):
                wsl = (ot % 2) * (DT * 128)
                if ot == 0:
                    p.step("s", lambda e, o=ot, w=wsl: e.dma_start(
                        out=wobuf[:, w:w + DT * 128], in_=woutd[o]),
                        waits=[("wo", max(0, c * 8 - 1))], inc=("wod", 16))
                psm = (ps_a, ps_b)[ot % 2]
                for kt in range(DT):
                    p.step("t", lambda e, o=ot, k=kt, w=wsl, ps=psm: e.matmul(
                        ps[:], wobuf[:, w + k * 128:w + (k + 1) * 128],
                        shbf[:, k * TC:(k + 1) * TC],
                        start=(k == 0), stop=(k == DT - 1)),
                        waits=([("y2", c * DT + DT), ("mxe", max(0, c * 8 + ot - 1)),
                                ("wod", 16 * (c * 8 + ot + 1))]
                               if kt == 0 else []),
                        inc=("wo", 1) if kt == DT - 1 else None)
                if ot < 7:
                    w2 = ((ot + 1) % 2) * (DT * 128)
                    p.step("s", lambda e, o=ot + 1, w=w2: e.dma_start(
                        out=wobuf[:, w:w + DT * 128], in_=woutd[o]),
                        waits=[("wo", c * 8 + ot)], inc=("wod", 16))
                msl = (ot % 2) * TC
                p.step("v", lambda e, m=msl, ps=psm: e.tensor_copy(
                    out=mixst[:, m:m + TC], in_=ps[:]),
                       waits=[("wo", c * 8 + ot + 1), ("smw", max(0, c * 8 + ot - 1))],
                       inc=("mxe", 1))

                def em_smw(e, sems, c_=c, o=ot, m=msl):
                    sp_slotw(e, sems, f"mx{c_}_{o}", shmx[0, o], shmx[1, o],
                             mixst[:, m:m + TC])
                    if o == 7:
                        sp_flagw(e, sems, f"mw{c_}", 4 * c_ + 2)
                    e.nop().then_inc(sems["smw"], 1)
                p.step("s", em_smw,
                       waits=[("mxe", c * 8 + ot + 1)] + ([("mclr", c + 1)] if ot == 0 else []),
                       inc=("smw", 1), raw=True)

            def em_mrdy(g, sems, c_=c):
                gp_poll(g, sems, f"mw{c_}", 0, 1002 + 2 * c_)
                g.engine_nop().then_inc(sems["mrdy"], 1)
            p.step("g", em_mrdy, inc=("mrdy", 1), raw=True)

            for ot in range(8):
                def em_smr(e, sems, c_=c, o=ot):
                    pb = (o % 2) * TC
                    e.dma_start(out=mx0[:, pb:pb + TC], in_=shmx[0, o]).then_inc(sems["sws"], 16)
                    spk["ws"] += 16
                    e.wait_ge(sems["sws"], spk["ws"])
                    e.nop().then_inc(sems["smr"], 1)
                    if o == 7:
                        e.wait_ge(sems["aws"], 16 * 8 * (c_ + 1))
                        sp_flagw(e, sems, f"ma{c_}", 4 * c_ + 3)
                p.step("s", em_smr,
                       waits=([("mrdy", c + 1)] if ot == 0 else []) +
                             [("mxs", max(0, c * 8 + ot - 1))],
                       inc=("smr", 1), raw=True)

                def em_amr(a, sems, c_=c, o=ot):
                    pb = (o % 2) * TC
                    a.dma_start(out=mx1[:, pb:pb + TC], in_=shmx[1, o]).then_inc(sems["aws"], 16)
                p.step("a", em_amr,
                       waits=([("mrdy", c + 1)] if ot == 0 else []) +
                             [("mxs", max(0, c * 8 + ot - 1))],
                       inc=("aws", 16), raw=True)
                pb = (ot % 2) * TC
                p.step("v", lambda e, b=pb: e.tensor_tensor(
                    out=mixfm[:], in0=mx0[:, b:b + TC], in1=mx1[:, b:b + TC], op=ALU.add),
                       waits=[("smr", c * 8 + ot + 1), ("aws", 16 * (c * 8 + ot + 1)),
                              ("mxt", max(0, c * 8 + ot))],
                       inc=("mxs", 1))
                pst = (ps_t1, ps_t2)[ot % 2]
                for tt in range(TT):
                    p.step("t", lambda e, t=tt, ps=pst: e.transpose(
                        ps[0:128, t * 128:(t + 1) * 128],
                        mixfm[:, t * 128:(t + 1) * 128], id_f[:]),
                        waits=([("mxs", c * 8 + ot + 1), ("x2", max(0, c * 8 + ot - 1)),
                                ("bcf", c * 32 + 32)]
                               if tt == 0 else []),
                        inc=("mxt", 1) if tt == TT - 1 else None)
                for tt in range(TT):
                    p.step("v", lambda e, o=ot, t=tt, ps=pst: e.tensor_tensor(
                        out=x_tm[:, t * DM + o * 128:t * DM + (o + 1) * 128],
                        in0=x_tm[:, t * DM + o * 128:t * DM + (o + 1) * 128],
                        in1=ps[0:128, t * 128:(t + 1) * 128], op=ALU.add),
                        waits=[("mxt", c * 8 + ot + 1)] if tt == 0 else [],
                        inc=("x2", 1) if tt == TT - 1 else None)

            for tt in range(TT):
                p.step("a", lambda e, t=tt: e.activation(
                    out=out_tm[:, t * DM:(t + 1) * DM], in_=x_tm[:, t * DM:(t + 1) * DM],
                    func=AF.Square, accum_out=stat[:, 4 + t:5 + t]),
                    waits=[("x2", c * 8 + 8), ("tp", c * FT + FT), ("out", 16 * TT * c)],
                    inc=("sq", 1))
                p.step("v", lambda e, t=tt: e.reduce_sum(
                    out=stat[:, t:t + 1], in_=x_tm[:, t * DM:(t + 1) * DM], axis=AX),
                    waits=[("x2", c * 8 + 8)], inc=("vst", 1))
            p.step("v", em_lnp_a, waits=[("sq", (2 * c + 2) * TT), ("vst", p.cnt("vst"))],
                   inc=("vst", 1))
            p.step("v", em_lnp_b, waits=[("vst", p.cnt("vst"))], inc=("vst", 1))
            p.step("a", em_lnr, waits=[("vst", p.cnt("vst"))], inc=("ln2", 1))
            p.step("v", em_nmr, waits=[("ln2", c + 1)], inc=("vst", 1))
            VST_N2 = p.cnt("vst")

            def em_h2(e):
                inst = None
                for tt in range(TT):
                    e.tensor_scalar(out=out_tm[:, tt * DM:(tt + 1) * DM],
                                    in0=x_tm[:, tt * DM:(tt + 1) * DM],
                                    scalar1=stat[:, 24 + tt:25 + tt], scalar2=stat[:, 28 + tt:29 + tt],
                                    op0=ALU.mult, op1=ALU.add)
                    e.tensor_tensor(out=out_tm[:, tt * DM:(tt + 1) * DM],
                                    in0=out_tm[:, tt * DM:(tt + 1) * DM], in1=cb1r[:], op=ALU.mult)
                    inst = e.tensor_tensor(out=out_tm[:, tt * DM:(tt + 1) * DM],
                                           in0=out_tm[:, tt * DM:(tt + 1) * DM], in1=sbr[:],
                                           op=ALU.add)
                return inst
            p.step("v", em_h2, waits=[("vst", VST_N2)], inc=("vst", 1))

            def em_q1a(e):
                inst = None
                for tt in range(TT):
                    inst = e.tensor_reduce(out=amax[:, tt:tt + 1],
                                           in_=out_tm[:, tt * DM:(tt + 1) * DM],
                                           axis=AX, op=ALU.max, apply_absolute_value=True)
                return inst
            p.step("v", em_q1a, inc=("qc", 1))
            p.step("v", lambda e: e.tensor_scalar(out=amax[:], in0=amax[:], scalar1=1e-5,
                                                  scalar2=None, op0=ALU.max),
                   waits=[("qc", p.cnt("qc"))], inc=("qc", 1))
            p.step("v", lambda e: e.reciprocal(out=as_s[:], in_=amax[:]),
                   waits=[("qc", p.cnt("qc"))], inc=("qc", 1))

            def em_q1b(e):
                e.tensor_scalar(out=was_s[:], in0=amax[:], scalar1=1.0 / 127.0, scalar2=None,
                                op0=ALU.mult)
                return e.tensor_scalar(out=as_s[:], in0=as_s[:], scalar1=127.0, scalar2=None,
                                       op0=ALU.mult)
            p.step("v", em_q1b, waits=[("qc", p.cnt("qc"))], inc=("qc", 1))
            p.step("v", lambda e: e.tensor_scalar(out=was_s[:], in0=was_s[:], scalar1=ws_bc[:],
                                                  scalar2=None, op0=ALU.mult),
                   waits=[("qc", p.cnt("qc"))], inc=("q1", 1))

            def em_q2(e):
                inst = None
                for tt in range(TT):
                    e.tensor_scalar(out=out_tm[:, tt * DM:(tt + 1) * DM],
                                    in0=out_tm[:, tt * DM:(tt + 1) * DM],
                                    scalar1=as_s[:, tt:tt + 1], scalar2=M_RND,
                                    op0=ALU.mult, op1=ALU.add)
                    e.tensor_scalar(out=out_tm[:, tt * DM:(tt + 1) * DM],
                                    in0=out_tm[:, tt * DM:(tt + 1) * DM],
                                    scalar1=M_RND, scalar2=127.0, op0=ALU.subtract, op1=ALU.min)
                    inst = e.tensor_scalar(out=shbf[:, 4096 + tt * DM:4096 + (tt + 1) * DM],
                                           in0=out_tm[:, tt * DM:(tt + 1) * DM],
                                           scalar1=-128.0, scalar2=None, op0=ALU.max)
                return inst
            p.step("v", em_q2, waits=[("q1", c + 1), ("qt", c * FT), ("wo", c * 8 + 8)],
                   inc=("q", 1))

            for ft in range(FT):
                psc = (ps_c1, ps_c2)[ft % 2]
                for tt in range(TT):
                    p.step("t", lambda e, f=ft, t=tt, ps=psc: e.transpose(
                        ps[0:128, t * 128:(t + 1) * 128],
                        shbf[:, 4096 + t * DM + f * 128:4096 + t * DM + (f + 1) * 128], id_bf[:]),
                        waits=([("q", c + 1), ("qf", max(0, c * FT + ft - 1))] if tt == 0 else []),
                        inc=("qt", 1) if tt == TT - 1 else None)
                p.step("a", lambda e, f=ft, ps=psc: e.copy(
                    out=shbf[:, f * TC:(f + 1) * TC], in_=ps[0:128, 0:TC]),
                    waits=[("qt", c * FT + ft + 1)], inc=("qf", 1))

            for ot in range(8):
                p.step("s", lambda e, o=ot: e.dma_start(out=wfbuf[:], in_=wffnd[o]),
                       waits=[("ff", c * 8 + ot)], inc=("wfd", 16))
                psm = (ps_a, ps_b)[ot % 2]
                for kt in range(8):
                    p.step("t", lambda e, o=ot, k=kt, ps=psm: e.matmul(
                        ps[:], wfbuf[:, k * 128:(k + 1) * 128],
                        shbf[:, k * TC:(k + 1) * TC],
                        start=(k == 0), stop=(k == 7)),
                        waits=([("qf", c * FT + FT), ("ffe0", max(0, c * 8 + ot - 1)),
                                ("wfd", 16 * (c * 8 + ot + 1))]
                               if kt == 0 else []),
                        inc=("ff", 1) if kt == 7 else None)
                p.step("v", lambda e, ps=psm: e.tensor_copy(out=mixfm[:], in_=ps[:]),
                       waits=[("ff", c * 8 + ot + 1), ("fft", max(0, c * 8 + ot))],
                       inc=("ffe0", 1))
                pst = (ps_t1, ps_t2)[ot % 2]
                for tt in range(TT):
                    p.step("t", lambda e, t=tt, ps=pst: e.transpose(
                        ps[0:128, t * 128:(t + 1) * 128],
                        mixfm[:, t * 128:(t + 1) * 128], id_f[:]),
                        waits=([("ffe0", c * 8 + ot + 1), ("ffd", max(0, c * 8 + ot - 1))]
                               if tt == 0 else []),
                        inc=("fft", 1) if tt == TT - 1 else None)

                def em_ffd(e, o=ot, ps=pst):
                    sl = lambda t: yt_sl[:, t * 128:(t + 1) * 128]
                    for tt in range(TT):
                        e.tensor_scalar(out=sl(tt), in0=ps[0:128, tt * 128:(tt + 1) * 128],
                                        scalar1=was_s[:, tt:tt + 1], scalar2=None, op0=ALU.mult)
                    for tt in range(TT):
                        e.tensor_tensor(out=sl(tt), in0=sl(tt),
                                        in1=gbr[:, o * 128:(o + 1) * 128], op=ALU.mult)
                    for tt in range(TT):
                        e.tensor_tensor(out=sl(tt), in0=sl(tt),
                                        in1=gbiasr[:, o * 128:(o + 1) * 128], op=ALU.add)
                    inst = None
                    for tt in range(TT):
                        col = tt * DM + o * 128
                        inst = e.tensor_tensor(out=out_tm[:, col:col + 128], in0=sl(tt),
                                               in1=x_tm[:, col:col + 128], op=ALU.add)
                    return inst
                p.step("v", em_ffd, waits=[("fft", c * 8 + ot + 1), ("out", 16 * TT * c)],
                       inc=("ffd", 1))

            def em_store(e, sems, c_=c):
                for t in range(TT):
                    e.dma_start(out=out_r[c_, t],
                                in_=out_tm[:, t * DM:(t + 1) * DM]).then_inc(sems["out"], 16)
            p.step("s", em_store, waits=[("ffd", c * 8 + 8)], inc=("out", 16 * TT), raw=True)

        p.step("s", lambda e: None, waits=[("out", 16 * TT * NCH)])

        if debug:
            dbg = {"xc_sb": xc_sb, "z_sb": z_sb, "brep": brep, "crep": crep,
                   "hstate": hstate, "x_tm": x_tm, "out_tm": out_tm,
                   "shbf": shbf, "xconv": xconv, "stat": stat, "amax": amax,
                   "as_s": as_s, "was_s": was_s, "mod_sb": mod_sb, "a_sb": a_sb,
                   "bdt_sb": bdt_sb, "w1_sb": w1_sb, "w2_sb": w2_sb,
                   "dl_sl": dl_sl, "g_sl": g_sl, "da_sl": da_sl, "u_sl": u_sl,
                   "h_sl": h_sl, "sm_sl": sm_sl, "ct_sb": ct_sb,
                   "ca1_sb": ca1_sb, "sa_sb": sa_sb, "ga_sb": ga_sb, "cb1r": cb1r,
                   "sbr": sbr, "gbr": gbr, "gbiasr": gbiasr, "mixfm": mixfm,
                   "dtlow": dtlow, "xdbc": xdbc, "wbuf": wbuf, "vec8": vec8}
            for nm, tsr in dbg.items():
                shp = list(tsr.shape)
                dd_ = nc.dram_tensor("dbg_" + nm, shp, tsr.dtype, kind="ExternalOutput").ap()
                p.step("s", lambda e, d=dd_, t=tsr: e.dma_start(out=d, in_=t[:]),
                       waits=[("out", 16 * TT * NCH)], inc=("dbg", 16))
            p.step("s", lambda e: None, waits=[("dbg", p.cnt("dbg"))])

        if trunc is not None:
            p.steps = p.steps[:trunc]
            import sys as _sys
            print(f"TRUNCATED to {trunc} of {len(p.steps)} steps", file=_sys.stderr)

        for extra in ("sws", "xpl", "aws"):
            p.sems.setdefault(extra, 0)
        sem_handles = {}
        for s in sorted(p.sems.keys()):
            sem_handles[s] = ctx.enter_context(nc.semaphore(s))

        with nc.Block() as block:
            def replay(tag):
                def body(eng):
                    for se, emit, waits, inc, raw in p.steps:
                        if se != tag:
                            continue
                        for nm, val in waits:
                            eng.wait_ge(sem_handles[nm], val)
                        if raw:
                            emit(eng, sem_handles)
                            continue
                        inst = emit(eng)
                        if inc is not None and inst is not None:
                            inst.then_inc(sem_handles[inc[0]], inc[1])
                return body

            block.sync(replay("s"))
            block.vector(replay("v"))
            block.scalar(replay("a"))
            block.tensor(replay("t"))
            block.gpsimd(replay("g"))

    return nc


_NC_CACHE = None


def kernel(**inputs):
    global _NC_CACHE
    x = inputs["L0_attn_hidden_states"]
    cc = inputs["L0_c"]
    ident = np.eye(128)
    sel = np.kron(np.eye(32), np.ones((1, 128))).astype(np.float32)

    win = np.ascontiguousarray(inputs["L0_attn_W_in"], np.float32)
    wout = np.ascontiguousarray(inputs["L0_attn_W_out"], np.float32)
    wx = np.ascontiguousarray(inputs["L0_attn_W_x"], np.float32)
    wdt = np.ascontiguousarray(inputs["L0_attn_W_dt"], np.float32)
    alog = np.ascontiguousarray(inputs["L0_attn_A_log"], np.float32)
    bdt = np.ascontiguousarray(inputs["L0_attn_b_dt"], np.float32).reshape(16, 128)
    dpar = np.ascontiguousarray(inputs["L0_attn_D_param"], np.float32).reshape(16, 128)
    cb1 = np.ascontiguousarray(inputs["L0_attn_conv_b1"], np.float32).reshape(16, 128)
    cb2 = np.ascontiguousarray(inputs["L0_attn_conv_b2"], np.float32).reshape(16, 128)
    cw1 = np.ascontiguousarray(inputs["L0_attn_conv_w1"], np.float32).reshape(16, 512)
    cw2 = np.ascontiguousarray(inputs["L0_attn_conv_w2"], np.float32).reshape(16, 512)

    common = {
        "gates": np.ascontiguousarray(inputs["L0_attn_conv_gates"], np.float32).reshape(1, 2),
        "packed": np.ascontiguousarray(inputs["L0_ffn_packed_weight"], np.int32).reshape(256, 1024),
        "wscale": np.ascontiguousarray(inputs["L0_ffn_weight_scale"], np.float32).reshape(1, 1),
        "fbias": np.ascontiguousarray(inputs["L0_ffn_bias"], np.float32).reshape(8, 128),
        "wmod": np.ascontiguousarray(inputs["L0_wmod"], np.float32),
        "idbf": ident.astype(BF),
        "idf": ident.astype(np.float32),
        "sel": sel.astype(BF),
        "onesr": np.ones((1, 128), np.float32),
    }

    if _NC_CACHE is None:
        _NC_CACHE = build_nc()
    nc = _NC_CACHE
    in_maps = []
    for cid in range(NCORES):
        b, h = cid // 2, cid % 2
        lo, hi = h * DI, (h + 1) * DI
        m = dict(common)
        m["x"] = np.ascontiguousarray(x[b], np.float32)
        m["cvec"] = np.ascontiguousarray(cc[b], np.float32).reshape(8, 128)
        m["w_in"] = np.ascontiguousarray(
            np.concatenate([win[lo:hi], win[2048 + lo:2048 + hi]], axis=0))
        m["w_out"] = np.ascontiguousarray(wout[:, lo:hi])
        m["w_x"] = np.ascontiguousarray(wx[:, lo:hi])
        m["w_dt"] = np.ascontiguousarray(wdt[lo:hi])
        m["a_log"] = np.ascontiguousarray(alog[lo:hi])
        m["b_dt"] = np.ascontiguousarray(bdt[h * DT:(h + 1) * DT])
        m["dpar"] = np.ascontiguousarray(dpar[h * DT:(h + 1) * DT])
        m["cb1"] = np.ascontiguousarray(cb1[h * DT:(h + 1) * DT])
        m["cb2"] = np.ascontiguousarray(cb2[h * DT:(h + 1) * DT])
        m["cw1"] = np.ascontiguousarray(cw1[h * DT:(h + 1) * DT])
        m["cw2"] = np.ascontiguousarray(cw2[h * DT:(h + 1) * DT])
        m["slid"] = np.array([[h]], np.int32)
        in_maps.append(m)

    res = run_bass_kernel_spmd(nc, in_maps, core_ids=list(range(NCORES)))
    out = np.stack([np.asarray(res.results[2 * b]["out"], np.float32) for b in range(B)])
    return out


# version 10
# speedup vs baseline: 1.1108x; 1.1108x over previous
import numpy as np
import ml_dtypes
from contextlib import ExitStack

import concourse.bass as bass
import concourse.mybir as mybir
from concourse.bass_utils import run_bass_kernel_spmd

F32 = mybir.dt.float32
BF16 = mybir.dt.bfloat16
I32 = mybir.dt.int32
AF = mybir.ActivationFunctionType
ALU = mybir.AluOpType
AX = mybir.AxisListType.X

B = 4
NCORES = 8
T = 4096
TC = 512
NCH = T // TC
TT = TC // 128
DM = 1024
FT = DM // 128
DI = 1024
DT = DI // 128
N = 16
SL = DT * N
RK = 64
NE = 2 * DI // 128
M_RND = 12582912.0
LN_EPS = 1e-5
HALO = 6

BF = ml_dtypes.bfloat16


class Plan:
    def __init__(self):
        self.sems = {}
        self.steps = []

    def cnt(self, name):
        return self.sems.get(name, 0)

    def step(self, eng, emit, waits=(), inc=None, raw=False):
        if inc is not None:
            name, amt = inc
            self.sems[name] = self.sems.get(name, 0) + amt
        w = [(n, v) for (n, v) in waits if v > 0]
        self.steps.append((eng, emit, w, None if raw else inc, raw))


def build_nc(trunc=None, debug=False):
    nc = bass.Bass("TRN2", target_bir_lowering=False, detect_race_conditions=False)

    x_d = nc.dram_tensor("x", [T, DM], F32, kind="ExternalInput").ap()
    w_in_d = nc.dram_tensor("w_in", [2 * DI, DM], F32, kind="ExternalInput").ap()
    w_out_d = nc.dram_tensor("w_out", [DM, DI], F32, kind="ExternalInput").ap()
    w_x_d = nc.dram_tensor("w_x", [96, DI], F32, kind="ExternalInput").ap()
    w_dt_d = nc.dram_tensor("w_dt", [DI, RK], F32, kind="ExternalInput").ap()
    a_log_d = nc.dram_tensor("a_log", [DI, N], F32, kind="ExternalInput").ap()
    b_dt_d = nc.dram_tensor("b_dt", [DT, 128], F32, kind="ExternalInput").ap()
    dpar_d = nc.dram_tensor("dpar", [DT, 128], F32, kind="ExternalInput").ap()
    cb1_d = nc.dram_tensor("cb1", [DT, 128], F32, kind="ExternalInput").ap()
    cb2_d = nc.dram_tensor("cb2", [DT, 128], F32, kind="ExternalInput").ap()
    cw1_d = nc.dram_tensor("cw1", [DT, 512], F32, kind="ExternalInput").ap()
    cw2_d = nc.dram_tensor("cw2", [DT, 512], F32, kind="ExternalInput").ap()
    gates_d = nc.dram_tensor("gates", [1, 2], F32, kind="ExternalInput").ap()
    packed_d = nc.dram_tensor("packed", [256, 1024], I32, kind="ExternalInput").ap()
    wscale_d = nc.dram_tensor("wscale", [1, 1], F32, kind="ExternalInput").ap()
    fbias_d = nc.dram_tensor("fbias", [8, 128], F32, kind="ExternalInput").ap()
    cvec_d = nc.dram_tensor("cvec", [8, 128], F32, kind="ExternalInput").ap()
    wmod_d = nc.dram_tensor("wmod", [DM, 6 * DM], F32, kind="ExternalInput").ap()
    idbf_d = nc.dram_tensor("idbf", [128, 128], BF16, kind="ExternalInput").ap()
    idf_d = nc.dram_tensor("idf", [128, 128], F32, kind="ExternalInput").ap()
    sel_d = nc.dram_tensor("sel", [32, 32 * 128], BF16, kind="ExternalInput").ap()
    ones_d = nc.dram_tensor("onesr", [1, 128], F32, kind="ExternalInput").ap()

    sl_d = nc.dram_tensor("slid", [1, 1], I32, kind="ExternalInput").ap()
    out_d = nc.dram_tensor("out", [T, DM], F32, kind="ExternalOutput").ap()
    shxd = nc.dram_tensor("shxd", [2, 96, TC], F32, kind="Internal",
                          addr_space="Shared").ap()
    shmx = nc.dram_tensor("shmx", [2, 8, 128, TC], F32, kind="Internal",
                          addr_space="Shared").ap()
    shf = nc.dram_tensor("shf", [2, 1, 4], I32, kind="Internal",
                         addr_space="Shared").ap()
    wintd = nc.dram_tensor("w_inT_d", [NE, 128, DM], BF16, kind="Internal").ap()
    woutd = nc.dram_tensor("w_outT_d", [8, 128, DT * 128], BF16, kind="Internal").ap()
    wffnd = nc.dram_tensor("w_ffnT_d", [8, 128, 8 * 128], BF16, kind="Internal").ap()

    p = Plan()

    with ExitStack() as ctx:
        sb = lambda name, shape, dt: ctx.enter_context(nc.sbuf_tensor(name, shape, dt))
        psb = lambda name, shape, dt: ctx.enter_context(nc.psum_tensor(name, shape, dt))

        wxT = sb("wxT", [128, DT * 96], BF16)
        wdtT = sb("wdtT", [64, DT * 128], BF16)
        a_sb = sb("a_sb", [128, DT * N], F32)
        bdt_sb = sb("bdt_sb", [128, DT], F32)
        d_sb = sb("d_sb", [128, DT], F32)
        b1_sb = sb("b1_sb", [128, DT], F32)
        b2_sb = sb("b2_sb", [128, DT], F32)
        w1_sb = sb("w1_sb", [128, DT * 4], F32)
        w2_sb = sb("w2_sb", [128, DT * 4], F32)
        ca1_sb = sb("ca1_sb", [128, FT], F32)
        sa_sb = sb("sa_sb", [128, FT], F32)
        ga_sb = sb("ga_sb", [128, FT], F32)
        cb1r = sb("cb1r", [128, DM], F32)
        sbr = sb("sbr", [128, DM], F32)
        gbr = sb("gbr", [128, DM], F32)
        gbiasr = sb("gbiasr", [128, DM], F32)
        ws_bc = sb("ws_bc", [128, 1], F32)
        g0_bc = sb("g0_bc", [128, 1], F32)
        g1_bc = sb("g1_bc", [128, 1], F32)
        id_bf = sb("id_bf", [128, 128], BF16)
        id_f = sb("id_f", [128, 128], F32)
        sel_sb = sb("sel_sb", [32, 32 * 128], BF16)
        ones_f = sb("ones_f", [1, 128], F32)
        hstate = sb("hstate", [128, SL], F32)
        mod_sb = sb("mod_sb", [128, 48], F32)
        ct_sb = sb("ct_sb", [128, 8], F32)
        gt_sb = sb("gt_sb", [1, 8], F32)
        vec8 = sb("vec8", [128, 64], F32)
        cst = sb("cst", [128, 4], F32)

        with ExitStack() as pctx:
            psbuf = lambda name, shape, dt: pctx.enter_context(nc.sbuf_tensor(name, shape, dt))
            pw0 = psbuf("pw0", [128, 2048], F32)
            pw1 = psbuf("pw1", [128, 2048], F32)
            pwb = psbuf("pwb", [128, 2048], BF16)
            pwb2 = psbuf("pwb2", [128, 2048], BF16)
            pmod0 = psbuf("pmod0", [128, 6144], F32)
            pmod1 = psbuf("pmod1", [128, 6144], F32)
            pwx = psbuf("pwx", [96, DI], F32)
            pwxb = psbuf("pwxb", [96, DI], BF16)
            ppk = psbuf("ppk", [128, 2048], I32)
            pi1 = psbuf("pi1", [128, 1024], I32)
            praw = psbuf("praw", [16, 512], F32)
            praw2 = psbuf("praw2", [16, 512], F32)
            pa_raw = psbuf("pa_raw", [128, DT * N], F32)
            grow = psbuf("grow", [1, 1024], F32)
            bias_f = psbuf("bias_f", [128, 8], F32)
            row8 = psbuf("row8", [8, 128], F32)

        x_tm = sb("x_tm", [128, TT * DM], F32)
        out_tm = sb("out_tm", [128, TT * DM], F32)
        xconv = sb("xconv", [128, DT * (TC + HALO)], BF16)
        z_sb = sb("z_sb", [128, DT * TC], BF16)
        xc_sb = sb("xc_sb", [128, DT * TC], BF16)
        shbf = sb("shbf", [128, 2 * FT * TC], BF16)
        brep = sb("brep", [128, N * TC], BF16)
        crep = sb("crep", [128, N * TC], BF16)
        wbuf = sb("wbuf", [128, 2 * DM], BF16)
        wobuf = sb("wobuf", [128, 2 * DT * 128], BF16)
        wfbuf = sb("wfbuf", [128, 8 * 128], BF16)
        dtlow = sb("dtlow", [64, TC], BF16)
        xdbc = sb("xdbc", [32, TC], BF16)
        xdblp = sb("xdblp", [96, TC], F32)
        xdb0 = sb("xdb0", [96, TC], F32)
        xdb1 = sb("xdb1", [96, TC], F32)
        mx0 = sb("mx0", [128, 2 * TC], F32)
        mx1 = sb("mx1", [128, 2 * TC], F32)
        slsb = sb("slsb", [1, 1], I32)
        fsb = sb("fsb", [1, 8], I32)
        flagtab = sb("flagtab", [1, 72], I32)
        mixst = sb("mixst", [128, 2 * TC], F32)
        sm_sl = sb("sm_sl", [128, TC], F32)
        dl_sl = sb("dl_sl", [128, TC], F32)
        g_sl = sb("g_sl", [128, TC], BF16)
        da_sl = sb("da_sl", [128, 2 * TC], F32)
        u_sl = sb("u_sl", [128, TC], BF16)
        h_sl = sb("h_sl", [128, N * TC], BF16)
        t1b_sl = sb("t1b_sl", [128, TC], BF16)
        yt_sl = sb("yt_sl", [128, TC], F32)
        acc1 = sb("acc1", [128, TC], F32)
        acc2 = sb("acc2", [128, TC], F32)
        mixfm = sb("mixfm", [128, TC], F32)
        stat = sb("stat", [128, 32], F32)
        amax = sb("amax", [128, TT], F32)
        as_s = sb("as_s", [128, TT], F32)
        was_s = sb("was_s", [128, TT], F32)

        ps_a = psb("ps_a", [128, TC], F32)
        ps_b = psb("ps_b", [128, TC], F32)
        ps_t1 = psb("ps_t1", [128, TC], F32)
        ps_t2 = psb("ps_t2", [128, TC], F32)
        ps_c1 = psb("ps_c1", [128, TC], BF16)
        ps_c2 = psb("ps_c2", [128, TC], BF16)
        ps_r = psb("ps_r", [128, 512], F32)

        L = "ld"

        def ld(dst, src):
            p.step("s", lambda e, d=dst, s=src: e.dma_start(out=d, in_=s), inc=(L, 16))

        ld(id_bf[:], idbf_d)
        ld(id_f[:], idf_d)
        ld(sel_sb[:], sel_d)
        ld(ones_f[:], ones_d)
        a_r = a_log_d.rearrange("(dt p) n -> dt p n", p=128)
        for dd in range(DT):
            ld(pa_raw[:, dd * N:(dd + 1) * N], a_r[dd])
        ld(praw[0:DT, 0:128], b_dt_d)
        ld(praw[0:DT, 128:256], dpar_d)
        ld(praw[0:DT, 256:384], cb1_d)
        ld(praw[0:DT, 384:512], cb2_d)
        ld(praw2[0:DT, 0:512], cw1_d)
        ld(pw0[0:DT, 0:512], cw2_d)
        ld(pw0[0:8, 1024:1152], fbias_d)
        ld(pw0[0:8, 1280:1408], cvec_d)
        ld(gt_sb[0:1, 0:2], gates_d)
        ld(pw0[0:1, 1536:1537], wscale_d)
        ld(pwx[:], w_x_d)
        wdt_r = w_dt_d.rearrange("(dt p) r -> dt p r", p=128)
        for dd in range(DT):
            ld(pw1[:, dd * 64:(dd + 1) * 64], wdt_r[dd])
        ld(ppk[:, 0:1024], packed_d[0:128, :])
        ld(ppk[:, 1024:2048], packed_d[128:256, :])
        nld = p.cnt(L)

        p.step("a", lambda e: e.activation(out=pa_raw[:], in_=pa_raw[:], func=AF.Exp),
               waits=[(L, nld)], inc=("pa", 1))
        p.step("v", lambda e: e.tensor_scalar(out=a_sb[:], in0=pa_raw[:], scalar1=-1.0,
                                              scalar2=None, op0=ALU.mult),
               waits=[("pa", p.cnt("pa"))], inc=("pv", 1))

        def tr_small(dst, src_ap):
            p.step("t", lambda e, s=src_ap: e.transpose(ps_t1[0:128, 0:DT], s, id_f[0:DT, 0:DT]),
                   waits=[(L, nld), ("pvsm", p.cnt("pvsm"))], inc=("ptsm", 1))
            p.step("v", lambda e, d=dst: e.tensor_copy(out=d, in_=ps_t1[0:128, 0:DT]),
                   waits=[("ptsm", p.cnt("ptsm"))], inc=("pvsm", 1))

        tr_small(bdt_sb[:], praw[0:DT, 0:128])
        tr_small(d_sb[:], praw[0:DT, 128:256])
        tr_small(b1_sb[:], praw[0:DT, 256:384])
        tr_small(b2_sb[:], praw[0:DT, 384:512])

        for k in range(4):
            src = praw2[0:DT, :].rearrange("a (pp k) -> a k pp", k=4)[:, k]
            p.step("t", lambda e, s=src: e.transpose(ps_t1[0:128, 0:DT], s, id_f[0:DT, 0:DT]),
                   waits=[(L, nld), ("pvsm", p.cnt("pvsm"))], inc=("ptsm", 1))
            dst = w1_sb[:].rearrange("a (d k) -> a k d", k=4)[:, k]
            p.step("v", lambda e, d=dst: e.tensor_copy(out=d, in_=ps_t1[0:128, 0:DT]),
                   waits=[("ptsm", p.cnt("ptsm"))], inc=("pvsm", 1))
        for k in range(4):
            src = pw0[0:DT, 0:512].rearrange("a (pp k) -> a k pp", k=4)[:, k]
            p.step("t", lambda e, s=src: e.transpose(ps_t1[0:128, 0:DT], s, id_f[0:DT, 0:DT]),
                   waits=[("pvsm", p.cnt("pvsm"))], inc=("ptsm", 1))
            dst = w2_sb[:].rearrange("a (d k) -> a k d", k=4)[:, k]
            p.step("v", lambda e, d=dst: e.tensor_copy(out=d, in_=ps_t1[0:128, 0:DT]),
                   waits=[("ptsm", p.cnt("ptsm"))], inc=("pvsm", 1))

        p.step("t", lambda e: e.transpose(ps_t1[0:128, 16:24], pw0[0:8, 1024:1152], id_f[0:8, 0:8]),
               waits=[(L, nld), ("pvsm", p.cnt("pvsm"))], inc=("ptsm", 1))
        p.step("t", lambda e: e.transpose(ps_t1[0:128, 24:32], pw0[0:8, 1280:1408], id_f[0:8, 0:8]),
               inc=("ptsm", 1))
        p.step("v", lambda e: e.tensor_copy(out=bias_f[:], in_=ps_t1[0:128, 16:24]),
               waits=[("ptsm", p.cnt("ptsm"))], inc=("pvsm", 1))
        p.step("a", lambda e: e.activation(out=ct_sb[:], in_=ps_t1[0:128, 24:32], func=AF.Silu),
               waits=[("ptsm", p.cnt("ptsm"))], inc=("pa", 1))
        PTSM_SMALL = p.cnt("ptsm")
        PA_CT = p.cnt("pa")

        p.step("a", lambda e: e.activation(out=gt_sb[0:1, 2:4], in_=gt_sb[0:1, 0:2], func=AF.Exp),
               waits=[(L, nld)], inc=("pa", 1))

        p.step("v", lambda e: e.tensor_tensor(out=gt_sb[0:1, 4:5], in0=gt_sb[0:1, 2:3],
                                              in1=gt_sb[0:1, 3:4], op=ALU.add),
               waits=[("pa", p.cnt("pa"))], inc=("gs", 1))
        p.step("v", lambda e: e.reciprocal(out=gt_sb[0:1, 5:6], in_=gt_sb[0:1, 4:5]),
               waits=[("gs", 1)], inc=("gs", 1))
        p.step("v", lambda e: e.tensor_tensor(out=gt_sb[0:1, 6:7], in0=gt_sb[0:1, 2:3],
                                              in1=gt_sb[0:1, 5:6], op=ALU.mult),
               waits=[("gs", 2)], inc=("gs", 1))
        p.step("v", lambda e: e.tensor_tensor(out=gt_sb[0:1, 7:8], in0=gt_sb[0:1, 3:4],
                                              in1=gt_sb[0:1, 5:6], op=ALU.mult),
               waits=[("gs", 3)], inc=("pv", 1))
        p.step("t", lambda e: e.matmul(ps_r[0:128, 0:1], ones_f[0:1, :], gt_sb[0:1, 6:7],
                                       start=True, stop=True),
               waits=[("pv", p.cnt("pv"))], inc=("pt", 1))
        p.step("t", lambda e: e.matmul(ps_r[0:128, 1:2], ones_f[0:1, :], gt_sb[0:1, 7:8],
                                       start=True, stop=True), inc=("pt", 1))
        p.step("t", lambda e: e.matmul(ps_r[0:128, 2:3], ones_f[0:1, :], pw0[0:1, 1536:1537],
                                       start=True, stop=True), waits=[(L, nld)], inc=("pt", 1))
        p.step("v", lambda e: e.tensor_copy(out=g0_bc[:], in_=ps_r[0:128, 0:1]),
               waits=[("pt", p.cnt("pt"))], inc=("pv", 1))
        p.step("v", lambda e: e.tensor_copy(out=g1_bc[:], in_=ps_r[0:128, 1:2]), inc=("pv", 1))
        p.step("v", lambda e: e.tensor_copy(out=ws_bc[:], in_=ps_r[0:128, 2:3]), inc=("pv", 1))
        PV_G = p.cnt("pv")

        for kt in range(8):
            buf = (pmod0, pmod1)[kt % 2]
            p.step("s", lambda e, b=buf, k=kt: e.dma_start(out=b[:], in_=wmod_d[k * 128:(k + 1) * 128, :]),
                   waits=[("ptm", max(0, p.cnt("ptm") - 48))], inc=("wm", 16))
            for jt in range(48):
                p.step("t", lambda e, b=buf, j=jt, k=kt: e.matmul(
                    ps_a[0:128, j:j + 1], b[:, j * 128:(j + 1) * 128], ct_sb[:, k:k + 1],
                    start=True, stop=True),
                    waits=([("wm", 16 * (kt + 1)), ("pa", p.cnt("pa")),
                            ("pmodf", max(0, kt))] if jt == 0 else []),
                    inc=("ptm", 1))
            if kt == 0:
                p.step("v", lambda e: e.tensor_copy(out=mod_sb[:], in_=ps_a[0:128, 0:48]),
                       waits=[("ptm", p.cnt("ptm"))], inc=("pmodf", 1))
            else:
                p.step("v", lambda e: e.tensor_tensor(out=mod_sb[:], in0=mod_sb[:],
                                                      in1=ps_a[0:128, 0:48], op=ALU.add),
                       waits=[("ptm", p.cnt("ptm"))], inc=("pmodf", 1))
        p.step("v", lambda e: e.engine_nop(), waits=[("pmodf", 8)], inc=("pv", 1))

        def em_modpost(e):
            ga_m, gb_m = mod_sb[:, 16:24], mod_sb[:, 40:48]
            e.tensor_scalar(out=ca1_sb[:], in0=mod_sb[:, 8:16], scalar1=1.0, scalar2=None, op0=ALU.add)
            e.tensor_copy(out=sa_sb[:], in_=mod_sb[:, 0:8])
            e.tensor_scalar(out=vec8[:, 0:8], in0=ga_m, scalar1=0.0, scalar2=None, op0=ALU.is_ge)
            e.tensor_scalar(out=vec8[:, 8:16], in0=gb_m, scalar1=0.0, scalar2=None, op0=ALU.is_ge)
            e.tensor_scalar(out=vec8[:, 16:24], in0=ga_m, scalar1=0.25, scalar2=None, op0=ALU.max)
            e.tensor_scalar(out=vec8[:, 24:32], in0=gb_m, scalar1=0.25, scalar2=None, op0=ALU.max)
            e.tensor_scalar(out=vec8[:, 32:40], in0=ga_m, scalar1=-0.25, scalar2=None, op0=ALU.min)
            e.tensor_scalar(out=vec8[:, 40:48], in0=gb_m, scalar1=-0.25, scalar2=None, op0=ALU.min)
            e.tensor_tensor(out=vec8[:, 16:24], in0=vec8[:, 16:24], in1=vec8[:, 0:8], op=ALU.mult)
            e.tensor_tensor(out=vec8[:, 24:32], in0=vec8[:, 24:32], in1=vec8[:, 8:16], op=ALU.mult)
            e.tensor_scalar(out=vec8[:, 0:8], in0=vec8[:, 0:8], scalar1=-1.0, scalar2=1.0,
                            op0=ALU.mult, op1=ALU.add)
            e.tensor_scalar(out=vec8[:, 8:16], in0=vec8[:, 8:16], scalar1=-1.0, scalar2=1.0,
                            op0=ALU.mult, op1=ALU.add)
            e.tensor_tensor(out=vec8[:, 32:40], in0=vec8[:, 32:40], in1=vec8[:, 0:8], op=ALU.mult)
            e.tensor_tensor(out=vec8[:, 40:48], in0=vec8[:, 40:48], in1=vec8[:, 8:16], op=ALU.mult)
            e.tensor_tensor(out=ga_sb[:], in0=vec8[:, 16:24], in1=vec8[:, 32:40], op=ALU.add)
            return e.tensor_tensor(out=vec8[:, 48:56], in0=vec8[:, 24:32], in1=vec8[:, 40:48],
                                   op=ALU.add)
        p.step("v", em_modpost, waits=[("pv", p.cnt("pv"))], inc=("mp", 1))
        p.step("v", lambda e: e.tensor_scalar(out=vec8[:, 56:64], in0=mod_sb[:, 32:40],
                                              scalar1=1.0, scalar2=None, op0=ALU.add),
               waits=[("mp", 1)], inc=("pv", 1))
        PV_MOD = p.cnt("pv")

        def make_rep(src_col_ap, dst):
            p.step("t", lambda e, s=src_col_ap: e.transpose(ps_t1[0:8, 0:128], s, id_f[:]),
                   waits=[("pv", PV_MOD), ("prv", p.cnt("prv")), ("pa", PA_CT)],
                   inc=("prt", 1))
            p.step("v", lambda e: e.tensor_copy(out=row8[:], in_=ps_t1[0:8, 0:128]),
                   waits=[("prt", p.cnt("prt")), ("prd", p.cnt("prd"))], inc=("prv", 1))
            def em_growdma(e, sems):
                for i in range(8):
                    e.dma_start(out=grow[0:1, i * 128:(i + 1) * 128],
                                in_=row8[i:i + 1, :]).then_inc(sems["prd"], 16)
            p.step("s", em_growdma,
                   waits=[("prv", p.cnt("prv")), ("prt2", p.cnt("prt2"))], inc=("prd", 128), raw=True)
            for half in range(2):
                p.step("t", lambda e, h=half: e.matmul(
                    ps_r[0:128, 0:512] if h == 0 else ps_r[0:128, 0:512],
                    ones_f[0:1, :], grow[0:1, h * 512:(h + 1) * 512],
                    start=True, stop=True),
                    waits=[("prd", p.cnt("prd")), ("prv2", p.cnt("prv2"))], inc=("prt2", 1))
                p.step("v", lambda e, d=dst, h=half: e.tensor_copy(
                    out=d[:, h * 512:(h + 1) * 512], in_=ps_r[0:128, 0:512]),
                    waits=[("prt2", p.cnt("prt2"))], inc=("prv2", 1))

        make_rep(vec8[:, 56:64], cb1r)
        make_rep(mod_sb[:, 24:32], sbr)
        make_rep(vec8[:, 48:56], gbr)
        make_rep(bias_f[:], gbiasr)
        p.step("v", lambda e: e.tensor_tensor(out=gbiasr[:], in0=gbiasr[:], in1=gbr[:], op=ALU.mult),
               waits=[("prv2", p.cnt("prv2"))], inc=("pv", 1))

        p.step("v", lambda e: e.tensor_copy(out=pwxb[:], in_=pwx[:]),
               waits=[(L, nld)], inc=("pwxc", 1))
        for dd in range(DT):
            p.step("t", lambda e, d=dd: e.transpose(
                ps_c1[0:128, 0:96], pwxb[0:96, d * 128:(d + 1) * 128], id_bf[0:96, 0:96]),
                waits=[("pwxc", 1), ("pwxv", p.cnt("pwxv"))], inc=("pwxt", 1))
            p.step("v", lambda e, d=dd: e.tensor_copy(
                out=wxT[:, d * 96:(d + 1) * 96], in_=ps_c1[0:128, 0:96]),
                waits=[("pwxt", p.cnt("pwxt"))], inc=("pwxv", 1))

        p.step("v", lambda e: e.tensor_copy(out=pwb[:, 0:DT * 64], in_=pw1[:, 0:DT * 64]),
               waits=[(L, nld)], inc=("pdtc", 1))
        for dd in range(DT):
            p.step("t", lambda e, d=dd: e.transpose(
                ps_c2[0:64, 0:128], pwb[:, d * 64:(d + 1) * 64], id_bf[:]),
                waits=[("pdtc", 1), ("pdtv", p.cnt("pdtv"))], inc=("pdtt", 1))
            p.step("v", lambda e, d=dd: e.tensor_copy(
                out=wdtT[0:64, d * 128:(d + 1) * 128], in_=ps_c2[0:64, 0:128]),
                waits=[("pdtt", p.cnt("pdtt"))], inc=("pdtv", 1))
        PDTV = p.cnt("pdtv")

        for j in range(8):
            src = ppk[:, (j % 2) * 1024:(j % 2) * 1024 + 1024]
            sh = 2 * (j // 2)

            def em_unpack(e, s=src, sh_=sh):
                e.tensor_scalar(out=pi1[:], in0=s, scalar1=sh_, scalar2=3,
                                op0=ALU.logical_shift_right, op1=ALU.bitwise_and)
                e.tensor_copy(out=pw0[:, 0:1024], in_=pi1[:])
                return e.tensor_scalar(out=pwb[:, 0:1024], in0=pw0[:, 0:1024], scalar1=1.0,
                                       scalar2=None, op0=ALU.subtract)
            p.step("v", em_unpack,
                   waits=[(L, nld), ("ptsm", PTSM_SMALL), ("pdtt", p.cnt("pdtt")),
                          ("pft", p.cnt("pft"))],
                   inc=("pfu", 1))
            for kt in range(8):
                p.step("t", lambda e, k=kt: e.transpose(
                    ps_c1[0:128, 0:128], pwb[:, k * 128:(k + 1) * 128], id_bf[:]),
                    waits=[("pfu", p.cnt("pfu")), ("pfv", p.cnt("pfv")), ("pwxv", DT)],
                    inc=("pft", 1))
                p.step("v", lambda e, k=kt: e.tensor_copy(
                    out=pwb[:, 1024 + k * 128:1024 + (k + 1) * 128], in_=ps_c1[0:128, 0:128]),
                    waits=[("pft", p.cnt("pft")), ("pfs", 16 * j)], inc=("pfv", 1))
            p.step("s", lambda e, j_=j: e.dma_start(out=wffnd[j_], in_=pwb[:, 1024:2048]),
                   waits=[("pfv", p.cnt("pfv"))], inc=("pfs", 16))
        PFT = p.cnt("pft")

        for ot in range(8):
            buf = (pw0, pw1)[ot % 2]
            p.step("s", lambda e, b=buf, o=ot: e.dma_start(
                out=b[:, 0:DI], in_=w_out_d[o * 128:(o + 1) * 128, :]),
                waits=[("pov", max(0, p.cnt("pov") - DT)), ("pfu", p.cnt("pfu"))],
                inc=("pod", 16))

            def em_wout_cast(e, b=buf, o=ot):
                e.tensor_scalar(out=b[:, 0:DI], in0=b[:, 0:DI], scalar1=ga_sb[:, o:o + 1],
                                scalar2=None, op0=ALU.mult)
                return e.tensor_copy(out=pwb[:, 0:DI], in_=b[:, 0:DI])
            p.step("v", em_wout_cast,
                   waits=[("pod", 16 * (ot + 1)), ("pv", PV_MOD), ("pft", PFT),
                          ("pot", p.cnt("pot")), ("pfs", 128)],
                   inc=("poc", 1))
            for dd in range(DT):
                p.step("t", lambda e, d=dd: e.transpose(
                    ps_c1[0:128, 0:128], pwb[:, d * 128:(d + 1) * 128], id_bf[:]),
                    waits=[("poc", p.cnt("poc")), ("pov", p.cnt("pov")), ("pfv", 64)],
                    inc=("pot", 1))
                p.step("v", lambda e, d=dd: e.tensor_copy(
                    out=pwb2[:, d * 128:(d + 1) * 128], in_=ps_c1[0:128, 0:128]),
                    waits=[("pot", p.cnt("pot")), ("pos", 16 * ot)], inc=("pov", 1))
            p.step("s", lambda e, o=ot: e.dma_start(
                out=woutd[o], in_=pwb2[:, 0:DT * 128]),
                waits=[("pov", p.cnt("pov"))], inc=("pos", 16))
        POT = p.cnt("pot")

        for et in range(NE):
            buf = (pw0, pw1)[et % 2]
            p.step("s", lambda e, b=buf, E=et: e.dma_start(
                out=b[:, 0:1024], in_=w_in_d[E * 128:(E + 1) * 128, :]),
                waits=[("pic", max(0, p.cnt("pic") - 1)), ("pov", p.cnt("pov"))],
                inc=("pid", 16))
            p.step("v", lambda e, b=buf: e.tensor_copy(out=pwb[:, 0:1024], in_=b[:, 0:1024]),
                   waits=[("pid", 16 * (et + 1)), ("pot", POT), ("pit", p.cnt("pit"))],
                   inc=("pic", 1))
            for kt in range(8):
                p.step("t", lambda e, k=kt: e.transpose(
                    ps_c1[0:128, 0:128], pwb[:, k * 128:(k + 1) * 128], id_bf[:]),
                    waits=[("pic", p.cnt("pic")), ("piv", p.cnt("piv")), ("pov", 8 * DT)],
                    inc=("pit", 1))
                p.step("v", lambda e, k=kt: e.tensor_copy(
                    out=pwb[:, 1024 + k * 128:1024 + (k + 1) * 128], in_=ps_c1[0:128, 0:128]),
                    waits=[("pit", p.cnt("pit")), ("pis", 16 * et)], inc=("piv", 1))
            p.step("s", lambda e, E=et: e.dma_start(out=wintd[E], in_=pwb[:, 1024:2048]),
                   waits=[("piv", p.cnt("piv"))], inc=("pis", 16))

        p.step("g", lambda e: e.memset(hstate[:], 0.0), inc=("pg", 1))
        p.step("g", lambda e: e.memset(xconv[:], 0.0),
               waits=[(L, nld), ("pis", p.cnt("pis")), ("piv", p.cnt("piv")),
                      ("pov", p.cnt("pov")), ("pfv", p.cnt("pfv")),
                      ("pwxv", p.cnt("pwxv")), ("pdtv", p.cnt("pdtv")),
                      ("pvsm", p.cnt("pvsm")), ("prv2", p.cnt("prv2")),
                      ("pv", p.cnt("pv")), ("pa", p.cnt("pa")),
                      ("ptm", p.cnt("ptm")), ("pit", p.cnt("pit")),
                      ("pot", p.cnt("pot")), ("pft", p.cnt("pft")),
                      ("prd", p.cnt("prd")), ("wm", p.cnt("wm")),
                      ("pod", p.cnt("pod")), ("pid", p.cnt("pid")),
                      ("pmodf", p.cnt("pmodf")), ("pfu", p.cnt("pfu")),
                      ("poc", p.cnt("poc")), ("pic", p.cnt("pic"))],
               inc=("pg", 1))
        p.step("g", lambda e: e.memset(cst[:, 0:1], LN_EPS), inc=("pg", 1))
        p.step("g", lambda e: e.memset(cst[:, 1:2], -0.5), inc=("pg", 1))
        p.step("g", lambda e: e.memset(cst[:, 2:3], 1.0), inc=("pg", 1))

        p.step("s", lambda e: e.dma_start(out=gt_sb[0:1, 0:1], in_=ones_d[0:1, 0:1]),
               waits=[("pis", p.cnt("pis")), (L, nld), ("wm", p.cnt("wm")),
                      ("pod", p.cnt("pod")), ("pid", p.cnt("pid")), ("prd", p.cnt("prd"))],
               inc=("pdone", 16))
        p.step("v", lambda e: e.engine_nop(),
               waits=[("pv", p.cnt("pv")), ("prv2", p.cnt("prv2")), ("pvsm", p.cnt("pvsm")),
                      ("pfv", p.cnt("pfv")), ("pov", p.cnt("pov")), ("piv", p.cnt("piv")),
                      ("pwxv", p.cnt("pwxv")), ("pdtv", PDTV)],
               inc=("pdone", 1))
        p.step("a", lambda e: e.copy(out=gt_sb[0:1, 1:2], in_=gt_sb[0:1, 1:2]),
               waits=[("pa", p.cnt("pa"))], inc=("pdone", 1))
        p.step("t", lambda e: e.matmul(ps_r[0:128, 3:4], ones_f[0:1, :], ones_f[0:1, 0:1],
                                       start=True, stop=True),
               waits=[("ptm", p.cnt("ptm")), ("prt2", p.cnt("prt2")), ("pit", p.cnt("pit")),
                      ("prv2", p.cnt("prv2")), ("pv", p.cnt("pv"))],
               inc=("pdone", 1))
        p.step("g", lambda e: e.engine_nop(), waits=[("pg", p.cnt("pg"))], inc=("pdone", 1))
        PD = p.cnt("pdone")

        ET = mybir.EngineType
        slreg = list(nc.alloc_registers("slreg", engines=[ET.Pool]))[0]
        pcnt = list(nc.alloc_registers("pollcnt", engines=[ET.Pool]))[0]
        fvreg = list(nc.alloc_registers("fvreg", engines=[ET.Pool]))[0]
        slreg_sp = list(nc.alloc_registers("slreg_sp", engines=[ET.SP]))[0]
        spk = {"ws": 0}

        def em_flagtab(g, sems):
            g.memset(flagtab[0:1, 0:1], 999)
            g.memset(flagtab[0:1, 1:2], 1000)
            g.memset(flagtab[0:1, 2:3], 999)
            for c_ in range(NCH):
                vals = [(1001 + 2 * c_, 1000 + 2 * c_), (1001 + 2 * c_, 1001 + 2 * c_),
                        (1002 + 2 * c_, 1001 + 2 * c_), (1002 + 2 * c_, 1002 + 2 * c_)]
                for e_, (wv, rv) in enumerate(vals):
                    off = 4 + 2 * (4 * c_ + e_)
                    g.memset(flagtab[0:1, off:off + 1], wv)
                    g.memset(flagtab[0:1, off + 1:off + 2], rv)
            g.engine_nop().then_inc(sems["pfl"], 1)
        p.step("g", em_flagtab, inc=("pfl", 1), raw=True)

        def sp_slotw(e, sems, tag, dst0, dst1, srcap):
            nc.br_cmp(slreg_sp, 0, on_true=f"sw0_{tag}", on_false=f"sw1_{tag}",
                      comp_op="IS_EQ", engines=[ET.SP])
            with nc.body(f"sw0_{tag}", valid_engines=[ET.SP]):
                e.dma_start(out=dst0, in_=srcap).then_inc(sems["sws"], 16)
                nc.br(f"swj_{tag}", engines=[ET.SP])
            with nc.body(f"sw1_{tag}", valid_engines=[ET.SP]):
                e.dma_start(out=dst1, in_=srcap).then_inc(sems["sws"], 16)
                nc.br(f"swj_{tag}", engines=[ET.SP])
            nc.switch_bb(f"swj_{tag}")
            spk["ws"] += 16
            e.wait_ge(sems["sws"], spk["ws"])

        def sp_flagw(e, sems, tag, entry):
            off = 4 + 2 * entry
            sp_slotw(e, sems, f"f{tag}", shf[0, 0:1, 0:2], shf[1, 0:1, 0:2],
                     flagtab[0:1, off:off + 2])

        def gp_poll(g, sems, tag, word, target, exact=False):
            nc.br(f"pp_{tag}", engines=[ET.Pool])
            with nc.body(f"pp_{tag}", valid_engines=[ET.Pool]):
                g.reg_alu(pcnt, pcnt, 16, op=ALU.add)
                nc.br_cmp(slreg, 0, on_true=f"pr1_{tag}", on_false=f"pr0_{tag}",
                          comp_op="IS_EQ", engines=[ET.Pool])
                with nc.body(f"pr0_{tag}", valid_engines=[ET.Pool]):
                    g.dma_start(out=fsb[0:1, 0:4], in_=shf[0, 0:1, 0:4]).then_inc(sems["xpl"], 16)
                    g.wait_ge(sems["xpl"], pcnt)
                    g.reg_load(fvreg, fsb[0:1, word:word + 1])
                    nc.br(f"pc_{tag}", engines=[ET.Pool])
                with nc.body(f"pr1_{tag}", valid_engines=[ET.Pool]):
                    g.dma_start(out=fsb[0:1, 4:8], in_=shf[1, 0:1, 0:4]).then_inc(sems["xpl"], 16)
                    g.wait_ge(sems["xpl"], pcnt)
                    g.reg_load(fvreg, fsb[0:1, 4 + word:5 + word])
                    nc.br(f"pc_{tag}", engines=[ET.Pool])
                nc.switch_bb(f"pc_{tag}")
                cop = "IS_EQ" if exact else "IS_GE"
                nc.br_cmp(fvreg, target, on_true=f"pdn_{tag}", on_false=f"pp_{tag}",
                          comp_op=cop, engines=[ET.Pool])
            nc.switch_bb(f"pdn_{tag}")

        p.step("s", lambda e: e.dma_start(out=slsb[:], in_=sl_d), inc=("sld", 16))

        def em_spinit(e, sems):
            e.reg_load(slreg_sp, slsb[0:1, 0:1])
            sp_slotw(e, sems, "init", shf[0, 0:1, 0:3], shf[1, 0:1, 0:3],
                     flagtab[0:1, 0:3])
            e.nop().then_inc(sems["sxi"], 1)
        p.step("s", em_spinit, waits=[("sld", 16), ("pfl", 1)], inc=("sxi", 1), raw=True)

        def em_xinit(g, sems):
            g.reg_load(slreg, slsb[0:1, 0:1])
            nc.reg_mov(pcnt, 0)
            gp_poll(g, sems, "init", 2, 999, exact=True)
            g.engine_nop().then_inc(sems["xin"], 1)
        p.step("g", em_xinit, waits=[("pdone", PD), ("sld", 16)], inc=("xin", 1), raw=True)

        x_r = x_d.rearrange("(c tt pp) f -> c tt pp f", pp=128, tt=TT)
        out_r = out_d.rearrange("(c tt pp) f -> c tt pp f", pp=128, tt=TT)

        for c in range(NCH):
            fw = [("pdone", PD)] if c == 0 else []

            def em_xload(e, sems, c_=c):
                for t in range(TT):
                    e.dma_start(out=x_tm[:, t * DM:(t + 1) * DM],
                                in_=x_r[c_, t]).then_inc(sems["x"], 16)
            p.step("s", em_xload, waits=fw + [("ffd", c * 8)], inc=("x", 16 * TT), raw=True)

            for tt in range(TT):
                p.step("a", lambda e, t=tt: e.activation(
                    out=out_tm[:, t * DM:(t + 1) * DM], in_=x_tm[:, t * DM:(t + 1) * DM],
                    func=AF.Square, accum_out=stat[:, 4 + t:5 + t]),
                    waits=fw + [("x", 16 * TT * (c + 1)), ("q", c), ("out", 16 * TT * c)], inc=("sq", 1))
                p.step("v", lambda e, t=tt: e.reduce_sum(
                    out=stat[:, t:t + 1], in_=x_tm[:, t * DM:(t + 1) * DM], axis=AX),
                    waits=fw + [("x", 16 * TT * (c + 1)), ("ff", c * 8)], inc=("vst", 1))

            def em_lnp_a(e):
                e.tensor_scalar(out=stat[:, 8:8 + TT], in0=stat[:, 0:TT], scalar1=1.0 / DM,
                                scalar2=None, op0=ALU.mult)
                e.tensor_scalar(out=stat[:, 12:12 + TT], in0=stat[:, 4:4 + TT], scalar1=1.0 / DM,
                                scalar2=None, op0=ALU.mult)
                return e.tensor_tensor(out=stat[:, 16:16 + TT], in0=stat[:, 8:8 + TT],
                                       in1=stat[:, 8:8 + TT], op=ALU.mult)

            def em_lnp_b(e):
                return e.tensor_tensor(out=stat[:, 12:12 + TT], in0=stat[:, 12:12 + TT],
                                       in1=stat[:, 16:16 + TT], op=ALU.subtract)

            def em_lnr(e):
                e.activation(out=stat[:, 20:20 + TT], in_=stat[:, 12:12 + TT], func=AF.Ln,
                             bias=cst[:, 0:1])
                return e.activation(out=stat[:, 24:24 + TT], in_=stat[:, 20:20 + TT], func=AF.Exp,
                                    scale=cst[:, 1:2])

            def em_nmr(e):
                return e.scalar_tensor_tensor(out=stat[:, 28:28 + TT], in0=stat[:, 8:8 + TT],
                                              scalar=-1.0, in1=stat[:, 24:24 + TT],
                                              op0=ALU.mult, op1=ALU.mult)

            p.step("v", em_lnp_a, waits=[("sq", (2 * c + 1) * TT), ("vst", p.cnt("vst"))],
                   inc=("vst", 1))
            p.step("v", em_lnp_b, waits=[("vst", p.cnt("vst"))], inc=("vst", 1))
            p.step("a", em_lnr, waits=[("vst", p.cnt("vst"))], inc=("ln1", 1))
            p.step("v", em_nmr, waits=[("ln1", c + 1)], inc=("vst", 1))
            VST_N1 = p.cnt("vst")

            for tt in range(TT):
                p.step("v", lambda e, t=tt: e.tensor_scalar(
                    out=out_tm[:, t * DM:(t + 1) * DM], in0=x_tm[:, t * DM:(t + 1) * DM],
                    scalar1=stat[:, 24 + t:25 + t], scalar2=stat[:, 28 + t:29 + t],
                    op0=ALU.mult, op1=ALU.add),
                    waits=[("vst", VST_N1), ("out", 16 * TT * c)], inc=("xlns", 1))
            for ft in range(FT):
                pst = (ps_t1, ps_t2)[ft % 2]
                for tt in range(TT):
                    p.step("t", lambda e, f=ft, t=tt, ps=pst: e.transpose(
                        ps[0:128, t * 128:(t + 1) * 128],
                        out_tm[:, t * DM + f * 128:t * DM + (f + 1) * 128], id_f[:]),
                        waits=([("xlns", c * TT + TT), ("tpf", max(0, c * FT + ft - 1)),
                                ("ffd", c * 8)] if tt == 0 else []),
                        inc=("tp", 1) if tt == TT - 1 else None)
                for tt in range(TT):
                    p.step("v", lambda e, f=ft, t=tt, ps=pst: e.tensor_scalar(
                        out=shbf[:, f * TC + t * 128:f * TC + (t + 1) * 128],
                        in0=ps[0:128, t * 128:(t + 1) * 128],
                        scalar1=ca1_sb[:, f:f + 1], scalar2=sa_sb[:, f:f + 1],
                        op0=ALU.mult, op1=ALU.add),
                        waits=([("tp", c * FT + ft + 1), ("ff", c * 8)] if tt == 0 else []),
                        inc=("tpf", 1) if tt == TT - 1 else None)

            for et in range(NE):
                ws = et % 2
                p.step("s", lambda e, E=et, w=ws: e.dma_start(
                    out=wbuf[:, w * DM:(w + 1) * DM], in_=wintd[E]),
                    waits=[("xz", max(0, c * NE + et - 1))], inc=("wd", 16))
                psm = (ps_a, ps_b)[et % 2]
                if et < NE // 2:
                    bw = ("mmfx", max(0, c * (NE // 2) + et - 1))
                else:
                    bw = ("mmfz", max(0, c * (NE // 2) + (et - NE // 2) - 1))
                for kt in range(8):
                    p.step("t", lambda e, k=kt, w=ws, ps=psm: e.matmul(
                        ps[:], wbuf[:, w * DM + k * 128:w * DM + (k + 1) * 128],
                        shbf[:, k * TC:(k + 1) * TC],
                        start=(k == 0), stop=(k == 7)),
                        waits=([("wd", 16 * (c * NE + et + 1)), ("tpf", c * FT + FT), bw,
                                ("ffe0", c * 8)] if kt == 0 else []),
                        inc=("xz", 1) if kt == 7 else None)
                if et < NE // 2:
                    p.step("v", lambda e, E=et, ps=psm: e.tensor_copy(
                        out=xconv[:, E * (TC + HALO) + HALO:(E + 1) * (TC + HALO)], in_=ps[:]),
                        waits=[("xz", c * NE + et + 1), ("wo", c * 8), ("halo", c)],
                        inc=("mmfx", 1))
                else:
                    p.step("a", lambda e, E=et, ps=psm: e.activation(
                        out=z_sb[:, (E - NE // 2) * TC:(E - NE // 2 + 1) * TC], in_=ps[:],
                        func=AF.Silu),
                        waits=[("xz", c * NE + et + 1), ("y2", c * DT)],
                        inc=("mmfz", 1))

            for dd in range(DT):
                base = dd * (TC + HALO)

                def em_c1(e, d=dd, b=base):
                    e.tensor_scalar(out=acc1[:], in0=xconv[:, b + 3:b + 3 + TC],
                                    scalar1=w1_sb[:, d * 4:d * 4 + 1], scalar2=None, op0=ALU.mult)
                    e.scalar_tensor_tensor(out=acc1[:], in0=xconv[:, b + 4:b + 4 + TC],
                                           scalar=w1_sb[:, d * 4 + 1:d * 4 + 2], in1=acc1[:],
                                           op0=ALU.mult, op1=ALU.add)
                    e.scalar_tensor_tensor(out=acc1[:], in0=xconv[:, b + 5:b + 5 + TC],
                                           scalar=w1_sb[:, d * 4 + 2:d * 4 + 3], in1=acc1[:],
                                           op0=ALU.mult, op1=ALU.add)
                    return e.scalar_tensor_tensor(out=acc1[:], in0=xconv[:, b + 6:b + 6 + TC],
                                                  scalar=w1_sb[:, d * 4 + 3:d * 4 + 4], in1=acc1[:],
                                                  op0=ALU.mult, op1=ALU.add)
                p.step("v", em_c1,
                       waits=[("mmfx", c * DT + dd + 1), ("cvs", max(0, c * DT + dd))],
                       inc=("cva", 1))

                def em_c2(e, d=dd, b=base):
                    e.tensor_scalar(out=acc2[:], in0=xconv[:, b + 0:b + 0 + TC],
                                    scalar1=w2_sb[:, d * 4:d * 4 + 1], scalar2=None, op0=ALU.mult)
                    e.scalar_tensor_tensor(out=acc2[:], in0=xconv[:, b + 2:b + 2 + TC],
                                           scalar=w2_sb[:, d * 4 + 1:d * 4 + 2], in1=acc2[:],
                                           op0=ALU.mult, op1=ALU.add)
                    e.scalar_tensor_tensor(out=acc2[:], in0=xconv[:, b + 4:b + 4 + TC],
                                           scalar=w2_sb[:, d * 4 + 2:d * 4 + 3], in1=acc2[:],
                                           op0=ALU.mult, op1=ALU.add)
                    return e.scalar_tensor_tensor(out=acc2[:], in0=xconv[:, b + 6:b + 6 + TC],
                                                  scalar=w2_sb[:, d * 4 + 3:d * 4 + 4], in1=acc2[:],
                                                  op0=ALU.mult, op1=ALU.add)
                p.step("v", em_c2, inc=("cva", 1))

                def em_si(e, d=dd):
                    e.activation(out=acc1[:], in_=acc1[:], func=AF.Silu, bias=b1_sb[:, d:d + 1])
                    return e.activation(out=acc2[:], in_=acc2[:], func=AF.Silu, bias=b2_sb[:, d:d + 1])
                p.step("a", em_si, waits=[("cva", (c * DT + dd) * 2 + 2)], inc=("cvs", 1))

                def em_gate(e, d=dd):
                    e.tensor_scalar(out=acc1[:], in0=acc1[:], scalar1=g0_bc[:], scalar2=None,
                                    op0=ALU.mult)
                    return e.scalar_tensor_tensor(out=xc_sb[:, d * TC:(d + 1) * TC], in0=acc2[:],
                                                  scalar=g1_bc[:], in1=acc1[:],
                                                  op0=ALU.mult, op1=ALU.add)
                p.step("v", em_gate, waits=[("cvs", c * DT + dd + 1)], inc=("cvg", 1))

            def em_halo(e):
                inst = None
                for d in range(DT):
                    bb = d * (TC + HALO)
                    inst = e.tensor_copy(out=xconv[:, bb:bb + HALO],
                                         in_=xconv[:, bb + TC:bb + TC + HALO])
                return inst
            p.step("g", em_halo, waits=[("cvg", c * DT + DT)], inc=("halo", 1))

            for dd in range(DT):
                p.step("t", lambda e, d=dd: e.matmul(
                    ps_a[0:96, :], wxT[:, d * 96:(d + 1) * 96], xc_sb[:, d * TC:(d + 1) * TC],
                    start=(d == 0), stop=(d == DT - 1)),
                    waits=([("cvg", c * DT + dd + 1)] +
                           ([("xde", c), ("mmfx", c * DT + DT), ("mmfz", c * DT + DT)]
                            if dd == 0 else [])),
                    inc=("xd", 1))
            p.step("v", lambda e: e.tensor_copy(out=xdblp[:], in_=ps_a[0:96, :]),
                   waits=[("xd", c * DT + DT), ("pdt", c * DT), ("sxw", c)], inc=("xde0", 1))

            def em_xclr(g, sems, c_=c):
                gp_poll(g, sems, f"xr{c_}", 1, 1000 + 2 * c_)
                g.engine_nop().then_inc(sems["xclr"], 1)
            p.step("g", em_xclr, waits=[("xin", 1)] if c == 0 else [],
                   inc=("xclr", 1), raw=True)

            def em_sxw(e, sems, c_=c):
                sp_slotw(e, sems, f"xd{c_}", shxd[0], shxd[1], xdblp[:])
                sp_flagw(e, sems, f"xw{c_}", 4 * c_ + 0)
                e.nop().then_inc(sems["sxw"], 1)
            p.step("s", em_sxw, waits=[("xde0", c + 1), ("xclr", c + 1), ("sxi", 1)],
                   inc=("sxw", 1), raw=True)

            def em_xrdy(g, sems, c_=c):
                gp_poll(g, sems, f"xw{c_}", 0, 1001 + 2 * c_)
                g.engine_nop().then_inc(sems["xrdy"], 1)
            p.step("g", em_xrdy, inc=("xrdy", 1), raw=True)

            def em_sxr(e, sems, c_=c):
                e.dma_start(out=xdb0[:], in_=shxd[0]).then_inc(sems["sws"], 16)
                e.dma_start(out=xdb1[:], in_=shxd[1]).then_inc(sems["sws"], 16)
                spk["ws"] += 32
                e.wait_ge(sems["sws"], spk["ws"])
                e.nop().then_inc(sems["sxr"], 1)
                sp_flagw(e, sems, f"xa{c_}", 4 * c_ + 1)
            p.step("s", em_sxr, waits=[("xrdy", c + 1), ("xde", c)],
                   inc=("sxr", 1), raw=True)

            def em_xde(e):
                e.tensor_tensor(out=xdblp[:], in0=xdb0[:], in1=xdb1[:], op=ALU.add)
                e.tensor_copy(out=dtlow[:], in_=xdblp[0:64, :])
                return e.tensor_copy(out=xdbc[:], in_=xdblp[64:96, :])
            p.step("v", em_xde, waits=[("sxr", c + 1)], inc=("xde", 1))

            def em_dtmm(dd):
                psd = (ps_a, ps_b)[dd % 2]
                p.step("t", lambda e, d=dd, ps=psd: e.matmul(
                    ps[:], wdtT[0:64, d * 128:(d + 1) * 128], dtlow[:], start=True, stop=True),
                    waits=([("xde", c + 1), ("dl", max(0, c * DT + dd - 1))] +
                           ([("mmfx", c * DT + DT), ("mmfz", c * DT + DT)] if dd < 2 else [])),
                    inc=("pdt", 1))
            em_dtmm(0)
            em_dtmm(1)

            border = [16 * k + n for n in range(16) for k in range(2)]
            bc_q = []
            for idx, j in enumerate(border):
                psc = (ps_t1, ps_t2)[idx % 2]
                p.step("t", lambda e, j_=j, ps=psc: e.matmul(
                    ps[:], sel_sb[:, j_ * 128:(j_ + 1) * 128], xdbc[:], start=True, stop=True),
                    waits=[("xde", c + 1), ("bcf", max(0, c * 32 + idx - 1)),
                           ("tpf", c * FT + FT)],
                    inc=("bc", 1))
                bc_q.append((idx, j))

            def pop_bc(k):
                for _ in range(k):
                    if not bc_q:
                        return
                    idx, j = bc_q.pop(0)
                    dst = brep if j < 16 else crep
                    jn = j % 16
                    p.step("a", lambda e, d=dst, n_=jn, ps=(ps_t1, ps_t2)[idx % 2]: e.copy(
                        out=d[:, n_ * TC:(n_ + 1) * TC], in_=ps[:]),
                        waits=[("bc", c * 32 + idx + 1), ("sl", c * SL)],
                        inc=("bcf", 1))

            for dd in range(DT):
                psd = (ps_a, ps_b)[dd % 2]
                if dd >= 2:
                    em_dtmm(dd)
                if dd == 0:
                    pop_bc(4)
                p.step("v", lambda e, d=dd, ps=psd: e.tensor_scalar(
                    out=sm_sl[:], in0=ps[:], scalar1=bdt_sb[:, d:d + 1], scalar2=30.0,
                    op0=ALU.add, op1=ALU.min),
                    waits=[("pdt", c * DT + dd + 1), ("sfl", max(0, c * DT + dd))],
                    inc=("sm", 1))

                def em_sfl(e):
                    e.activation(out=mixfm[:], in_=sm_sl[:], func=AF.Exp)
                    return e.activation(out=sm_sl[:], in_=mixfm[:], func=AF.Ln, bias=cst[:, 2:3])
                p.step("a", em_sfl,
                       waits=[("sm", c * DT + dd + 1), ("dl", max(0, c * DT + dd))],
                       inc=("sfl", 1))
                if dd == 0:
                    pop_bc(2)

                def em_dl(e, d=dd, ps=psd):
                    e.scalar_tensor_tensor(out=dl_sl[:], in0=ps[:], scalar=bdt_sb[:, d:d + 1],
                                           in1=sm_sl[:], op0=ALU.add, op1=ALU.max)
                    return e.tensor_tensor(out=g_sl[:], in0=dl_sl[:],
                                           in1=xc_sb[:, d * TC:(d + 1) * TC], op=ALU.mult)
                p.step("v", em_dl,
                       waits=[("sfl", c * DT + dd + 1), ("da", c * SL + dd * 16)],
                       inc=("dl", 1))

                for n in range(N):
                    si = dd * N + n
                    ds = si % 2
                    p.step("a", lambda e, d=dd, n_=n, s=ds: e.activation(
                        out=da_sl[:, s * TC:(s + 1) * TC], in_=dl_sl[:], func=AF.Exp,
                        scale=a_sb[:, d * N + n_:d * N + n_ + 1]),
                        waits=[("dl", c * DT + dd + 1), ("scn", max(0, c * SL + si - 1))],
                        inc=("da", 1))
                    if dd == 0:
                        pop_bc(2)

                    def em_slab(e, d=dd, n_=n, s=si, dsl=ds):
                        e.tensor_tensor(out=u_sl[:], in0=g_sl[:],
                                        in1=brep[:, n_ * TC:(n_ + 1) * TC], op=ALU.mult)
                        return e.tensor_tensor_scan(
                            out=h_sl[:, n_ * TC:(n_ + 1) * TC],
                            data0=da_sl[:, dsl * TC:(dsl + 1) * TC], data1=u_sl[:],
                            initial=hstate[:, d * N + n_:d * N + n_ + 1],
                            op0=ALU.mult, op1=ALU.add)
                    p.step("v", em_slab,
                           waits=[("da", c * SL + si + 1),
                                  ("sl", max(0, c * SL + (dd - 1) * N + n + 1)),
                                  ("st", max(0, c * DT + dd))] +
                                 ([("bcf", c * 32 + 2 * n + 1)] if dd == 0 else []),
                           inc=("scn", 1))

                    def em_y(g, d=dd, n_=n):
                        yv = xconv[:, d * (TC + HALO) + HALO:(d + 1) * (TC + HALO)]
                        if n_ == 0:
                            return g.tensor_tensor(out=yv, in0=h_sl[:, n_ * TC:(n_ + 1) * TC],
                                                   in1=crep[:, n_ * TC:(n_ + 1) * TC], op=ALU.mult)
                        g.tensor_tensor(out=t1b_sl[:], in0=h_sl[:, n_ * TC:(n_ + 1) * TC],
                                        in1=crep[:, n_ * TC:(n_ + 1) * TC], op=ALU.mult)
                        return g.tensor_tensor(out=yv, in0=yv, in1=t1b_sl[:], op=ALU.add)
                    p.step("g", em_y,
                           waits=[("scn", c * SL + si + 1)] +
                                 ([("halo", c + 1)] if n == 0 else []) +
                                 ([("bcf", c * 32 + 2 * n + 2)] if dd == 0 else []),
                           inc=("sl", 1))

                hv = h_sl[:].rearrange("p (n t) -> p t n", t=TC)[:, TC - 1]
                p.step("g", lambda e, d=dd, h=hv: e.tensor_copy(
                    out=hstate[:, d * N:(d + 1) * N], in_=h),
                    waits=[("scn", c * SL + dd * N + N)], inc=("st", 1))

                def em_y2(e, d=dd):
                    yv = xconv[:, d * (TC + HALO) + HALO:(d + 1) * (TC + HALO)]
                    e.scalar_tensor_tensor(out=yt_sl[:], in0=xc_sb[:, d * TC:(d + 1) * TC],
                                           scalar=d_sb[:, d:d + 1], in1=yv,
                                           op0=ALU.mult, op1=ALU.add)
                    return e.tensor_tensor(out=shbf[:, d * TC:(d + 1) * TC], in0=yt_sl[:],
                                           in1=z_sb[:, d * TC:(d + 1) * TC], op=ALU.mult)
                p.step("v", em_y2, waits=[("mmfz", c * (NE // 2) + NE // 2),
                                          ("sl", c * SL + dd * N + N)], inc=("y2", 1))

            def em_mclr(g, sems, c_=c):
                gp_poll(g, sems, f"mr{c_}", 1, 1001 + 2 * c_)
                g.engine_nop().then_inc(sems["mclr"], 1)
            p.step("g", em_mclr, inc=("mclr", 1), raw=True)

            for ot in range(8):
                wsl = (ot % 2) * (DT * 128)
                if ot == 0:
                    p.step("s", lambda e, o=ot, w=wsl: e.dma_start(
                        out=wobuf[:, w:w + DT * 128], in_=woutd[o]),
                        waits=[("wo", max(0, c * 8 - 1))], inc=("wod", 16))
                psm = (ps_a, ps_b)[ot % 2]
                for kt in range(DT):
                    p.step("t", lambda e, o=ot, k=kt, w=wsl, ps=psm: e.matmul(
                        ps[:], wobuf[:, w + k * 128:w + (k + 1) * 128],
                        shbf[:, k * TC:(k + 1) * TC],
                        start=(k == 0), stop=(k == DT - 1)),
                        waits=([("y2", c * DT + DT), ("mxe", max(0, c * 8 + ot - 1)),
                                ("wod", 16 * (c * 8 + ot + 1))]
                               if kt == 0 else []),
                        inc=("wo", 1) if kt == DT - 1 else None)
                if ot < 7:
                    w2 = ((ot + 1) % 2) * (DT * 128)
                    p.step("s", lambda e, o=ot + 1, w=w2: e.dma_start(
                        out=wobuf[:, w:w + DT * 128], in_=woutd[o]),
                        waits=[("wo", c * 8 + ot)], inc=("wod", 16))
                msl = (ot % 2) * TC
                p.step("v", lambda e, m=msl, ps=psm: e.tensor_copy(
                    out=mixst[:, m:m + TC], in_=ps[:]),
                       waits=[("wo", c * 8 + ot + 1), ("smw", max(0, c * 8 + ot - 1))],
                       inc=("mxe", 1))

                def em_smw(e, sems, c_=c, o=ot, m=msl):
                    sp_slotw(e, sems, f"mx{c_}_{o}", shmx[0, o], shmx[1, o],
                             mixst[:, m:m + TC])
                    if o == 7:
                        sp_flagw(e, sems, f"mw{c_}", 4 * c_ + 2)
                    e.nop().then_inc(sems["smw"], 1)
                p.step("s", em_smw,
                       waits=[("mxe", c * 8 + ot + 1)] + ([("mclr", c + 1)] if ot == 0 else []),
                       inc=("smw", 1), raw=True)

            def em_mrdy(g, sems, c_=c):
                gp_poll(g, sems, f"mw{c_}", 0, 1002 + 2 * c_)
                g.engine_nop().then_inc(sems["mrdy"], 1)
            p.step("g", em_mrdy, inc=("mrdy", 1), raw=True)

            for ot in range(8):
                def em_smr(e, sems, c_=c, o=ot):
                    pb = (o % 2) * TC
                    e.dma_start(out=mx0[:, pb:pb + TC], in_=shmx[0, o]).then_inc(sems["sws"], 16)
                    spk["ws"] += 16
                    e.wait_ge(sems["sws"], spk["ws"])
                    e.nop().then_inc(sems["smr"], 1)
                    if o == 7:
                        e.wait_ge(sems["aws"], 16 * 8 * (c_ + 1))
                        sp_flagw(e, sems, f"ma{c_}", 4 * c_ + 3)
                p.step("s", em_smr,
                       waits=([("mrdy", c + 1)] if ot == 0 else []) +
                             [("mxs", max(0, c * 8 + ot - 1))],
                       inc=("smr", 1), raw=True)

                def em_amr(a, sems, c_=c, o=ot):
                    pb = (o % 2) * TC
                    a.dma_start(out=mx1[:, pb:pb + TC], in_=shmx[1, o]).then_inc(sems["aws"], 16)
                p.step("a", em_amr,
                       waits=([("mrdy", c + 1)] if ot == 0 else []) +
                             [("mxs", max(0, c * 8 + ot - 1))],
                       inc=("aws", 16), raw=True)
                pb = (ot % 2) * TC
                p.step("v", lambda e, b=pb: e.tensor_tensor(
                    out=mixfm[:], in0=mx0[:, b:b + TC], in1=mx1[:, b:b + TC], op=ALU.add),
                       waits=[("smr", c * 8 + ot + 1), ("aws", 16 * (c * 8 + ot + 1)),
                              ("mxt", max(0, c * 8 + ot))],
                       inc=("mxs", 1))
                pst = (ps_t1, ps_t2)[ot % 2]
                for tt in range(TT):
                    p.step("t", lambda e, t=tt, ps=pst: e.transpose(
                        ps[0:128, t * 128:(t + 1) * 128],
                        mixfm[:, t * 128:(t + 1) * 128], id_f[:]),
                        waits=([("mxs", c * 8 + ot + 1), ("x2", max(0, c * 8 + ot - 1)),
                                ("bcf", c * 32 + 32)]
                               if tt == 0 else []),
                        inc=("mxt", 1) if tt == TT - 1 else None)
                for tt in range(TT):
                    p.step("v", lambda e, o=ot, t=tt, ps=pst: e.tensor_tensor(
                        out=x_tm[:, t * DM + o * 128:t * DM + (o + 1) * 128],
                        in0=x_tm[:, t * DM + o * 128:t * DM + (o + 1) * 128],
                        in1=ps[0:128, t * 128:(t + 1) * 128], op=ALU.add),
                        waits=[("mxt", c * 8 + ot + 1)] if tt == 0 else [],
                        inc=("x2", 1) if tt == TT - 1 else None)

            for tt in range(TT):
                p.step("a", lambda e, t=tt: e.activation(
                    out=out_tm[:, t * DM:(t + 1) * DM], in_=x_tm[:, t * DM:(t + 1) * DM],
                    func=AF.Square, accum_out=stat[:, 4 + t:5 + t]),
                    waits=[("x2", c * 8 + 8), ("tp", c * FT + FT), ("out", 16 * TT * c)],
                    inc=("sq", 1))
                p.step("v", lambda e, t=tt: e.reduce_sum(
                    out=stat[:, t:t + 1], in_=x_tm[:, t * DM:(t + 1) * DM], axis=AX),
                    waits=[("x2", c * 8 + 8)], inc=("vst", 1))
            p.step("v", em_lnp_a, waits=[("sq", (2 * c + 2) * TT), ("vst", p.cnt("vst"))],
                   inc=("vst", 1))
            p.step("v", em_lnp_b, waits=[("vst", p.cnt("vst"))], inc=("vst", 1))
            p.step("a", em_lnr, waits=[("vst", p.cnt("vst"))], inc=("ln2", 1))
            p.step("v", em_nmr, waits=[("ln2", c + 1)], inc=("vst", 1))
            VST_N2 = p.cnt("vst")

            def em_h2(e):
                inst = None
                for tt in range(TT):
                    e.tensor_scalar(out=out_tm[:, tt * DM:(tt + 1) * DM],
                                    in0=x_tm[:, tt * DM:(tt + 1) * DM],
                                    scalar1=stat[:, 24 + tt:25 + tt], scalar2=stat[:, 28 + tt:29 + tt],
                                    op0=ALU.mult, op1=ALU.add)
                    e.tensor_tensor(out=out_tm[:, tt * DM:(tt + 1) * DM],
                                    in0=out_tm[:, tt * DM:(tt + 1) * DM], in1=cb1r[:], op=ALU.mult)
                    inst = e.tensor_tensor(out=out_tm[:, tt * DM:(tt + 1) * DM],
                                           in0=out_tm[:, tt * DM:(tt + 1) * DM], in1=sbr[:],
                                           op=ALU.add)
                return inst
            p.step("v", em_h2, waits=[("vst", VST_N2)], inc=("vst", 1))

            def em_q1a(e):
                inst = None
                for tt in range(TT):
                    inst = e.tensor_reduce(out=amax[:, tt:tt + 1],
                                           in_=out_tm[:, tt * DM:(tt + 1) * DM],
                                           axis=AX, op=ALU.max, apply_absolute_value=True)
                return inst
            p.step("v", em_q1a, inc=("qc", 1))
            p.step("v", lambda e: e.tensor_scalar(out=amax[:], in0=amax[:], scalar1=1e-5,
                                                  scalar2=None, op0=ALU.max),
                   waits=[("qc", p.cnt("qc"))], inc=("qc", 1))
            p.step("v", lambda e: e.reciprocal(out=as_s[:], in_=amax[:]),
                   waits=[("qc", p.cnt("qc"))], inc=("qc", 1))

            def em_q1b(e):
                e.tensor_scalar(out=was_s[:], in0=amax[:], scalar1=1.0 / 127.0, scalar2=None,
                                op0=ALU.mult)
                return e.tensor_scalar(out=as_s[:], in0=as_s[:], scalar1=127.0, scalar2=None,
                                       op0=ALU.mult)
            p.step("v", em_q1b, waits=[("qc", p.cnt("qc"))], inc=("qc", 1))
            p.step("v", lambda e: e.tensor_scalar(out=was_s[:], in0=was_s[:], scalar1=ws_bc[:],
                                                  scalar2=None, op0=ALU.mult),
                   waits=[("qc", p.cnt("qc"))], inc=("q1", 1))

            def em_q2(e):
                inst = None
                for tt in range(TT):
                    e.tensor_scalar(out=out_tm[:, tt * DM:(tt + 1) * DM],
                                    in0=out_tm[:, tt * DM:(tt + 1) * DM],
                                    scalar1=as_s[:, tt:tt + 1], scalar2=M_RND,
                                    op0=ALU.mult, op1=ALU.add)
                    e.tensor_scalar(out=out_tm[:, tt * DM:(tt + 1) * DM],
                                    in0=out_tm[:, tt * DM:(tt + 1) * DM],
                                    scalar1=M_RND, scalar2=127.0, op0=ALU.subtract, op1=ALU.min)
                    inst = e.tensor_scalar(out=shbf[:, 4096 + tt * DM:4096 + (tt + 1) * DM],
                                           in0=out_tm[:, tt * DM:(tt + 1) * DM],
                                           scalar1=-128.0, scalar2=None, op0=ALU.max)
                return inst
            p.step("v", em_q2, waits=[("q1", c + 1), ("qt", c * FT), ("wo", c * 8 + 8)],
                   inc=("q", 1))

            for ft in range(FT):
                psc = (ps_c1, ps_c2)[ft % 2]
                for tt in range(TT):
                    p.step("t", lambda e, f=ft, t=tt, ps=psc: e.transpose(
                        ps[0:128, t * 128:(t + 1) * 128],
                        shbf[:, 4096 + t * DM + f * 128:4096 + t * DM + (f + 1) * 128], id_bf[:]),
                        waits=([("q", c + 1), ("qf", max(0, c * FT + ft - 1))] if tt == 0 else []),
                        inc=("qt", 1) if tt == TT - 1 else None)
                p.step("a", lambda e, f=ft, ps=psc: e.copy(
                    out=shbf[:, f * TC:(f + 1) * TC], in_=ps[0:128, 0:TC]),
                    waits=[("qt", c * FT + ft + 1)], inc=("qf", 1))

            for ot in range(8):
                p.step("s", lambda e, o=ot: e.dma_start(out=wfbuf[:], in_=wffnd[o]),
                       waits=[("ff", c * 8 + ot)], inc=("wfd", 16))
                psm = (ps_a, ps_b)[ot % 2]
                for kt in range(8):
                    p.step("t", lambda e, o=ot, k=kt, ps=psm: e.matmul(
                        ps[:], wfbuf[:, k * 128:(k + 1) * 128],
                        shbf[:, k * TC:(k + 1) * TC],
                        start=(k == 0), stop=(k == 7)),
                        waits=([("qf", c * FT + FT), ("ffe0", max(0, c * 8 + ot - 1)),
                                ("wfd", 16 * (c * 8 + ot + 1))]
                               if kt == 0 else []),
                        inc=("ff", 1) if kt == 7 else None)
                p.step("v", lambda e, ps=psm: e.tensor_copy(out=mixfm[:], in_=ps[:]),
                       waits=[("ff", c * 8 + ot + 1), ("fft", max(0, c * 8 + ot))],
                       inc=("ffe0", 1))
                pst = (ps_t1, ps_t2)[ot % 2]
                for tt in range(TT):
                    p.step("t", lambda e, t=tt, ps=pst: e.transpose(
                        ps[0:128, t * 128:(t + 1) * 128],
                        mixfm[:, t * 128:(t + 1) * 128], id_f[:]),
                        waits=([("ffe0", c * 8 + ot + 1), ("ffd", max(0, c * 8 + ot - 1))]
                               if tt == 0 else []),
                        inc=("fft", 1) if tt == TT - 1 else None)

                def em_ffd(e, o=ot, ps=pst):
                    sl = lambda t: yt_sl[:, t * 128:(t + 1) * 128]
                    for tt in range(TT):
                        e.tensor_scalar(out=sl(tt), in0=ps[0:128, tt * 128:(tt + 1) * 128],
                                        scalar1=was_s[:, tt:tt + 1], scalar2=None, op0=ALU.mult)
                    for tt in range(TT):
                        e.tensor_tensor(out=sl(tt), in0=sl(tt),
                                        in1=gbr[:, o * 128:(o + 1) * 128], op=ALU.mult)
                    for tt in range(TT):
                        e.tensor_tensor(out=sl(tt), in0=sl(tt),
                                        in1=gbiasr[:, o * 128:(o + 1) * 128], op=ALU.add)
                    inst = None
                    for tt in range(TT):
                        col = tt * DM + o * 128
                        inst = e.tensor_tensor(out=out_tm[:, col:col + 128], in0=sl(tt),
                                               in1=x_tm[:, col:col + 128], op=ALU.add)
                    return inst
                p.step("v", em_ffd, waits=[("fft", c * 8 + ot + 1), ("out", 16 * TT * c)],
                       inc=("ffd", 1))

            def em_store(e, sems, c_=c):
                for t in range(TT):
                    e.dma_start(out=out_r[c_, t],
                                in_=out_tm[:, t * DM:(t + 1) * DM]).then_inc(sems["out"], 16)
            p.step("s", em_store, waits=[("ffd", c * 8 + 8)], inc=("out", 16 * TT), raw=True)

        p.step("s", lambda e: None, waits=[("out", 16 * TT * NCH)])

        if debug:
            dbg = {"xc_sb": xc_sb, "z_sb": z_sb, "brep": brep, "crep": crep,
                   "hstate": hstate, "x_tm": x_tm, "out_tm": out_tm,
                   "shbf": shbf, "xconv": xconv, "stat": stat, "amax": amax,
                   "as_s": as_s, "was_s": was_s, "mod_sb": mod_sb, "a_sb": a_sb,
                   "bdt_sb": bdt_sb, "w1_sb": w1_sb, "w2_sb": w2_sb,
                   "dl_sl": dl_sl, "g_sl": g_sl, "da_sl": da_sl, "u_sl": u_sl,
                   "h_sl": h_sl, "sm_sl": sm_sl, "ct_sb": ct_sb,
                   "ca1_sb": ca1_sb, "sa_sb": sa_sb, "ga_sb": ga_sb, "cb1r": cb1r,
                   "sbr": sbr, "gbr": gbr, "gbiasr": gbiasr, "mixfm": mixfm,
                   "dtlow": dtlow, "xdbc": xdbc, "wbuf": wbuf, "vec8": vec8}
            for nm, tsr in dbg.items():
                shp = list(tsr.shape)
                dd_ = nc.dram_tensor("dbg_" + nm, shp, tsr.dtype, kind="ExternalOutput").ap()
                p.step("s", lambda e, d=dd_, t=tsr: e.dma_start(out=d, in_=t[:]),
                       waits=[("out", 16 * TT * NCH)], inc=("dbg", 16))
            p.step("s", lambda e: None, waits=[("dbg", p.cnt("dbg"))])

        if trunc is not None:
            p.steps = p.steps[:trunc]
            import sys as _sys
            print(f"TRUNCATED to {trunc} of {len(p.steps)} steps", file=_sys.stderr)

        for extra in ("sws", "xpl", "aws"):
            p.sems.setdefault(extra, 0)
        sem_handles = {}
        for s in sorted(p.sems.keys()):
            sem_handles[s] = ctx.enter_context(nc.semaphore(s))

        with nc.Block() as block:
            def replay(tag):
                def body(eng):
                    for se, emit, waits, inc, raw in p.steps:
                        if se != tag:
                            continue
                        for nm, val in waits:
                            eng.wait_ge(sem_handles[nm], val)
                        if raw:
                            emit(eng, sem_handles)
                            continue
                        inst = emit(eng)
                        if inc is not None and inst is not None:
                            inst.then_inc(sem_handles[inc[0]], inc[1])
                return body

            block.sync(replay("s"))
            block.vector(replay("v"))
            block.scalar(replay("a"))
            block.tensor(replay("t"))
            block.gpsimd(replay("g"))

    return nc


_NC_CACHE = None


def kernel(**inputs):
    global _NC_CACHE
    x = inputs["L0_attn_hidden_states"]
    cc = inputs["L0_c"]
    ident = np.eye(128)
    sel = np.kron(np.eye(32), np.ones((1, 128))).astype(np.float32)

    win = np.ascontiguousarray(inputs["L0_attn_W_in"], np.float32)
    wout = np.ascontiguousarray(inputs["L0_attn_W_out"], np.float32)
    wx = np.ascontiguousarray(inputs["L0_attn_W_x"], np.float32)
    wdt = np.ascontiguousarray(inputs["L0_attn_W_dt"], np.float32)
    alog = np.ascontiguousarray(inputs["L0_attn_A_log"], np.float32)
    bdt = np.ascontiguousarray(inputs["L0_attn_b_dt"], np.float32).reshape(16, 128)
    dpar = np.ascontiguousarray(inputs["L0_attn_D_param"], np.float32).reshape(16, 128)
    cb1 = np.ascontiguousarray(inputs["L0_attn_conv_b1"], np.float32).reshape(16, 128)
    cb2 = np.ascontiguousarray(inputs["L0_attn_conv_b2"], np.float32).reshape(16, 128)
    cw1 = np.ascontiguousarray(inputs["L0_attn_conv_w1"], np.float32).reshape(16, 512)
    cw2 = np.ascontiguousarray(inputs["L0_attn_conv_w2"], np.float32).reshape(16, 512)

    common = {
        "gates": np.ascontiguousarray(inputs["L0_attn_conv_gates"], np.float32).reshape(1, 2),
        "packed": np.ascontiguousarray(inputs["L0_ffn_packed_weight"], np.int32).reshape(256, 1024),
        "wscale": np.ascontiguousarray(inputs["L0_ffn_weight_scale"], np.float32).reshape(1, 1),
        "fbias": np.ascontiguousarray(inputs["L0_ffn_bias"], np.float32).reshape(8, 128),
        "wmod": np.ascontiguousarray(inputs["L0_wmod"], np.float32),
        "idbf": ident.astype(BF),
        "idf": ident.astype(np.float32),
        "sel": sel.astype(BF),
        "onesr": np.ones((1, 128), np.float32),
    }

    if _NC_CACHE is None:
        _NC_CACHE = build_nc()
    nc = _NC_CACHE
    in_maps = []
    for cid in range(NCORES):
        b, h = cid // 2, cid % 2
        lo, hi = h * DI, (h + 1) * DI
        m = dict(common)
        m["x"] = np.ascontiguousarray(x[b], np.float32)
        m["cvec"] = np.ascontiguousarray(cc[b], np.float32).reshape(8, 128)
        m["w_in"] = np.ascontiguousarray(
            np.concatenate([win[lo:hi], win[2048 + lo:2048 + hi]], axis=0))
        m["w_out"] = np.ascontiguousarray(wout[:, lo:hi])
        m["w_x"] = np.ascontiguousarray(wx[:, lo:hi])
        m["w_dt"] = np.ascontiguousarray(wdt[lo:hi])
        m["a_log"] = np.ascontiguousarray(alog[lo:hi])
        m["b_dt"] = np.ascontiguousarray(bdt[h * DT:(h + 1) * DT])
        m["dpar"] = np.ascontiguousarray(dpar[h * DT:(h + 1) * DT])
        m["cb1"] = np.ascontiguousarray(cb1[h * DT:(h + 1) * DT])
        m["cb2"] = np.ascontiguousarray(cb2[h * DT:(h + 1) * DT])
        m["cw1"] = np.ascontiguousarray(cw1[h * DT:(h + 1) * DT])
        m["cw2"] = np.ascontiguousarray(cw2[h * DT:(h + 1) * DT])
        m["slid"] = np.array([[h]], np.int32)
        in_maps.append(m)

    res = run_bass_kernel_spmd(nc, in_maps, core_ids=list(range(NCORES)))
    out = np.stack([np.asarray(res.results[2 * b]["out"], np.float32) for b in range(B)])
    return out


# version 11
# speedup vs baseline: 1.3112x; 1.1804x over previous
import numpy as np
import ml_dtypes
from contextlib import ExitStack

import concourse.bass as bass
import concourse.mybir as mybir
from concourse.bass_utils import run_bass_kernel_spmd

F32 = mybir.dt.float32
BF16 = mybir.dt.bfloat16
I32 = mybir.dt.int32
AF = mybir.ActivationFunctionType
ALU = mybir.AluOpType
AX = mybir.AxisListType.X

B = 4
NCORES = 8
T = 4096
TC = 512
NCH = T // TC
TT = TC // 128
DM = 1024
FT = DM // 128
DI = 1024
DT = DI // 128
N = 16
SL = DT * N
RK = 64
NE = 2 * DI // 128
M_RND = 12582912.0
LN_EPS = 1e-5
HALO = 6

BF = ml_dtypes.bfloat16


class Plan:
    def __init__(self):
        self.sems = {}
        self.steps = []

    def cnt(self, name):
        return self.sems.get(name, 0)

    def step(self, eng, emit, waits=(), inc=None, raw=False):
        if inc is not None:
            name, amt = inc
            self.sems[name] = self.sems.get(name, 0) + amt
        w = [(n, v) for (n, v) in waits if v > 0]
        self.steps.append((eng, emit, w, None if raw else inc, raw))


def build_nc(trunc=None, debug=False):
    nc = bass.Bass("TRN2", target_bir_lowering=False, detect_race_conditions=False)

    x_d = nc.dram_tensor("x", [T, DM], F32, kind="ExternalInput").ap()
    w_in_d = nc.dram_tensor("w_in", [2 * DI, DM], F32, kind="ExternalInput").ap()
    w_out_d = nc.dram_tensor("w_out", [DM, DI], F32, kind="ExternalInput").ap()
    w_x_d = nc.dram_tensor("w_x", [96, DI], F32, kind="ExternalInput").ap()
    w_dt_d = nc.dram_tensor("w_dt", [DI, RK], F32, kind="ExternalInput").ap()
    a_log_d = nc.dram_tensor("a_log", [DI, N], F32, kind="ExternalInput").ap()
    b_dt_d = nc.dram_tensor("b_dt", [DT, 128], F32, kind="ExternalInput").ap()
    dpar_d = nc.dram_tensor("dpar", [DT, 128], F32, kind="ExternalInput").ap()
    cb1_d = nc.dram_tensor("cb1", [DT, 128], F32, kind="ExternalInput").ap()
    cb2_d = nc.dram_tensor("cb2", [DT, 128], F32, kind="ExternalInput").ap()
    cw1_d = nc.dram_tensor("cw1", [DT, 512], F32, kind="ExternalInput").ap()
    cw2_d = nc.dram_tensor("cw2", [DT, 512], F32, kind="ExternalInput").ap()
    gates_d = nc.dram_tensor("gates", [1, 2], F32, kind="ExternalInput").ap()
    packed_d = nc.dram_tensor("packed", [256, 1024], I32, kind="ExternalInput").ap()
    wscale_d = nc.dram_tensor("wscale", [1, 1], F32, kind="ExternalInput").ap()
    fbias_d = nc.dram_tensor("fbias", [8, 128], F32, kind="ExternalInput").ap()
    cvec_d = nc.dram_tensor("cvec", [8, 128], F32, kind="ExternalInput").ap()
    wmod_d = nc.dram_tensor("wmod", [DM, 6 * DM], F32, kind="ExternalInput").ap()
    idbf_d = nc.dram_tensor("idbf", [128, 128], BF16, kind="ExternalInput").ap()
    idf_d = nc.dram_tensor("idf", [128, 128], F32, kind="ExternalInput").ap()
    sel_d = nc.dram_tensor("sel", [32, 32 * 128], BF16, kind="ExternalInput").ap()
    ones_d = nc.dram_tensor("onesr", [1, 128], F32, kind="ExternalInput").ap()

    sl_d = nc.dram_tensor("slid", [1, 1], I32, kind="ExternalInput").ap()
    out_d = nc.dram_tensor("out", [T, DM], F32, kind="ExternalOutput").ap()
    shxd = nc.dram_tensor("shxd", [2, 96, TC], F32, kind="Internal",
                          addr_space="Shared").ap()
    shmx = nc.dram_tensor("shmx", [2, 8, 128, TC], F32, kind="Internal",
                          addr_space="Shared").ap()
    shf = nc.dram_tensor("shf", [2, 1, 4], I32, kind="Internal",
                         addr_space="Shared").ap()
    wintd = nc.dram_tensor("w_inT_d", [NE, 128, DM], BF16, kind="Internal").ap()
    woutd = nc.dram_tensor("w_outT_d", [8, 128, DT * 128], BF16, kind="Internal").ap()
    wffnd = nc.dram_tensor("w_ffnT_d", [8, 128, 8 * 128], BF16, kind="Internal").ap()

    p = Plan()

    with ExitStack() as ctx:
        sb = lambda name, shape, dt: ctx.enter_context(nc.sbuf_tensor(name, shape, dt))
        psb = lambda name, shape, dt: ctx.enter_context(nc.psum_tensor(name, shape, dt))

        wxT = sb("wxT", [128, DT * 96], BF16)
        wdtT = sb("wdtT", [64, DT * 128], BF16)
        a_sb = sb("a_sb", [128, DT * N], F32)
        bdt_sb = sb("bdt_sb", [128, DT], F32)
        d_sb = sb("d_sb", [128, DT], F32)
        b1_sb = sb("b1_sb", [128, DT], F32)
        b2_sb = sb("b2_sb", [128, DT], F32)
        w1_sb = sb("w1_sb", [128, DT * 4], F32)
        w2_sb = sb("w2_sb", [128, DT * 4], F32)
        ca1_sb = sb("ca1_sb", [128, FT], F32)
        sa_sb = sb("sa_sb", [128, FT], F32)
        ga_sb = sb("ga_sb", [128, FT], F32)
        cb1r = sb("cb1r", [128, DM], F32)
        sbr = sb("sbr", [128, DM], F32)
        gbr = sb("gbr", [128, DM], F32)
        gbiasr = sb("gbiasr", [128, DM], F32)
        ws_bc = sb("ws_bc", [128, 1], F32)
        g0_bc = sb("g0_bc", [128, 1], F32)
        g1_bc = sb("g1_bc", [128, 1], F32)
        id_bf = sb("id_bf", [128, 128], BF16)
        id_f = sb("id_f", [128, 128], F32)
        sel_sb = sb("sel_sb", [32, 32 * 128], BF16)
        ones_f = sb("ones_f", [1, 128], F32)
        hstate = sb("hstate", [128, SL], F32)
        mod_sb = sb("mod_sb", [128, 48], F32)
        ct_sb = sb("ct_sb", [128, 8], F32)
        gt_sb = sb("gt_sb", [1, 8], F32)
        vec8 = sb("vec8", [128, 64], F32)
        cst = sb("cst", [128, 4], F32)

        with ExitStack() as pctx:
            psbuf = lambda name, shape, dt: pctx.enter_context(nc.sbuf_tensor(name, shape, dt))
            pw0 = psbuf("pw0", [128, 2048], F32)
            pw1 = psbuf("pw1", [128, 2048], F32)
            pwb = psbuf("pwb", [128, 2048], BF16)
            pwb2 = psbuf("pwb2", [128, 2048], BF16)
            pmod0 = psbuf("pmod0", [128, 6144], F32)
            pmod1 = psbuf("pmod1", [128, 6144], F32)
            pwx = psbuf("pwx", [96, DI], F32)
            pwxb = psbuf("pwxb", [96, DI], BF16)
            ppk = psbuf("ppk", [128, 2048], I32)
            pi1 = psbuf("pi1", [128, 1024], I32)
            praw = psbuf("praw", [16, 512], F32)
            praw2 = psbuf("praw2", [16, 512], F32)
            pa_raw = psbuf("pa_raw", [128, DT * N], F32)
            grow = psbuf("grow", [1, 1024], F32)
            bias_f = psbuf("bias_f", [128, 8], F32)
            row8 = psbuf("row8", [8, 128], F32)

        x_tm = sb("x_tm", [128, TT * DM], F32)
        out_tm = sb("out_tm", [128, TT * DM], F32)
        xconv = sb("xconv", [128, DT * (TC + HALO)], BF16)
        z_sb = sb("z_sb", [128, DT * TC], BF16)
        xc_sb = sb("xc_sb", [128, DT * TC], BF16)
        shbf = sb("shbf", [128, 2 * FT * TC], BF16)
        brep = sb("brep", [128, N * TC], BF16)
        crep = sb("crep", [128, N * TC], BF16)
        wbuf = sb("wbuf", [128, 2 * DM], BF16)
        wobuf = sb("wobuf", [128, 2 * DT * 128], BF16)
        wfbuf = sb("wfbuf", [128, 8 * 128], BF16)
        dtlow = sb("dtlow", [64, TC], BF16)
        xdbc = sb("xdbc", [32, TC], BF16)
        xdblp = sb("xdblp", [96, TC], F32)
        xdb0 = sb("xdb0", [96, TC], F32)
        xdb1 = sb("xdb1", [96, TC], F32)
        mx0 = sb("mx0", [128, 2 * TC], F32)
        mx1 = sb("mx1", [128, 2 * TC], F32)
        slsb = sb("slsb", [1, 1], I32)
        fsb = sb("fsb", [1, 8], I32)
        flagtab = sb("flagtab", [1, 72], I32)
        mixst = sb("mixst", [128, 2 * TC], F32)
        sm_sl = sb("sm_sl", [128, TC], F32)
        dl_sl = sb("dl_sl", [128, TC], F32)
        g_sl = sb("g_sl", [128, TC], BF16)
        da_sl = sb("da_sl", [128, 2 * TC], F32)
        u_sl = sb("u_sl", [128, TC], BF16)
        h_sl = sb("h_sl", [128, N * TC], BF16)
        t1b_sl = sb("t1b_sl", [128, TC], BF16)
        yt_sl = sb("yt_sl", [128, TC], F32)
        acc1 = sb("acc1", [128, TC], F32)
        acc2 = sb("acc2", [128, TC], F32)
        mixfm = sb("mixfm", [128, TC], F32)
        stat = sb("stat", [128, 32], F32)
        amax = sb("amax", [128, TT], F32)
        as_s = sb("as_s", [128, TT], F32)
        was_s = sb("was_s", [128, TT], F32)

        ps_a = psb("ps_a", [128, TC], F32)
        ps_b = psb("ps_b", [128, TC], F32)
        ps_t1 = psb("ps_t1", [128, TC], F32)
        ps_t2 = psb("ps_t2", [128, TC], F32)
        ps_c1 = psb("ps_c1", [128, TC], BF16)
        ps_c2 = psb("ps_c2", [128, TC], BF16)
        ps_r = psb("ps_r", [128, 512], F32)

        L = "ld"

        def ld(dst, src):
            p.step("s", lambda e, d=dst, s=src: e.dma_start(out=d, in_=s), inc=(L, 16))

        ld(id_bf[:], idbf_d)
        ld(id_f[:], idf_d)
        ld(sel_sb[:], sel_d)
        ld(ones_f[:], ones_d)
        a_r = a_log_d.rearrange("(dt p) n -> dt p n", p=128)
        for dd in range(DT):
            ld(pa_raw[:, dd * N:(dd + 1) * N], a_r[dd])
        ld(praw[0:DT, 0:128], b_dt_d)
        ld(praw[0:DT, 128:256], dpar_d)
        ld(praw[0:DT, 256:384], cb1_d)
        ld(praw[0:DT, 384:512], cb2_d)
        ld(praw2[0:DT, 0:512], cw1_d)
        ld(pw0[0:DT, 0:512], cw2_d)
        ld(pw0[0:8, 1024:1152], fbias_d)
        ld(pw0[0:8, 1280:1408], cvec_d)
        ld(gt_sb[0:1, 0:2], gates_d)
        ld(pw0[0:1, 1536:1537], wscale_d)
        ld(pwx[:], w_x_d)
        wdt_r = w_dt_d.rearrange("(dt p) r -> dt p r", p=128)
        for dd in range(DT):
            ld(pw1[:, dd * 64:(dd + 1) * 64], wdt_r[dd])
        ld(ppk[:, 0:1024], packed_d[0:128, :])
        ld(ppk[:, 1024:2048], packed_d[128:256, :])
        nld = p.cnt(L)

        p.step("a", lambda e: e.activation(out=pa_raw[:], in_=pa_raw[:], func=AF.Exp),
               waits=[(L, nld)], inc=("pa", 1))
        p.step("v", lambda e: e.tensor_scalar(out=a_sb[:], in0=pa_raw[:], scalar1=-1.0,
                                              scalar2=None, op0=ALU.mult),
               waits=[("pa", p.cnt("pa"))], inc=("pv", 1))

        def tr_small(dst, src_ap):
            p.step("t", lambda e, s=src_ap: e.transpose(ps_t1[0:128, 0:DT], s, id_f[0:DT, 0:DT]),
                   waits=[(L, nld), ("pvsm", p.cnt("pvsm"))], inc=("ptsm", 1))
            p.step("v", lambda e, d=dst: e.tensor_copy(out=d, in_=ps_t1[0:128, 0:DT]),
                   waits=[("ptsm", p.cnt("ptsm"))], inc=("pvsm", 1))

        tr_small(bdt_sb[:], praw[0:DT, 0:128])
        tr_small(d_sb[:], praw[0:DT, 128:256])
        tr_small(b1_sb[:], praw[0:DT, 256:384])
        tr_small(b2_sb[:], praw[0:DT, 384:512])

        for k in range(4):
            src = praw2[0:DT, :].rearrange("a (pp k) -> a k pp", k=4)[:, k]
            p.step("t", lambda e, s=src: e.transpose(ps_t1[0:128, 0:DT], s, id_f[0:DT, 0:DT]),
                   waits=[(L, nld), ("pvsm", p.cnt("pvsm"))], inc=("ptsm", 1))
            dst = w1_sb[:].rearrange("a (d k) -> a k d", k=4)[:, k]
            p.step("v", lambda e, d=dst: e.tensor_copy(out=d, in_=ps_t1[0:128, 0:DT]),
                   waits=[("ptsm", p.cnt("ptsm"))], inc=("pvsm", 1))
        for k in range(4):
            src = pw0[0:DT, 0:512].rearrange("a (pp k) -> a k pp", k=4)[:, k]
            p.step("t", lambda e, s=src: e.transpose(ps_t1[0:128, 0:DT], s, id_f[0:DT, 0:DT]),
                   waits=[("pvsm", p.cnt("pvsm"))], inc=("ptsm", 1))
            dst = w2_sb[:].rearrange("a (d k) -> a k d", k=4)[:, k]
            p.step("v", lambda e, d=dst: e.tensor_copy(out=d, in_=ps_t1[0:128, 0:DT]),
                   waits=[("ptsm", p.cnt("ptsm"))], inc=("pvsm", 1))

        p.step("t", lambda e: e.transpose(ps_t1[0:128, 16:24], pw0[0:8, 1024:1152], id_f[0:8, 0:8]),
               waits=[(L, nld), ("pvsm", p.cnt("pvsm"))], inc=("ptsm", 1))
        p.step("t", lambda e: e.transpose(ps_t1[0:128, 24:32], pw0[0:8, 1280:1408], id_f[0:8, 0:8]),
               inc=("ptsm", 1))
        p.step("v", lambda e: e.tensor_copy(out=bias_f[:], in_=ps_t1[0:128, 16:24]),
               waits=[("ptsm", p.cnt("ptsm"))], inc=("pvsm", 1))
        p.step("a", lambda e: e.activation(out=ct_sb[:], in_=ps_t1[0:128, 24:32], func=AF.Silu),
               waits=[("ptsm", p.cnt("ptsm"))], inc=("pa", 1))
        PTSM_SMALL = p.cnt("ptsm")
        PA_CT = p.cnt("pa")

        p.step("a", lambda e: e.activation(out=gt_sb[0:1, 2:4], in_=gt_sb[0:1, 0:2], func=AF.Exp),
               waits=[(L, nld)], inc=("pa", 1))

        p.step("v", lambda e: e.tensor_tensor(out=gt_sb[0:1, 4:5], in0=gt_sb[0:1, 2:3],
                                              in1=gt_sb[0:1, 3:4], op=ALU.add),
               waits=[("pa", p.cnt("pa"))], inc=("gs", 1))
        p.step("v", lambda e: e.reciprocal(out=gt_sb[0:1, 5:6], in_=gt_sb[0:1, 4:5]),
               waits=[("gs", 1)], inc=("gs", 1))
        p.step("v", lambda e: e.tensor_tensor(out=gt_sb[0:1, 6:7], in0=gt_sb[0:1, 2:3],
                                              in1=gt_sb[0:1, 5:6], op=ALU.mult),
               waits=[("gs", 2)], inc=("gs", 1))
        p.step("v", lambda e: e.tensor_tensor(out=gt_sb[0:1, 7:8], in0=gt_sb[0:1, 3:4],
                                              in1=gt_sb[0:1, 5:6], op=ALU.mult),
               waits=[("gs", 3)], inc=("pv", 1))
        p.step("t", lambda e: e.matmul(ps_r[0:128, 0:1], ones_f[0:1, :], gt_sb[0:1, 6:7],
                                       start=True, stop=True),
               waits=[("pv", p.cnt("pv"))], inc=("pt", 1))
        p.step("t", lambda e: e.matmul(ps_r[0:128, 1:2], ones_f[0:1, :], gt_sb[0:1, 7:8],
                                       start=True, stop=True), inc=("pt", 1))
        p.step("t", lambda e: e.matmul(ps_r[0:128, 2:3], ones_f[0:1, :], pw0[0:1, 1536:1537],
                                       start=True, stop=True), waits=[(L, nld)], inc=("pt", 1))
        p.step("v", lambda e: e.tensor_copy(out=g0_bc[:], in_=ps_r[0:128, 0:1]),
               waits=[("pt", p.cnt("pt"))], inc=("pv", 1))
        p.step("v", lambda e: e.tensor_copy(out=g1_bc[:], in_=ps_r[0:128, 1:2]), inc=("pv", 1))
        p.step("v", lambda e: e.tensor_copy(out=ws_bc[:], in_=ps_r[0:128, 2:3]), inc=("pv", 1))
        PV_G = p.cnt("pv")

        for kt in range(8):
            buf = (pmod0, pmod1)[kt % 2]
            p.step("s", lambda e, b=buf, k=kt: e.dma_start(out=b[:], in_=wmod_d[k * 128:(k + 1) * 128, :]),
                   waits=[("ptm", max(0, p.cnt("ptm") - 48))], inc=("wm", 16))
            for jt in range(48):
                p.step("t", lambda e, b=buf, j=jt, k=kt: e.matmul(
                    ps_a[0:128, j:j + 1], b[:, j * 128:(j + 1) * 128], ct_sb[:, k:k + 1],
                    start=True, stop=True),
                    waits=([("wm", 16 * (kt + 1)), ("pa", p.cnt("pa")),
                            ("pmodf", max(0, kt))] if jt == 0 else []),
                    inc=("ptm", 1))
            if kt == 0:
                p.step("v", lambda e: e.tensor_copy(out=mod_sb[:], in_=ps_a[0:128, 0:48]),
                       waits=[("ptm", p.cnt("ptm"))], inc=("pmodf", 1))
            else:
                p.step("v", lambda e: e.tensor_tensor(out=mod_sb[:], in0=mod_sb[:],
                                                      in1=ps_a[0:128, 0:48], op=ALU.add),
                       waits=[("ptm", p.cnt("ptm"))], inc=("pmodf", 1))
        p.step("v", lambda e: e.engine_nop(), waits=[("pmodf", 8)], inc=("pv", 1))

        def em_modpost(e):
            ga_m, gb_m = mod_sb[:, 16:24], mod_sb[:, 40:48]
            e.tensor_scalar(out=ca1_sb[:], in0=mod_sb[:, 8:16], scalar1=1.0, scalar2=None, op0=ALU.add)
            e.tensor_copy(out=sa_sb[:], in_=mod_sb[:, 0:8])
            e.tensor_scalar(out=vec8[:, 0:8], in0=ga_m, scalar1=0.0, scalar2=None, op0=ALU.is_ge)
            e.tensor_scalar(out=vec8[:, 8:16], in0=gb_m, scalar1=0.0, scalar2=None, op0=ALU.is_ge)
            e.tensor_scalar(out=vec8[:, 16:24], in0=ga_m, scalar1=0.25, scalar2=None, op0=ALU.max)
            e.tensor_scalar(out=vec8[:, 24:32], in0=gb_m, scalar1=0.25, scalar2=None, op0=ALU.max)
            e.tensor_scalar(out=vec8[:, 32:40], in0=ga_m, scalar1=-0.25, scalar2=None, op0=ALU.min)
            e.tensor_scalar(out=vec8[:, 40:48], in0=gb_m, scalar1=-0.25, scalar2=None, op0=ALU.min)
            e.tensor_tensor(out=vec8[:, 16:24], in0=vec8[:, 16:24], in1=vec8[:, 0:8], op=ALU.mult)
            e.tensor_tensor(out=vec8[:, 24:32], in0=vec8[:, 24:32], in1=vec8[:, 8:16], op=ALU.mult)
            e.tensor_scalar(out=vec8[:, 0:8], in0=vec8[:, 0:8], scalar1=-1.0, scalar2=1.0,
                            op0=ALU.mult, op1=ALU.add)
            e.tensor_scalar(out=vec8[:, 8:16], in0=vec8[:, 8:16], scalar1=-1.0, scalar2=1.0,
                            op0=ALU.mult, op1=ALU.add)
            e.tensor_tensor(out=vec8[:, 32:40], in0=vec8[:, 32:40], in1=vec8[:, 0:8], op=ALU.mult)
            e.tensor_tensor(out=vec8[:, 40:48], in0=vec8[:, 40:48], in1=vec8[:, 8:16], op=ALU.mult)
            e.tensor_tensor(out=ga_sb[:], in0=vec8[:, 16:24], in1=vec8[:, 32:40], op=ALU.add)
            return e.tensor_tensor(out=vec8[:, 48:56], in0=vec8[:, 24:32], in1=vec8[:, 40:48],
                                   op=ALU.add)
        p.step("v", em_modpost, waits=[("pv", p.cnt("pv"))], inc=("mp", 1))
        p.step("v", lambda e: e.tensor_scalar(out=vec8[:, 56:64], in0=mod_sb[:, 32:40],
                                              scalar1=1.0, scalar2=None, op0=ALU.add),
               waits=[("mp", 1)], inc=("pv", 1))
        PV_MOD = p.cnt("pv")

        def make_rep(src_col_ap, dst):
            p.step("t", lambda e, s=src_col_ap: e.transpose(ps_t1[0:8, 0:128], s, id_f[:]),
                   waits=[("pv", PV_MOD), ("prv", p.cnt("prv")), ("pa", PA_CT)],
                   inc=("prt", 1))
            p.step("v", lambda e: e.tensor_copy(out=row8[:], in_=ps_t1[0:8, 0:128]),
                   waits=[("prt", p.cnt("prt")), ("prd", p.cnt("prd"))], inc=("prv", 1))
            def em_growdma(e, sems):
                for i in range(8):
                    e.dma_start(out=grow[0:1, i * 128:(i + 1) * 128],
                                in_=row8[i:i + 1, :]).then_inc(sems["prd"], 16)
            p.step("s", em_growdma,
                   waits=[("prv", p.cnt("prv")), ("prt2", p.cnt("prt2"))], inc=("prd", 128), raw=True)
            for half in range(2):
                p.step("t", lambda e, h=half: e.matmul(
                    ps_r[0:128, 0:512] if h == 0 else ps_r[0:128, 0:512],
                    ones_f[0:1, :], grow[0:1, h * 512:(h + 1) * 512],
                    start=True, stop=True),
                    waits=[("prd", p.cnt("prd")), ("prv2", p.cnt("prv2"))], inc=("prt2", 1))
                p.step("v", lambda e, d=dst, h=half: e.tensor_copy(
                    out=d[:, h * 512:(h + 1) * 512], in_=ps_r[0:128, 0:512]),
                    waits=[("prt2", p.cnt("prt2"))], inc=("prv2", 1))

        make_rep(vec8[:, 56:64], cb1r)
        make_rep(mod_sb[:, 24:32], sbr)
        make_rep(vec8[:, 48:56], gbr)
        make_rep(bias_f[:], gbiasr)
        p.step("v", lambda e: e.tensor_tensor(out=gbiasr[:], in0=gbiasr[:], in1=gbr[:], op=ALU.mult),
               waits=[("prv2", p.cnt("prv2"))], inc=("pv", 1))

        p.step("v", lambda e: e.tensor_copy(out=pwxb[:], in_=pwx[:]),
               waits=[(L, nld)], inc=("pwxc", 1))
        for dd in range(DT):
            p.step("t", lambda e, d=dd: e.transpose(
                ps_c1[0:128, 0:96], pwxb[0:96, d * 128:(d + 1) * 128], id_bf[0:96, 0:96]),
                waits=[("pwxc", 1), ("pwxv", p.cnt("pwxv"))], inc=("pwxt", 1))
            p.step("v", lambda e, d=dd: e.tensor_copy(
                out=wxT[:, d * 96:(d + 1) * 96], in_=ps_c1[0:128, 0:96]),
                waits=[("pwxt", p.cnt("pwxt"))], inc=("pwxv", 1))

        p.step("v", lambda e: e.tensor_copy(out=pwb[:, 0:DT * 64], in_=pw1[:, 0:DT * 64]),
               waits=[(L, nld)], inc=("pdtc", 1))
        for dd in range(DT):
            p.step("t", lambda e, d=dd: e.transpose(
                ps_c2[0:64, 0:128], pwb[:, d * 64:(d + 1) * 64], id_bf[:]),
                waits=[("pdtc", 1), ("pdtv", p.cnt("pdtv"))], inc=("pdtt", 1))
            p.step("v", lambda e, d=dd: e.tensor_copy(
                out=wdtT[0:64, d * 128:(d + 1) * 128], in_=ps_c2[0:64, 0:128]),
                waits=[("pdtt", p.cnt("pdtt"))], inc=("pdtv", 1))
        PDTV = p.cnt("pdtv")

        for j in range(8):
            src = ppk[:, (j % 2) * 1024:(j % 2) * 1024 + 1024]
            sh = 2 * (j // 2)

            def em_unpack(e, s=src, sh_=sh):
                e.tensor_scalar(out=pi1[:], in0=s, scalar1=sh_, scalar2=3,
                                op0=ALU.logical_shift_right, op1=ALU.bitwise_and)
                e.tensor_copy(out=pw0[:, 0:1024], in_=pi1[:])
                return e.tensor_scalar(out=pwb[:, 0:1024], in0=pw0[:, 0:1024], scalar1=1.0,
                                       scalar2=None, op0=ALU.subtract)
            p.step("v", em_unpack,
                   waits=[(L, nld), ("ptsm", PTSM_SMALL), ("pdtt", p.cnt("pdtt")),
                          ("pft", p.cnt("pft"))],
                   inc=("pfu", 1))
            for kt in range(8):
                p.step("t", lambda e, k=kt: e.transpose(
                    ps_c1[0:128, 0:128], pwb[:, k * 128:(k + 1) * 128], id_bf[:]),
                    waits=[("pfu", p.cnt("pfu")), ("pfv", p.cnt("pfv")), ("pwxv", DT)],
                    inc=("pft", 1))
                p.step("v", lambda e, k=kt: e.tensor_copy(
                    out=pwb[:, 1024 + k * 128:1024 + (k + 1) * 128], in_=ps_c1[0:128, 0:128]),
                    waits=[("pft", p.cnt("pft")), ("pfs", 16 * j)], inc=("pfv", 1))
            p.step("s", lambda e, j_=j: e.dma_start(out=wffnd[j_], in_=pwb[:, 1024:2048]),
                   waits=[("pfv", p.cnt("pfv"))], inc=("pfs", 16))
        PFT = p.cnt("pft")

        for ot in range(8):
            buf = (pw0, pw1)[ot % 2]
            p.step("s", lambda e, b=buf, o=ot: e.dma_start(
                out=b[:, 0:DI], in_=w_out_d[o * 128:(o + 1) * 128, :]),
                waits=[("pov", max(0, p.cnt("pov") - DT)), ("pfu", p.cnt("pfu"))],
                inc=("pod", 16))

            def em_wout_cast(e, b=buf, o=ot):
                e.tensor_scalar(out=b[:, 0:DI], in0=b[:, 0:DI], scalar1=ga_sb[:, o:o + 1],
                                scalar2=None, op0=ALU.mult)
                return e.tensor_copy(out=pwb[:, 0:DI], in_=b[:, 0:DI])
            p.step("v", em_wout_cast,
                   waits=[("pod", 16 * (ot + 1)), ("pv", PV_MOD), ("pft", PFT),
                          ("pot", p.cnt("pot")), ("pfs", 128)],
                   inc=("poc", 1))
            for dd in range(DT):
                p.step("t", lambda e, d=dd: e.transpose(
                    ps_c1[0:128, 0:128], pwb[:, d * 128:(d + 1) * 128], id_bf[:]),
                    waits=[("poc", p.cnt("poc")), ("pov", p.cnt("pov")), ("pfv", 64)],
                    inc=("pot", 1))
                p.step("v", lambda e, d=dd: e.tensor_copy(
                    out=pwb2[:, d * 128:(d + 1) * 128], in_=ps_c1[0:128, 0:128]),
                    waits=[("pot", p.cnt("pot")), ("pos", 16 * ot)], inc=("pov", 1))
            p.step("s", lambda e, o=ot: e.dma_start(
                out=woutd[o], in_=pwb2[:, 0:DT * 128]),
                waits=[("pov", p.cnt("pov"))], inc=("pos", 16))
        POT = p.cnt("pot")

        for et in range(NE):
            buf = (pw0, pw1)[et % 2]
            p.step("s", lambda e, b=buf, E=et: e.dma_start(
                out=b[:, 0:1024], in_=w_in_d[E * 128:(E + 1) * 128, :]),
                waits=[("pic", max(0, p.cnt("pic") - 1)), ("pov", p.cnt("pov"))],
                inc=("pid", 16))
            p.step("v", lambda e, b=buf: e.tensor_copy(out=pwb[:, 0:1024], in_=b[:, 0:1024]),
                   waits=[("pid", 16 * (et + 1)), ("pot", POT), ("pit", p.cnt("pit"))],
                   inc=("pic", 1))
            for kt in range(8):
                p.step("t", lambda e, k=kt: e.transpose(
                    ps_c1[0:128, 0:128], pwb[:, k * 128:(k + 1) * 128], id_bf[:]),
                    waits=[("pic", p.cnt("pic")), ("piv", p.cnt("piv")), ("pov", 8 * DT)],
                    inc=("pit", 1))
                p.step("v", lambda e, k=kt: e.tensor_copy(
                    out=pwb[:, 1024 + k * 128:1024 + (k + 1) * 128], in_=ps_c1[0:128, 0:128]),
                    waits=[("pit", p.cnt("pit")), ("pis", 16 * et)], inc=("piv", 1))
            p.step("s", lambda e, E=et: e.dma_start(out=wintd[E], in_=pwb[:, 1024:2048]),
                   waits=[("piv", p.cnt("piv"))], inc=("pis", 16))

        p.step("g", lambda e: e.memset(hstate[:], 0.0), inc=("pg", 1))
        p.step("g", lambda e: e.memset(xconv[:], 0.0),
               waits=[(L, nld), ("pis", p.cnt("pis")), ("piv", p.cnt("piv")),
                      ("pov", p.cnt("pov")), ("pfv", p.cnt("pfv")),
                      ("pwxv", p.cnt("pwxv")), ("pdtv", p.cnt("pdtv")),
                      ("pvsm", p.cnt("pvsm")), ("prv2", p.cnt("prv2")),
                      ("pv", p.cnt("pv")), ("pa", p.cnt("pa")),
                      ("ptm", p.cnt("ptm")), ("pit", p.cnt("pit")),
                      ("pot", p.cnt("pot")), ("pft", p.cnt("pft")),
                      ("prd", p.cnt("prd")), ("wm", p.cnt("wm")),
                      ("pod", p.cnt("pod")), ("pid", p.cnt("pid")),
                      ("pmodf", p.cnt("pmodf")), ("pfu", p.cnt("pfu")),
                      ("poc", p.cnt("poc")), ("pic", p.cnt("pic"))],
               inc=("pg", 1))
        p.step("g", lambda e: e.memset(cst[:, 0:1], LN_EPS), inc=("pg", 1))
        p.step("g", lambda e: e.memset(cst[:, 1:2], -0.5), inc=("pg", 1))
        p.step("g", lambda e: e.memset(cst[:, 2:3], 1.0), inc=("pg", 1))

        p.step("s", lambda e: e.dma_start(out=gt_sb[0:1, 0:1], in_=ones_d[0:1, 0:1]),
               waits=[("pis", p.cnt("pis")), (L, nld), ("wm", p.cnt("wm")),
                      ("pod", p.cnt("pod")), ("pid", p.cnt("pid")), ("prd", p.cnt("prd"))],
               inc=("pdone", 16))
        p.step("v", lambda e: e.engine_nop(),
               waits=[("pv", p.cnt("pv")), ("prv2", p.cnt("prv2")), ("pvsm", p.cnt("pvsm")),
                      ("pfv", p.cnt("pfv")), ("pov", p.cnt("pov")), ("piv", p.cnt("piv")),
                      ("pwxv", p.cnt("pwxv")), ("pdtv", PDTV)],
               inc=("pdone", 1))
        p.step("a", lambda e: e.copy(out=gt_sb[0:1, 1:2], in_=gt_sb[0:1, 1:2]),
               waits=[("pa", p.cnt("pa"))], inc=("pdone", 1))
        p.step("t", lambda e: e.matmul(ps_r[0:128, 3:4], ones_f[0:1, :], ones_f[0:1, 0:1],
                                       start=True, stop=True),
               waits=[("ptm", p.cnt("ptm")), ("prt2", p.cnt("prt2")), ("pit", p.cnt("pit")),
                      ("prv2", p.cnt("prv2")), ("pv", p.cnt("pv"))],
               inc=("pdone", 1))
        p.step("g", lambda e: e.engine_nop(), waits=[("pg", p.cnt("pg"))], inc=("pdone", 1))
        PD = p.cnt("pdone")

        ET = mybir.EngineType
        slreg = list(nc.alloc_registers("slreg", engines=[ET.Pool]))[0]
        pcnt = list(nc.alloc_registers("pollcnt", engines=[ET.Pool]))[0]
        fvreg = list(nc.alloc_registers("fvreg", engines=[ET.Pool]))[0]
        slreg_sp = list(nc.alloc_registers("slreg_sp", engines=[ET.SP]))[0]
        spk = {"ws": 0}

        def em_flagtab(g, sems):
            g.memset(flagtab[0:1, 0:1], 999)
            g.memset(flagtab[0:1, 1:2], 1000)
            g.memset(flagtab[0:1, 2:3], 999)
            for c_ in range(NCH):
                vals = [(1001 + 2 * c_, 1000 + 2 * c_), (1001 + 2 * c_, 1001 + 2 * c_),
                        (1002 + 2 * c_, 1001 + 2 * c_), (1002 + 2 * c_, 1002 + 2 * c_)]
                for e_, (wv, rv) in enumerate(vals):
                    off = 4 + 2 * (4 * c_ + e_)
                    g.memset(flagtab[0:1, off:off + 1], wv)
                    g.memset(flagtab[0:1, off + 1:off + 2], rv)
            g.engine_nop().then_inc(sems["pfl"], 1)
        p.step("g", em_flagtab, inc=("pfl", 1), raw=True)

        def sp_slotw(e, sems, tag, dst0, dst1, srcap):
            nc.br_cmp(slreg_sp, 0, on_true=f"sw0_{tag}", on_false=f"sw1_{tag}",
                      comp_op="IS_EQ", engines=[ET.SP])
            with nc.body(f"sw0_{tag}", valid_engines=[ET.SP]):
                e.dma_start(out=dst0, in_=srcap).then_inc(sems["sws"], 16)
                nc.br(f"swj_{tag}", engines=[ET.SP])
            with nc.body(f"sw1_{tag}", valid_engines=[ET.SP]):
                e.dma_start(out=dst1, in_=srcap).then_inc(sems["sws"], 16)
                nc.br(f"swj_{tag}", engines=[ET.SP])
            nc.switch_bb(f"swj_{tag}")
            spk["ws"] += 16
            e.wait_ge(sems["sws"], spk["ws"])

        def sp_flagw(e, sems, tag, entry):
            off = 4 + 2 * entry
            sp_slotw(e, sems, f"f{tag}", shf[0, 0:1, 0:2], shf[1, 0:1, 0:2],
                     flagtab[0:1, off:off + 2])

        def gp_poll(g, sems, tag, word, target, exact=False):
            nc.br(f"pp_{tag}", engines=[ET.Pool])
            with nc.body(f"pp_{tag}", valid_engines=[ET.Pool]):
                g.reg_alu(pcnt, pcnt, 16, op=ALU.add)
                nc.br_cmp(slreg, 0, on_true=f"pr1_{tag}", on_false=f"pr0_{tag}",
                          comp_op="IS_EQ", engines=[ET.Pool])
                with nc.body(f"pr0_{tag}", valid_engines=[ET.Pool]):
                    g.dma_start(out=fsb[0:1, 0:4], in_=shf[0, 0:1, 0:4]).then_inc(sems["xpl"], 16)
                    g.wait_ge(sems["xpl"], pcnt)
                    g.reg_load(fvreg, fsb[0:1, word:word + 1])
                    nc.br(f"pc_{tag}", engines=[ET.Pool])
                with nc.body(f"pr1_{tag}", valid_engines=[ET.Pool]):
                    g.dma_start(out=fsb[0:1, 4:8], in_=shf[1, 0:1, 0:4]).then_inc(sems["xpl"], 16)
                    g.wait_ge(sems["xpl"], pcnt)
                    g.reg_load(fvreg, fsb[0:1, 4 + word:5 + word])
                    nc.br(f"pc_{tag}", engines=[ET.Pool])
                nc.switch_bb(f"pc_{tag}")
                cop = "IS_EQ" if exact else "IS_GE"
                nc.br_cmp(fvreg, target, on_true=f"pdn_{tag}", on_false=f"pp_{tag}",
                          comp_op=cop, engines=[ET.Pool])
            nc.switch_bb(f"pdn_{tag}")

        p.step("s", lambda e: e.dma_start(out=slsb[:], in_=sl_d), inc=("sld", 16))

        def em_spinit(e, sems):
            e.reg_load(slreg_sp, slsb[0:1, 0:1])
            sp_slotw(e, sems, "init", shf[0, 0:1, 0:3], shf[1, 0:1, 0:3],
                     flagtab[0:1, 0:3])
            e.nop().then_inc(sems["sxi"], 1)
        p.step("s", em_spinit, waits=[("sld", 16), ("pfl", 1)], inc=("sxi", 1), raw=True)

        def em_xinit(g, sems):
            g.reg_load(slreg, slsb[0:1, 0:1])
            nc.reg_mov(pcnt, 0)
            gp_poll(g, sems, "init", 2, 999, exact=True)
            g.engine_nop().then_inc(sems["xin"], 1)
        p.step("g", em_xinit, waits=[("pdone", PD), ("sld", 16)], inc=("xin", 1), raw=True)

        x_r = x_d.rearrange("(c tt pp) f -> c tt pp f", pp=128, tt=TT)
        out_r = out_d.rearrange("(c tt pp) f -> c tt pp f", pp=128, tt=TT)

        for c in range(NCH):
            fw = [("pdone", PD)] if c == 0 else []

            def em_xload(e, sems, c_=c):
                for t in range(TT):
                    e.dma_start(out=x_tm[:, t * DM:(t + 1) * DM],
                                in_=x_r[c_, t]).then_inc(sems["x"], 16)
            p.step("s", em_xload, waits=fw + [("ffd", c * 8)], inc=("x", 16 * TT), raw=True)

            for tt in range(TT):
                p.step("a", lambda e, t=tt: e.activation(
                    out=out_tm[:, t * DM:(t + 1) * DM], in_=x_tm[:, t * DM:(t + 1) * DM],
                    func=AF.Square, accum_out=stat[:, 4 + t:5 + t]),
                    waits=fw + [("x", 16 * TT * (c + 1)), ("q", c), ("out", 16 * TT * c)], inc=("sq", 1))
                p.step("v", lambda e, t=tt: e.reduce_sum(
                    out=stat[:, t:t + 1], in_=x_tm[:, t * DM:(t + 1) * DM], axis=AX),
                    waits=fw + [("x", 16 * TT * (c + 1)), ("ff", c * 8)], inc=("vst", 1))

            def em_lnp_a(e):
                e.tensor_scalar(out=stat[:, 8:8 + TT], in0=stat[:, 0:TT], scalar1=1.0 / DM,
                                scalar2=None, op0=ALU.mult)
                e.tensor_scalar(out=stat[:, 12:12 + TT], in0=stat[:, 4:4 + TT], scalar1=1.0 / DM,
                                scalar2=None, op0=ALU.mult)
                return e.tensor_tensor(out=stat[:, 16:16 + TT], in0=stat[:, 8:8 + TT],
                                       in1=stat[:, 8:8 + TT], op=ALU.mult)

            def em_lnp_b(e):
                return e.tensor_tensor(out=stat[:, 12:12 + TT], in0=stat[:, 12:12 + TT],
                                       in1=stat[:, 16:16 + TT], op=ALU.subtract)

            def em_lnr(e):
                e.activation(out=stat[:, 20:20 + TT], in_=stat[:, 12:12 + TT], func=AF.Ln,
                             bias=cst[:, 0:1])
                return e.activation(out=stat[:, 24:24 + TT], in_=stat[:, 20:20 + TT], func=AF.Exp,
                                    scale=cst[:, 1:2])

            def em_nmr(e):
                return e.scalar_tensor_tensor(out=stat[:, 28:28 + TT], in0=stat[:, 8:8 + TT],
                                              scalar=-1.0, in1=stat[:, 24:24 + TT],
                                              op0=ALU.mult, op1=ALU.mult)

            p.step("v", em_lnp_a, waits=[("sq", (2 * c + 1) * TT), ("vst", p.cnt("vst"))],
                   inc=("vst", 1))
            p.step("v", em_lnp_b, waits=[("vst", p.cnt("vst"))], inc=("vst", 1))
            p.step("a", em_lnr, waits=[("vst", p.cnt("vst"))], inc=("ln1", 1))
            p.step("v", em_nmr, waits=[("ln1", c + 1)], inc=("vst", 1))
            VST_N1 = p.cnt("vst")

            for tt in range(TT):
                p.step("v", lambda e, t=tt: e.tensor_scalar(
                    out=out_tm[:, t * DM:(t + 1) * DM], in0=x_tm[:, t * DM:(t + 1) * DM],
                    scalar1=stat[:, 24 + t:25 + t], scalar2=stat[:, 28 + t:29 + t],
                    op0=ALU.mult, op1=ALU.add),
                    waits=[("vst", VST_N1), ("out", 16 * TT * c)], inc=("xlns", 1))
            for ft in range(FT):
                pst = (ps_t1, ps_t2)[ft % 2]
                for tt in range(TT):
                    p.step("t", lambda e, f=ft, t=tt, ps=pst: e.transpose(
                        ps[0:128, t * 128:(t + 1) * 128],
                        out_tm[:, t * DM + f * 128:t * DM + (f + 1) * 128], id_f[:]),
                        waits=([("xlns", c * TT + TT), ("tpf", max(0, c * FT + ft - 1)),
                                ("ffd", c * 8)] if tt == 0 else []),
                        inc=("tp", 1) if tt == TT - 1 else None)
                for tt in range(TT):
                    p.step("v", lambda e, f=ft, t=tt, ps=pst: e.tensor_scalar(
                        out=shbf[:, f * TC + t * 128:f * TC + (t + 1) * 128],
                        in0=ps[0:128, t * 128:(t + 1) * 128],
                        scalar1=ca1_sb[:, f:f + 1], scalar2=sa_sb[:, f:f + 1],
                        op0=ALU.mult, op1=ALU.add),
                        waits=([("tp", c * FT + ft + 1), ("ff", c * 8)] if tt == 0 else []),
                        inc=("tpf", 1) if tt == TT - 1 else None)

            for et in range(NE):
                ws = et % 2
                p.step("s", lambda e, E=et, w=ws: e.dma_start(
                    out=wbuf[:, w * DM:(w + 1) * DM], in_=wintd[E]),
                    waits=[("xz", max(0, c * NE + et - 1))], inc=("wd", 16))
                psm = (ps_a, ps_b)[et % 2]
                if et < NE // 2:
                    bw = ("mmfx", max(0, c * (NE // 2) + et - 1))
                else:
                    bw = ("mmfz", max(0, c * (NE // 2) + (et - NE // 2) - 1))
                for kt in range(8):
                    p.step("t", lambda e, k=kt, w=ws, ps=psm: e.matmul(
                        ps[:], wbuf[:, w * DM + k * 128:w * DM + (k + 1) * 128],
                        shbf[:, k * TC:(k + 1) * TC],
                        start=(k == 0), stop=(k == 7)),
                        waits=([("wd", 16 * (c * NE + et + 1)), ("tpf", c * FT + FT), bw,
                                ("ffe0", c * 8)] if kt == 0 else []),
                        inc=("xz", 1) if kt == 7 else None)
                if et < NE // 2:
                    p.step("v", lambda e, E=et, ps=psm: e.tensor_copy(
                        out=xconv[:, E * (TC + HALO) + HALO:(E + 1) * (TC + HALO)], in_=ps[:]),
                        waits=[("xz", c * NE + et + 1), ("wo", c * 8), ("halo", c)],
                        inc=("mmfx", 1))
                else:
                    p.step("a", lambda e, E=et, ps=psm: e.activation(
                        out=z_sb[:, (E - NE // 2) * TC:(E - NE // 2 + 1) * TC], in_=ps[:],
                        func=AF.Silu),
                        waits=[("xz", c * NE + et + 1), ("y2", c * DT)],
                        inc=("mmfz", 1))

            for dd in range(DT):
                base = dd * (TC + HALO)

                def em_c1(e, d=dd, b=base):
                    e.tensor_scalar(out=acc1[:], in0=xconv[:, b + 3:b + 3 + TC],
                                    scalar1=w1_sb[:, d * 4:d * 4 + 1], scalar2=None, op0=ALU.mult)
                    e.scalar_tensor_tensor(out=acc1[:], in0=xconv[:, b + 4:b + 4 + TC],
                                           scalar=w1_sb[:, d * 4 + 1:d * 4 + 2], in1=acc1[:],
                                           op0=ALU.mult, op1=ALU.add)
                    e.scalar_tensor_tensor(out=acc1[:], in0=xconv[:, b + 5:b + 5 + TC],
                                           scalar=w1_sb[:, d * 4 + 2:d * 4 + 3], in1=acc1[:],
                                           op0=ALU.mult, op1=ALU.add)
                    return e.scalar_tensor_tensor(out=acc1[:], in0=xconv[:, b + 6:b + 6 + TC],
                                                  scalar=w1_sb[:, d * 4 + 3:d * 4 + 4], in1=acc1[:],
                                                  op0=ALU.mult, op1=ALU.add)
                p.step("v", em_c1,
                       waits=[("mmfx", c * DT + dd + 1), ("cvs", max(0, c * DT + dd))],
                       inc=("cva", 1))

                def em_c2(e, d=dd, b=base):
                    e.tensor_scalar(out=acc2[:], in0=xconv[:, b + 0:b + 0 + TC],
                                    scalar1=w2_sb[:, d * 4:d * 4 + 1], scalar2=None, op0=ALU.mult)
                    e.scalar_tensor_tensor(out=acc2[:], in0=xconv[:, b + 2:b + 2 + TC],
                                           scalar=w2_sb[:, d * 4 + 1:d * 4 + 2], in1=acc2[:],
                                           op0=ALU.mult, op1=ALU.add)
                    e.scalar_tensor_tensor(out=acc2[:], in0=xconv[:, b + 4:b + 4 + TC],
                                           scalar=w2_sb[:, d * 4 + 2:d * 4 + 3], in1=acc2[:],
                                           op0=ALU.mult, op1=ALU.add)
                    return e.scalar_tensor_tensor(out=acc2[:], in0=xconv[:, b + 6:b + 6 + TC],
                                                  scalar=w2_sb[:, d * 4 + 3:d * 4 + 4], in1=acc2[:],
                                                  op0=ALU.mult, op1=ALU.add)
                p.step("v", em_c2, inc=("cva", 1))

                def em_si(e, d=dd):
                    e.activation(out=acc1[:], in_=acc1[:], func=AF.Silu, bias=b1_sb[:, d:d + 1])
                    return e.activation(out=acc2[:], in_=acc2[:], func=AF.Silu, bias=b2_sb[:, d:d + 1])
                p.step("a", em_si, waits=[("cva", (c * DT + dd) * 2 + 2)], inc=("cvs", 1))

                def em_gate(e, d=dd):
                    e.tensor_scalar(out=acc1[:], in0=acc1[:], scalar1=g0_bc[:], scalar2=None,
                                    op0=ALU.mult)
                    return e.scalar_tensor_tensor(out=xc_sb[:, d * TC:(d + 1) * TC], in0=acc2[:],
                                                  scalar=g1_bc[:], in1=acc1[:],
                                                  op0=ALU.mult, op1=ALU.add)
                p.step("v", em_gate, waits=[("cvs", c * DT + dd + 1)], inc=("cvg", 1))

            def em_halo(e):
                inst = None
                for d in range(DT):
                    bb = d * (TC + HALO)
                    inst = e.tensor_copy(out=xconv[:, bb:bb + HALO],
                                         in_=xconv[:, bb + TC:bb + TC + HALO])
                return inst
            p.step("g", em_halo, waits=[("cvg", c * DT + DT)], inc=("halo", 1))

            for dd in range(DT):
                p.step("t", lambda e, d=dd: e.matmul(
                    ps_a[0:96, :], wxT[:, d * 96:(d + 1) * 96], xc_sb[:, d * TC:(d + 1) * TC],
                    start=(d == 0), stop=(d == DT - 1)),
                    waits=([("cvg", c * DT + dd + 1)] +
                           ([("xde", c), ("mmfx", c * DT + DT), ("mmfz", c * DT + DT)]
                            if dd == 0 else [])),
                    inc=("xd", 1))
            p.step("v", lambda e: e.tensor_copy(out=xdblp[:], in_=ps_a[0:96, :]),
                   waits=[("xd", c * DT + DT), ("pdt", c * DT), ("sxw", c)], inc=("xde0", 1))

            def em_xclr(g, sems, c_=c):
                gp_poll(g, sems, f"xr{c_}", 1, 1000 + 2 * c_)
                g.engine_nop().then_inc(sems["xclr"], 1)
            p.step("g", em_xclr, waits=[("xin", 1)] if c == 0 else [],
                   inc=("xclr", 1), raw=True)

            def em_sxw(e, sems, c_=c):
                sp_slotw(e, sems, f"xd{c_}", shxd[0], shxd[1], xdblp[:])
                sp_flagw(e, sems, f"xw{c_}", 4 * c_ + 0)
                e.nop().then_inc(sems["sxw"], 1)
            p.step("s", em_sxw, waits=[("xde0", c + 1), ("xclr", c + 1), ("sxi", 1)],
                   inc=("sxw", 1), raw=True)

            def em_xrdy(g, sems, c_=c):
                gp_poll(g, sems, f"xw{c_}", 0, 1001 + 2 * c_)
                g.engine_nop().then_inc(sems["xrdy"], 1)
            p.step("g", em_xrdy, inc=("xrdy", 1), raw=True)

            def em_sxr(e, sems, c_=c):
                e.dma_start(out=xdb0[:], in_=shxd[0]).then_inc(sems["sws"], 16)
                e.dma_start(out=xdb1[:], in_=shxd[1]).then_inc(sems["sws"], 16)
                spk["ws"] += 32
                e.wait_ge(sems["sws"], spk["ws"])
                e.nop().then_inc(sems["sxr"], 1)
                sp_flagw(e, sems, f"xa{c_}", 4 * c_ + 1)
            p.step("s", em_sxr, waits=[("xrdy", c + 1), ("xde", c)],
                   inc=("sxr", 1), raw=True)

            def em_xde(e):
                e.tensor_tensor(out=xdblp[:], in0=xdb0[:], in1=xdb1[:], op=ALU.add)
                e.tensor_copy(out=dtlow[:], in_=xdblp[0:64, :])
                return e.tensor_copy(out=xdbc[:], in_=xdblp[64:96, :])
            p.step("v", em_xde, waits=[("sxr", c + 1)], inc=("xde", 1))

            def em_dtmm(dd):
                psd = (ps_a, ps_b)[dd % 2]
                p.step("t", lambda e, d=dd, ps=psd: e.matmul(
                    ps[:], wdtT[0:64, d * 128:(d + 1) * 128], dtlow[:], start=True, stop=True),
                    waits=([("xde", c + 1), ("dl", max(0, c * DT + dd - 1))] +
                           ([("mmfx", c * DT + DT), ("mmfz", c * DT + DT)] if dd < 2 else [])),
                    inc=("pdt", 1))
            em_dtmm(0)
            em_dtmm(1)

            border = [16 * k + n for n in range(16) for k in range(2)]
            bc_q = []
            for idx, j in enumerate(border):
                psc = (ps_t1, ps_t2)[idx % 2]
                p.step("t", lambda e, j_=j, ps=psc: e.matmul(
                    ps[:], sel_sb[:, j_ * 128:(j_ + 1) * 128], xdbc[:], start=True, stop=True),
                    waits=[("xde", c + 1), ("bcf", max(0, c * 32 + idx - 1)),
                           ("tpf", c * FT + FT)],
                    inc=("bc", 1))
                bc_q.append((idx, j))

            def pop_bc(k):
                for _ in range(k):
                    if not bc_q:
                        return
                    idx, j = bc_q.pop(0)
                    dst = brep if j < 16 else crep
                    jn = j % 16
                    p.step("a", lambda e, d=dst, n_=jn, ps=(ps_t1, ps_t2)[idx % 2]: e.copy(
                        out=d[:, n_ * TC:(n_ + 1) * TC], in_=ps[:]),
                        waits=[("bc", c * 32 + idx + 1), ("sl", c * SL)],
                        inc=("bcf", 1))

            for dd in range(DT):
                psd = (ps_a, ps_b)[dd % 2]
                if dd >= 2:
                    em_dtmm(dd)
                if dd == 0:
                    pop_bc(4)
                p.step("v", lambda e, d=dd, ps=psd: e.tensor_scalar(
                    out=sm_sl[:], in0=ps[:], scalar1=bdt_sb[:, d:d + 1], scalar2=30.0,
                    op0=ALU.add, op1=ALU.min),
                    waits=[("pdt", c * DT + dd + 1), ("sfl", max(0, c * DT + dd))],
                    inc=("sm", 1))

                def em_sfl(e):
                    e.activation(out=mixfm[:], in_=sm_sl[:], func=AF.Exp)
                    return e.activation(out=sm_sl[:], in_=mixfm[:], func=AF.Ln, bias=cst[:, 2:3])
                p.step("a", em_sfl,
                       waits=[("sm", c * DT + dd + 1), ("dl", max(0, c * DT + dd))],
                       inc=("sfl", 1))
                if dd == 0:
                    pop_bc(2)

                def em_dl(e, d=dd, ps=psd):
                    e.scalar_tensor_tensor(out=dl_sl[:], in0=ps[:], scalar=bdt_sb[:, d:d + 1],
                                           in1=sm_sl[:], op0=ALU.add, op1=ALU.max)
                    return e.tensor_tensor(out=g_sl[:], in0=dl_sl[:],
                                           in1=xc_sb[:, d * TC:(d + 1) * TC], op=ALU.mult)
                p.step("v", em_dl,
                       waits=[("sfl", c * DT + dd + 1), ("da", c * SL + dd * 16)],
                       inc=("dl", 1))

                for n in range(N):
                    si = dd * N + n
                    ds = si % 2
                    p.step("a", lambda e, d=dd, n_=n, s=ds: e.activation(
                        out=da_sl[:, s * TC:(s + 1) * TC], in_=dl_sl[:], func=AF.Exp,
                        scale=a_sb[:, d * N + n_:d * N + n_ + 1]),
                        waits=[("dl", c * DT + dd + 1), ("scn", max(0, c * SL + si - 1))],
                        inc=("da", 1))
                    if dd == 0:
                        pop_bc(2)

                    def em_slab(e, d=dd, n_=n, s=si, dsl=ds):
                        e.tensor_tensor(out=u_sl[:], in0=g_sl[:],
                                        in1=brep[:, n_ * TC:(n_ + 1) * TC], op=ALU.mult)
                        return e.tensor_tensor_scan(
                            out=h_sl[:, n_ * TC:(n_ + 1) * TC],
                            data0=da_sl[:, dsl * TC:(dsl + 1) * TC], data1=u_sl[:],
                            initial=hstate[:, d * N + n_:d * N + n_ + 1],
                            op0=ALU.mult, op1=ALU.add)
                    p.step("v", em_slab,
                           waits=[("da", c * SL + si + 1),
                                  ("st", max(0, c * DT + dd))] +
                                 ([("bcf", c * 32 + 2 * n + 1)] if dd == 0 else []),
                           inc=("scn", 1))

                    def em_y(e, d=dd, n_=n):
                        yv = xconv[:, d * (TC + HALO) + HALO:(d + 1) * (TC + HALO)]
                        if n_ == 0:
                            return e.tensor_tensor(out=yv, in0=h_sl[:, n_ * TC:(n_ + 1) * TC],
                                                   in1=crep[:, n_ * TC:(n_ + 1) * TC], op=ALU.mult)
                        e.tensor_tensor(out=t1b_sl[:], in0=h_sl[:, n_ * TC:(n_ + 1) * TC],
                                        in1=crep[:, n_ * TC:(n_ + 1) * TC], op=ALU.mult)
                        return e.tensor_tensor(out=yv, in0=yv, in1=t1b_sl[:], op=ALU.add)
                    p.step("v", em_y,
                           waits=([("halo", c + 1)] if n == 0 else []) +
                                 ([("bcf", c * 32 + 2 * n + 2)] if dd == 0 else []),
                           inc=("sl", 1))

                hv = h_sl[:].rearrange("p (n t) -> p t n", t=TC)[:, TC - 1]
                p.step("g", lambda e, d=dd, h=hv: e.tensor_copy(
                    out=hstate[:, d * N:(d + 1) * N], in_=h),
                    waits=[("scn", c * SL + dd * N + N)], inc=("st", 1))

                def em_y2(e, d=dd):
                    yv = xconv[:, d * (TC + HALO) + HALO:(d + 1) * (TC + HALO)]
                    e.scalar_tensor_tensor(out=yt_sl[:], in0=xc_sb[:, d * TC:(d + 1) * TC],
                                           scalar=d_sb[:, d:d + 1], in1=yv,
                                           op0=ALU.mult, op1=ALU.add)
                    return e.tensor_tensor(out=shbf[:, d * TC:(d + 1) * TC], in0=yt_sl[:],
                                           in1=z_sb[:, d * TC:(d + 1) * TC], op=ALU.mult)
                p.step("v", em_y2, waits=[("mmfz", c * (NE // 2) + NE // 2)], inc=("y2", 1))

            def em_mclr(g, sems, c_=c):
                gp_poll(g, sems, f"mr{c_}", 1, 1001 + 2 * c_)
                g.engine_nop().then_inc(sems["mclr"], 1)
            p.step("g", em_mclr, inc=("mclr", 1), raw=True)

            for ot in range(8):
                wsl = (ot % 2) * (DT * 128)
                if ot == 0:
                    p.step("s", lambda e, o=ot, w=wsl: e.dma_start(
                        out=wobuf[:, w:w + DT * 128], in_=woutd[o]),
                        waits=[("wo", max(0, c * 8 - 1))], inc=("wod", 16))
                psm = (ps_a, ps_b)[ot % 2]
                for kt in range(DT):
                    p.step("t", lambda e, o=ot, k=kt, w=wsl, ps=psm: e.matmul(
                        ps[:], wobuf[:, w + k * 128:w + (k + 1) * 128],
                        shbf[:, k * TC:(k + 1) * TC],
                        start=(k == 0), stop=(k == DT - 1)),
                        waits=([("y2", c * DT + DT), ("mxe", max(0, c * 8 + ot - 1)),
                                ("wod", 16 * (c * 8 + ot + 1))]
                               if kt == 0 else []),
                        inc=("wo", 1) if kt == DT - 1 else None)
                if ot < 7:
                    w2 = ((ot + 1) % 2) * (DT * 128)
                    p.step("s", lambda e, o=ot + 1, w=w2: e.dma_start(
                        out=wobuf[:, w:w + DT * 128], in_=woutd[o]),
                        waits=[("wo", c * 8 + ot)], inc=("wod", 16))
                msl = (ot % 2) * TC
                p.step("v", lambda e, m=msl, ps=psm: e.tensor_copy(
                    out=mixst[:, m:m + TC], in_=ps[:]),
                       waits=[("wo", c * 8 + ot + 1), ("smw", max(0, c * 8 + ot - 1))],
                       inc=("mxe", 1))

                def em_smw(e, sems, c_=c, o=ot, m=msl):
                    sp_slotw(e, sems, f"mx{c_}_{o}", shmx[0, o], shmx[1, o],
                             mixst[:, m:m + TC])
                    if o == 7:
                        sp_flagw(e, sems, f"mw{c_}", 4 * c_ + 2)
                    e.nop().then_inc(sems["smw"], 1)
                p.step("s", em_smw,
                       waits=[("mxe", c * 8 + ot + 1)] + ([("mclr", c + 1)] if ot == 0 else []),
                       inc=("smw", 1), raw=True)

            def em_mrdy(g, sems, c_=c):
                gp_poll(g, sems, f"mw{c_}", 0, 1002 + 2 * c_)
                g.engine_nop().then_inc(sems["mrdy"], 1)
            p.step("g", em_mrdy, inc=("mrdy", 1), raw=True)

            for ot in range(8):
                def em_smr(e, sems, c_=c, o=ot):
                    pb = (o % 2) * TC
                    e.dma_start(out=mx0[:, pb:pb + TC], in_=shmx[0, o]).then_inc(sems["sws"], 16)
                    spk["ws"] += 16
                    e.wait_ge(sems["sws"], spk["ws"])
                    e.nop().then_inc(sems["smr"], 1)
                    if o == 7:
                        e.wait_ge(sems["aws"], 16 * 8 * (c_ + 1))
                        sp_flagw(e, sems, f"ma{c_}", 4 * c_ + 3)
                p.step("s", em_smr,
                       waits=([("mrdy", c + 1)] if ot == 0 else []) +
                             [("mxs", max(0, c * 8 + ot - 1))],
                       inc=("smr", 1), raw=True)

                def em_amr(a, sems, c_=c, o=ot):
                    pb = (o % 2) * TC
                    a.dma_start(out=mx1[:, pb:pb + TC], in_=shmx[1, o]).then_inc(sems["aws"], 16)
                p.step("a", em_amr,
                       waits=([("mrdy", c + 1)] if ot == 0 else []) +
                             [("mxs", max(0, c * 8 + ot - 1))],
                       inc=("aws", 16), raw=True)
                pb = (ot % 2) * TC
                p.step("v", lambda e, b=pb: e.tensor_tensor(
                    out=mixfm[:], in0=mx0[:, b:b + TC], in1=mx1[:, b:b + TC], op=ALU.add),
                       waits=[("smr", c * 8 + ot + 1), ("aws", 16 * (c * 8 + ot + 1)),
                              ("mxt", max(0, c * 8 + ot))],
                       inc=("mxs", 1))
                pst = (ps_t1, ps_t2)[ot % 2]
                for tt in range(TT):
                    p.step("t", lambda e, t=tt, ps=pst: e.transpose(
                        ps[0:128, t * 128:(t + 1) * 128],
                        mixfm[:, t * 128:(t + 1) * 128], id_f[:]),
                        waits=([("mxs", c * 8 + ot + 1), ("x2", max(0, c * 8 + ot - 1)),
                                ("bcf", c * 32 + 32)]
                               if tt == 0 else []),
                        inc=("mxt", 1) if tt == TT - 1 else None)
                for tt in range(TT):
                    p.step("v", lambda e, o=ot, t=tt, ps=pst: e.tensor_tensor(
                        out=x_tm[:, t * DM + o * 128:t * DM + (o + 1) * 128],
                        in0=x_tm[:, t * DM + o * 128:t * DM + (o + 1) * 128],
                        in1=ps[0:128, t * 128:(t + 1) * 128], op=ALU.add),
                        waits=[("mxt", c * 8 + ot + 1)] if tt == 0 else [],
                        inc=("x2", 1) if tt == TT - 1 else None)

            for tt in range(TT):
                p.step("a", lambda e, t=tt: e.activation(
                    out=out_tm[:, t * DM:(t + 1) * DM], in_=x_tm[:, t * DM:(t + 1) * DM],
                    func=AF.Square, accum_out=stat[:, 4 + t:5 + t]),
                    waits=[("x2", c * 8 + 8), ("tp", c * FT + FT), ("out", 16 * TT * c)],
                    inc=("sq", 1))
                p.step("v", lambda e, t=tt: e.reduce_sum(
                    out=stat[:, t:t + 1], in_=x_tm[:, t * DM:(t + 1) * DM], axis=AX),
                    waits=[("x2", c * 8 + 8)], inc=("vst", 1))
            p.step("v", em_lnp_a, waits=[("sq", (2 * c + 2) * TT), ("vst", p.cnt("vst"))],
                   inc=("vst", 1))
            p.step("v", em_lnp_b, waits=[("vst", p.cnt("vst"))], inc=("vst", 1))
            p.step("a", em_lnr, waits=[("vst", p.cnt("vst"))], inc=("ln2", 1))
            p.step("v", em_nmr, waits=[("ln2", c + 1)], inc=("vst", 1))
            VST_N2 = p.cnt("vst")

            def em_h2(e):
                inst = None
                for tt in range(TT):
                    e.tensor_scalar(out=out_tm[:, tt * DM:(tt + 1) * DM],
                                    in0=x_tm[:, tt * DM:(tt + 1) * DM],
                                    scalar1=stat[:, 24 + tt:25 + tt], scalar2=stat[:, 28 + tt:29 + tt],
                                    op0=ALU.mult, op1=ALU.add)
                    e.tensor_tensor(out=out_tm[:, tt * DM:(tt + 1) * DM],
                                    in0=out_tm[:, tt * DM:(tt + 1) * DM], in1=cb1r[:], op=ALU.mult)
                    inst = e.tensor_tensor(out=out_tm[:, tt * DM:(tt + 1) * DM],
                                           in0=out_tm[:, tt * DM:(tt + 1) * DM], in1=sbr[:],
                                           op=ALU.add)
                return inst
            p.step("v", em_h2, waits=[("vst", VST_N2)], inc=("vst", 1))

            def em_q1a(e):
                inst = None
                for tt in range(TT):
                    inst = e.tensor_reduce(out=amax[:, tt:tt + 1],
                                           in_=out_tm[:, tt * DM:(tt + 1) * DM],
                                           axis=AX, op=ALU.max, apply_absolute_value=True)
                return inst
            p.step("v", em_q1a, inc=("qc", 1))
            p.step("v", lambda e: e.tensor_scalar(out=amax[:], in0=amax[:], scalar1=1e-5,
                                                  scalar2=None, op0=ALU.max),
                   waits=[("qc", p.cnt("qc"))], inc=("qc", 1))
            p.step("v", lambda e: e.reciprocal(out=as_s[:], in_=amax[:]),
                   waits=[("qc", p.cnt("qc"))], inc=("qc", 1))

            def em_q1b(e):
                e.tensor_scalar(out=was_s[:], in0=amax[:], scalar1=1.0 / 127.0, scalar2=None,
                                op0=ALU.mult)
                return e.tensor_scalar(out=as_s[:], in0=as_s[:], scalar1=127.0, scalar2=None,
                                       op0=ALU.mult)
            p.step("v", em_q1b, waits=[("qc", p.cnt("qc"))], inc=("qc", 1))
            p.step("v", lambda e: e.tensor_scalar(out=was_s[:], in0=was_s[:], scalar1=ws_bc[:],
                                                  scalar2=None, op0=ALU.mult),
                   waits=[("qc", p.cnt("qc"))], inc=("q1", 1))

            def em_q2(e):
                inst = None
                for tt in range(TT):
                    e.tensor_scalar(out=out_tm[:, tt * DM:(tt + 1) * DM],
                                    in0=out_tm[:, tt * DM:(tt + 1) * DM],
                                    scalar1=as_s[:, tt:tt + 1], scalar2=M_RND,
                                    op0=ALU.mult, op1=ALU.add)
                    e.tensor_scalar(out=out_tm[:, tt * DM:(tt + 1) * DM],
                                    in0=out_tm[:, tt * DM:(tt + 1) * DM],
                                    scalar1=M_RND, scalar2=127.0, op0=ALU.subtract, op1=ALU.min)
                    inst = e.tensor_scalar(out=shbf[:, 4096 + tt * DM:4096 + (tt + 1) * DM],
                                           in0=out_tm[:, tt * DM:(tt + 1) * DM],
                                           scalar1=-128.0, scalar2=None, op0=ALU.max)
                return inst
            p.step("v", em_q2, waits=[("q1", c + 1), ("qt", c * FT), ("wo", c * 8 + 8)],
                   inc=("q", 1))

            for ft in range(FT):
                psc = (ps_c1, ps_c2)[ft % 2]
                for tt in range(TT):
                    p.step("t", lambda e, f=ft, t=tt, ps=psc: e.transpose(
                        ps[0:128, t * 128:(t + 1) * 128],
                        shbf[:, 4096 + t * DM + f * 128:4096 + t * DM + (f + 1) * 128], id_bf[:]),
                        waits=([("q", c + 1), ("qf", max(0, c * FT + ft - 1))] if tt == 0 else []),
                        inc=("qt", 1) if tt == TT - 1 else None)
                p.step("a", lambda e, f=ft, ps=psc: e.copy(
                    out=shbf[:, f * TC:(f + 1) * TC], in_=ps[0:128, 0:TC]),
                    waits=[("qt", c * FT + ft + 1)], inc=("qf", 1))

            for ot in range(8):
                p.step("s", lambda e, o=ot: e.dma_start(out=wfbuf[:], in_=wffnd[o]),
                       waits=[("ff", c * 8 + ot)], inc=("wfd", 16))
                psm = (ps_a, ps_b)[ot % 2]
                for kt in range(8):
                    p.step("t", lambda e, o=ot, k=kt, ps=psm: e.matmul(
                        ps[:], wfbuf[:, k * 128:(k + 1) * 128],
                        shbf[:, k * TC:(k + 1) * TC],
                        start=(k == 0), stop=(k == 7)),
                        waits=([("qf", c * FT + FT), ("ffe0", max(0, c * 8 + ot - 1)),
                                ("wfd", 16 * (c * 8 + ot + 1))]
                               if kt == 0 else []),
                        inc=("ff", 1) if kt == 7 else None)
                p.step("v", lambda e, ps=psm: e.tensor_copy(out=mixfm[:], in_=ps[:]),
                       waits=[("ff", c * 8 + ot + 1), ("fft", max(0, c * 8 + ot))],
                       inc=("ffe0", 1))
                pst = (ps_t1, ps_t2)[ot % 2]
                for tt in range(TT):
                    p.step("t", lambda e, t=tt, ps=pst: e.transpose(
                        ps[0:128, t * 128:(t + 1) * 128],
                        mixfm[:, t * 128:(t + 1) * 128], id_f[:]),
                        waits=([("ffe0", c * 8 + ot + 1), ("ffd", max(0, c * 8 + ot - 1))]
                               if tt == 0 else []),
                        inc=("fft", 1) if tt == TT - 1 else None)

                def em_ffd(e, o=ot, ps=pst):
                    sl = lambda t: yt_sl[:, t * 128:(t + 1) * 128]
                    for tt in range(TT):
                        e.tensor_scalar(out=sl(tt), in0=ps[0:128, tt * 128:(tt + 1) * 128],
                                        scalar1=was_s[:, tt:tt + 1], scalar2=None, op0=ALU.mult)
                    for tt in range(TT):
                        e.tensor_tensor(out=sl(tt), in0=sl(tt),
                                        in1=gbr[:, o * 128:(o + 1) * 128], op=ALU.mult)
                    for tt in range(TT):
                        e.tensor_tensor(out=sl(tt), in0=sl(tt),
                                        in1=gbiasr[:, o * 128:(o + 1) * 128], op=ALU.add)
                    inst = None
                    for tt in range(TT):
                        col = tt * DM + o * 128
                        inst = e.tensor_tensor(out=out_tm[:, col:col + 128], in0=sl(tt),
                                               in1=x_tm[:, col:col + 128], op=ALU.add)
                    return inst
                p.step("v", em_ffd, waits=[("fft", c * 8 + ot + 1), ("out", 16 * TT * c)],
                       inc=("ffd", 1))

            def em_store(e, sems, c_=c):
                for t in range(TT):
                    e.dma_start(out=out_r[c_, t],
                                in_=out_tm[:, t * DM:(t + 1) * DM]).then_inc(sems["out"], 16)
            p.step("s", em_store, waits=[("ffd", c * 8 + 8)], inc=("out", 16 * TT), raw=True)

        p.step("s", lambda e: None, waits=[("out", 16 * TT * NCH)])

        if debug:
            dbg = {"xc_sb": xc_sb, "z_sb": z_sb, "brep": brep, "crep": crep,
                   "hstate": hstate, "x_tm": x_tm, "out_tm": out_tm,
                   "shbf": shbf, "xconv": xconv, "stat": stat, "amax": amax,
                   "as_s": as_s, "was_s": was_s, "mod_sb": mod_sb, "a_sb": a_sb,
                   "bdt_sb": bdt_sb, "w1_sb": w1_sb, "w2_sb": w2_sb,
                   "dl_sl": dl_sl, "g_sl": g_sl, "da_sl": da_sl, "u_sl": u_sl,
                   "h_sl": h_sl, "sm_sl": sm_sl, "ct_sb": ct_sb,
                   "ca1_sb": ca1_sb, "sa_sb": sa_sb, "ga_sb": ga_sb, "cb1r": cb1r,
                   "sbr": sbr, "gbr": gbr, "gbiasr": gbiasr, "mixfm": mixfm,
                   "dtlow": dtlow, "xdbc": xdbc, "wbuf": wbuf, "vec8": vec8}
            for nm, tsr in dbg.items():
                shp = list(tsr.shape)
                dd_ = nc.dram_tensor("dbg_" + nm, shp, tsr.dtype, kind="ExternalOutput").ap()
                p.step("s", lambda e, d=dd_, t=tsr: e.dma_start(out=d, in_=t[:]),
                       waits=[("out", 16 * TT * NCH)], inc=("dbg", 16))
            p.step("s", lambda e: None, waits=[("dbg", p.cnt("dbg"))])

        if trunc is not None:
            p.steps = p.steps[:trunc]
            import sys as _sys
            print(f"TRUNCATED to {trunc} of {len(p.steps)} steps", file=_sys.stderr)

        for extra in ("sws", "xpl", "aws"):
            p.sems.setdefault(extra, 0)
        sem_handles = {}
        for s in sorted(p.sems.keys()):
            sem_handles[s] = ctx.enter_context(nc.semaphore(s))

        with nc.Block() as block:
            def replay(tag):
                def body(eng):
                    for se, emit, waits, inc, raw in p.steps:
                        if se != tag:
                            continue
                        for nm, val in waits:
                            eng.wait_ge(sem_handles[nm], val)
                        if raw:
                            emit(eng, sem_handles)
                            continue
                        inst = emit(eng)
                        if inc is not None and inst is not None:
                            inst.then_inc(sem_handles[inc[0]], inc[1])
                return body

            block.sync(replay("s"))
            block.vector(replay("v"))
            block.scalar(replay("a"))
            block.tensor(replay("t"))
            block.gpsimd(replay("g"))

    return nc


_NC_CACHE = None


def kernel(**inputs):
    global _NC_CACHE
    x = inputs["L0_attn_hidden_states"]
    cc = inputs["L0_c"]
    ident = np.eye(128)
    sel = np.kron(np.eye(32), np.ones((1, 128))).astype(np.float32)

    win = np.ascontiguousarray(inputs["L0_attn_W_in"], np.float32)
    wout = np.ascontiguousarray(inputs["L0_attn_W_out"], np.float32)
    wx = np.ascontiguousarray(inputs["L0_attn_W_x"], np.float32)
    wdt = np.ascontiguousarray(inputs["L0_attn_W_dt"], np.float32)
    alog = np.ascontiguousarray(inputs["L0_attn_A_log"], np.float32)
    bdt = np.ascontiguousarray(inputs["L0_attn_b_dt"], np.float32).reshape(16, 128)
    dpar = np.ascontiguousarray(inputs["L0_attn_D_param"], np.float32).reshape(16, 128)
    cb1 = np.ascontiguousarray(inputs["L0_attn_conv_b1"], np.float32).reshape(16, 128)
    cb2 = np.ascontiguousarray(inputs["L0_attn_conv_b2"], np.float32).reshape(16, 128)
    cw1 = np.ascontiguousarray(inputs["L0_attn_conv_w1"], np.float32).reshape(16, 512)
    cw2 = np.ascontiguousarray(inputs["L0_attn_conv_w2"], np.float32).reshape(16, 512)

    common = {
        "gates": np.ascontiguousarray(inputs["L0_attn_conv_gates"], np.float32).reshape(1, 2),
        "packed": np.ascontiguousarray(inputs["L0_ffn_packed_weight"], np.int32).reshape(256, 1024),
        "wscale": np.ascontiguousarray(inputs["L0_ffn_weight_scale"], np.float32).reshape(1, 1),
        "fbias": np.ascontiguousarray(inputs["L0_ffn_bias"], np.float32).reshape(8, 128),
        "wmod": np.ascontiguousarray(inputs["L0_wmod"], np.float32),
        "idbf": ident.astype(BF),
        "idf": ident.astype(np.float32),
        "sel": sel.astype(BF),
        "onesr": np.ones((1, 128), np.float32),
    }

    if _NC_CACHE is None:
        _NC_CACHE = build_nc()
    nc = _NC_CACHE
    in_maps = []
    for cid in range(NCORES):
        b, h = cid // 2, cid % 2
        lo, hi = h * DI, (h + 1) * DI
        m = dict(common)
        m["x"] = np.ascontiguousarray(x[b], np.float32)
        m["cvec"] = np.ascontiguousarray(cc[b], np.float32).reshape(8, 128)
        m["w_in"] = np.ascontiguousarray(
            np.concatenate([win[lo:hi], win[2048 + lo:2048 + hi]], axis=0))
        m["w_out"] = np.ascontiguousarray(wout[:, lo:hi])
        m["w_x"] = np.ascontiguousarray(wx[:, lo:hi])
        m["w_dt"] = np.ascontiguousarray(wdt[lo:hi])
        m["a_log"] = np.ascontiguousarray(alog[lo:hi])
        m["b_dt"] = np.ascontiguousarray(bdt[h * DT:(h + 1) * DT])
        m["dpar"] = np.ascontiguousarray(dpar[h * DT:(h + 1) * DT])
        m["cb1"] = np.ascontiguousarray(cb1[h * DT:(h + 1) * DT])
        m["cb2"] = np.ascontiguousarray(cb2[h * DT:(h + 1) * DT])
        m["cw1"] = np.ascontiguousarray(cw1[h * DT:(h + 1) * DT])
        m["cw2"] = np.ascontiguousarray(cw2[h * DT:(h + 1) * DT])
        m["slid"] = np.array([[h]], np.int32)
        in_maps.append(m)

    res = run_bass_kernel_spmd(nc, in_maps, core_ids=list(range(NCORES)))
    out = np.stack([np.asarray(res.results[2 * b]["out"], np.float32) for b in range(B)])
    return out
